# version 17
# baseline (speedup 1.0000x reference)
import contextlib
import numpy as np
import ml_dtypes
import concourse.bass as bass
import concourse.mybir as mybir
from concourse.bass_utils import run_bass_kernel_spmd

F32 = mybir.dt.float32
BF16 = mybir.dt.bfloat16
U8 = mybir.dt.uint8
AF = mybir.ActivationFunctionType
ALU = mybir.AluOpType
AXX = mybir.AxisListType.X

S = 8192
D = 4096
H = 32
DH = 128
HFF = 11008
NCORE = 8
TOWN = S // NCORE
HALO = 256
TEXT = TOWN + 2 * HALO
NDT = D // 128
NHT = HFF // 128
NG_IN = 32
NG_OUT = 86
SCALE = 1.0 / float(np.sqrt(DH))
NEG = -1e9

ENGS = ("sync", "tensor", "vector", "scalar", "gpsimd")


class Sem:
    def __init__(self, h):
        self.h = h
        self.n = 0


class Plan:
    def __init__(self):
        self.q = {e: [] for e in ENGS}

    def emit(self, eng, fn):
        self.q[eng].append(fn)

    def wait(self, eng, sem, val):
        if val > 0:
            self.emit(eng, lambda e, s=sem.h, v=val: e.wait_ge(s, v))

    def dma(self, eng, out, in_, sem, transpose=False):
        def f(e, o=out, i=in_, s=sem.h, t=transpose):
            e.dma_start(out=o, in_=i, transpose=t).then_inc(s, 16)
        self.emit(eng, f)
        sem.n += 16
        return sem.n

    def vsy(self, eng, sem):
        self.wait(eng, sem, sem.n)

    def op(self, eng, meth, *args, inc=None, inc_drain=None, **kw):
        def f(e, m=meth, a=args, k=kw, i=inc, d=inc_drain):
            ins = getattr(e, m)(*a, **k)
            if i is not None:
                ins.then_inc(i.h, 1)
            if d is not None:
                e.drain().then_inc(d.h, 1)
        self.emit(eng, f)
        if inc is not None:
            inc.n += 1
        if inc_drain is not None:
            inc_drain.n += 1
        if inc is not None or inc_drain is not None:
            return (inc or inc_drain).n


class Ctx:
    pass


DBG_OUT = ()


def _mk(c: Ctx):
    nc = c.nc
    ap = lambda *a, **k: nc.dram_tensor(*a, **k).ap()
    c.x_ext = ap("x_ext", [TEXT, D], F32, kind="ExternalInput")
    c.wqt = ap("wqt", [H * 128, NDT * 128], F32, kind="ExternalInput")
    c.wot = ap("wot", [NDT * 128, NDT * 128], F32, kind="ExternalInput")
    c.wkt = ap("wkt", [128, NDT * DH], F32, kind="ExternalInput")
    c.wvt = ap("wvt", [128, NDT * DH], F32, kind="ExternalInput")
    c.qg = ap("qg", [HFF, D], U8, kind="ExternalInput")
    c.qu = ap("qu", [HFF, D], U8, kind="ExternalInput")
    c.qd = ap("qd", [D, HFF], U8, kind="ExternalInput")
    c.zg = ap("zg", [HFF, NG_IN], U8, kind="ExternalInput")
    c.sg = ap("sg", [HFF, NG_IN], F32, kind="ExternalInput")
    c.zu = ap("zu", [HFF, NG_IN], U8, kind="ExternalInput")
    c.su = ap("su", [HFF, NG_IN], F32, kind="ExternalInput")
    c.zd = ap("zd", [D, NG_OUT], U8, kind="ExternalInput")
    c.sd = ap("sd", [D, NG_OUT], F32, kind="ExternalInput")
    c.cosq = ap("cosq", [128, TOWN], F32, kind="ExternalInput")
    c.sinq = ap("sinq", [128, TOWN], F32, kind="ExternalInput")
    c.cosk = ap("cosk", [128, TEXT], F32, kind="ExternalInput")
    c.sink = ap("sink", [128, TEXT], F32, kind="ExternalInput")
    c.alpc = ap("alpc", [128, 8], F32, kind="ExternalInput")
    c.onec = ap("onec", [128, 8], F32, kind="ExternalInput")
    c.maskc = ap("maskc", [128, 20], F32, kind="ExternalInput")
    c.idf = ap("idf", [128, 128], F32, kind="ExternalInput")
    c.out = ap("out", [TOWN, D], F32, kind="ExternalOutput")
    kk = lambda n: "ExternalOutput" if n in DBG_OUT else "Internal"
    c.xn_stash = ap("xn_stash", [TEXT, D], BF16, kind=kk("xn_stash"))
    c.gu_stash = ap("gu_stash", [2 * HFF, D], BF16, kind=kk("gu_stash"))
    c.wd_stash = ap("wd_stash", [D, HFF], BF16, kind=kk("wd_stash"))
    c.h_stash = ap("h_stash", [HFF, TOWN], BF16, kind=kk("h_stash"))
    c.projT_stash = ap("projT_st", [D, TOWN], F32, kind=kk("projT_st"))
    c.sum_stash = ap("sum_st", [D, TOWN], F32, kind=kk("sum_st"))

    sb = c.sb
    c.xnT_own = sb("xnT_own", [128, NDT * TOWN], BF16)
    c.alp_sb = sb("alp_sb", [128, 8], F32)
    c.one_sb = sb("one_sb", [128, 8], F32)
    c.msk_sb = sb("msk_sb", [128, 20], F32)
    c.idf_sb = sb("idf_sb", [128, 128], F32)
    c.bstack = contextlib.ExitStack()
    bsb = lambda n, s, d: c.bstack.enter_context(
        nc.sbuf_tensor(n, s, d, side="right"))
    c.v_aug = bsb("v_aug", [128, 12 * 132], BF16)
    c.krT = bsb("krT", [128, TEXT], BF16)
    c.cq_sb = bsb("cq_sb", [128, TOWN], F32)
    c.sq_sb = bsb("sq_sb", [128, TOWN], F32)
    c.g_halo = nc.sbuf_tensor("xnT_halo", [128, NDT * 512], BF16, side="right")
    c.xnT_halo = c.g_halo.__enter__()
    c.pb = [c.stack.enter_context(nc.psum_tensor(f"pb{i}", [128, 512], F32))
            for i in range(8)]

    for name in ("s_ld", "s_xt", "s_mm", "s_ve", "s_ac", "s_vsy"):
        setattr(c, name, Sem(c.stack.enter_context(nc.semaphore(name))))

    def sems(name, k):
        return [Sem(c.stack.enter_context(nc.semaphore(f"{name}{i}")))
                for i in range(k)]
    c.ldx = sems("ldx", 2)
    c.xw = sems("xw", 2)
    c.gq = sems("gq", 3)
    c.lq = sems("lq", 4)
    c.swl = sems("swl", 4)
    c.ldq = sems("ldq", 2)
    c.swd = sems("swd", 2)
    c.pan = sems("pan", 2)
    c.pans = sems("pans", 2)
    c.hw = sems("hw", 2)
    c.wd = sems("wd", 4)
    c.hh = sems("hh", 4)
    c.spr = sems("spr", 1)[0]
    c.sm = sems("sm", 2)
    c.pj = sems("pj", 2)
    c.dx = sems("dx", 2)
    c.stl = sems("stl", 2)
    c.ou = sems("ou", 2)
    c.bars = [Sem(c.stack.enter_context(nc.semaphore(f"bar{i}")))
              for i in range(10)]
    c.bar_i = 0


def barrier(c):
    sem = c.bars[c.bar_i]
    c.bar_i += 1
    for e in ENGS:
        def f(en, h=sem.h):
            en.drain()
            en.sem_inc(h, 1)
        c.P.emit(e, f)
    sem.n += 5
    for e in ENGS:
        c.P.wait(e, sem, sem.n)


def phase_a(c):
    P, nc = c.P, c.nc
    P.dma("sync", c.cq_sb[:], c.cosq, c.s_ld)
    P.dma("sync", c.sq_sb[:], c.sinq, c.s_ld)
    P.dma("sync", c.alp_sb[:], c.alpc, c.s_ld)
    P.dma("sync", c.one_sb[:], c.onec, c.s_ld)
    P.dma("sync", c.msk_sb[:], c.maskc, c.s_ld)
    P.dma("sync", c.idf_sb[:], c.idf, c.s_ld)

    ph = contextlib.ExitStack()
    t = lambda n, s, d: ph.enter_context(nc.sbuf_tensor(n, s, d))
    x_sb = [t("x_sb0", [128, D], F32), t("x_sb1", [128, D], F32)]
    xc_sb = t("xc_sb", [128, D], F32)
    sqr_sb = t("sqr_sb", [128, D], F32)
    xnb = [t("xnb0", [128, D], BF16), t("xnb1", [128, D], BF16)]
    lncol = t("lncol", [128, 4], F32)

    NSTRIP = TEXT // 128
    for st in range(NSTRIP):
        par = st % 2
        if st >= 2:
            P.wait("sync", c.xw[par], 16 * (st // 2))
        P.dma("sync", x_sb[par][:], c.x_ext[128 * st:128 * (st + 1), :],
              c.ldx[par])
        P.wait("vector", c.ldx[par], c.ldx[par].n)
        P.wait("vector", c.s_ac, 2 * st)
        if st >= 2:
            P.wait("vector", c.xw[par], 16 * (st // 2))
        P.op("vector", "reduce_sum", lncol[:, 0:1], x_sb[par][:], AXX,
             inc=c.s_vsy)
        P.vsy("vector", c.s_vsy)
        P.op("vector", "tensor_scalar", lncol[:, 1:2], lncol[:, 0:1],
             -1.0 / D, None, ALU.mult, inc=c.s_vsy)
        P.vsy("vector", c.s_vsy)
        P.op("vector", "tensor_scalar", xc_sb[:], x_sb[par][:],
             lncol[:, 1:2], None, ALU.add, inc=c.s_ve)
        P.wait("scalar", c.s_ve, c.s_ve.n)
        P.op("scalar", "activation", sqr_sb[:], xc_sb[:], AF.Square,
             inc=c.s_ac)
        P.wait("vector", c.s_ac, c.s_ac.n)
        P.op("vector", "reduce_sum", lncol[:, 2:3], sqr_sb[:], AXX,
             inc=c.s_vsy)
        P.vsy("vector", c.s_vsy)
        P.op("vector", "tensor_scalar", lncol[:, 2:3], lncol[:, 2:3],
             1.0 / D, 1e-5, ALU.mult, ALU.add, inc=c.s_ve)
        P.wait("scalar", c.s_ve, c.s_ve.n)
        P.op("scalar", "activation", lncol[:, 3:4], lncol[:, 2:3], AF.Sqrt,
             inc=c.s_ac)
        P.wait("vector", c.s_ac, c.s_ac.n)
        P.op("vector", "reciprocal", lncol[:, 0:1], lncol[:, 3:4],
             inc=c.s_vsy)
        P.vsy("vector", c.s_vsy)
        P.op("vector", "tensor_scalar", xnb[par][:], xc_sb[:],
             lncol[:, 0:1], None, ALU.mult, inc_drain=c.s_ve)
        P.wait("gpsimd", c.s_ve, c.s_ve.n)
        P.dma("gpsimd", c.xn_stash[128 * st:128 * (st + 1), :], xnb[par][:],
              c.xw[par])

    P.wait("sync", c.xw[0], 16 * (NSTRIP // 2))
    P.wait("sync", c.xw[1], 16 * (NSTRIP // 2))
    for dt in range(NDT):
        c0, c1 = 128 * dt, 128 * (dt + 1)
        P.dma("sync", c.xnT_own[:, TOWN * dt:TOWN * (dt + 1)],
              c.xn_stash[HALO:HALO + TOWN, c0:c1], c.s_xt, transpose=True)
        P.dma("sync", c.xnT_halo[:, 512 * dt:512 * dt + 256],
              c.xn_stash[0:HALO, c0:c1], c.s_xt, transpose=True)
        P.dma("sync", c.xnT_halo[:, 512 * dt + 256:512 * dt + 512],
              c.xn_stash[TEXT - HALO:TEXT, c0:c1], c.s_xt, transpose=True)
    ph.close()


def phase_b1(c):
    P, nc, pb = c.P, c.nc, c.pb
    ph = contextlib.ExitStack()
    t = lambda n, s, d: ph.enter_context(nc.sbuf_tensor(n, s, d))
    ck_sb = t("ck_sb", [128, TEXT], F32)
    sk_sb = t("sk_sb", [128, TEXT], F32)
    wk_bf = t("wk_bf", [128, NDT * DH], BF16)
    wv_bf = t("wv_bf", [128, NDT * DH], BF16)
    kf_sb = t("kf_sb", [128, 512], F32)
    ksw_sb = t("ksw_sb", [128, 512], F32)
    kt1_sb = t("kt1_sb", [128, 512], F32)

    P.dma("sync", ck_sb[:], c.cosk, c.s_ld)
    P.dma("sync", sk_sb[:], c.sink, c.s_ld)
    P.dma("gpsimd", wk_bf[:], c.wkt, c.s_ld)
    P.dma("gpsimd", wv_bf[:], c.wvt, c.s_ld)
    P.wait("tensor", c.s_ld, c.s_ld.n)
    P.wait("vector", c.s_ld, c.s_ld.n)
    P.wait("scalar", c.s_ld, c.s_ld.n)
    P.wait("tensor", c.s_xt, c.s_xt.n)

    def ext_rhs_pieces(lo, hi):
        segs = []
        if lo < 256:
            e = min(hi, 256)
            segs.append(("halo", lo, e - lo))
            lo = e
        if lo < hi and lo < 1280:
            e = min(hi, 1280)
            segs.append(("own", lo - 256, e - lo))
            lo = e
        if lo < hi:
            segs.append(("halo", 256 + (lo - 1280), hi - lo))
        return segs

    for ch in range(3):
        segs = ext_rhs_pieces(512 * ch, 512 * (ch + 1))
        off = 0
        for kind, start, width in segs:
            for dt in range(NDT):
                if kind == "own":
                    rhs = c.xnT_own[:, TOWN * dt + start:TOWN * dt + start + width]
                else:
                    rhs = c.xnT_halo[:, 512 * dt + start:512 * dt + start + width]
                P.op("tensor", "matmul", pb[ch][:, off:off + width],
                     wk_bf[:, 128 * dt:128 * (dt + 1)], rhs,
                     start=(dt == 0), stop=(dt == NDT - 1),
                     inc=(c.s_mm if dt == NDT - 1 else None))
            off += width
        P.wait("vector", c.s_mm, c.s_mm.n)
        cc = ck_sb[:, 512 * ch:512 * (ch + 1)]
        scs = sk_sb[:, 512 * ch:512 * (ch + 1)]
        P.op("vector", "tensor_copy", kf_sb[:], pb[ch][:])
        P.op("vector", "tensor_copy", ksw_sb[0:64, :], kf_sb[64:128, :])
        P.op("vector", "tensor_copy", ksw_sb[64:128, :], kf_sb[0:64, :])
        P.op("vector", "tensor_mul", kt1_sb[:], kf_sb[:], cc)
        P.op("vector", "tensor_mul", ksw_sb[:], ksw_sb[:], scs)
        P.op("vector", "tensor_add", c.krT[:, 512 * ch:512 * (ch + 1)],
             kt1_sb[:], ksw_sb[:], inc=c.s_ve)
    k_done = c.s_ve.n

    for tt in range(12):
        lo = 128 * tt
        if lo < 256:
            def lhs_t(dt, lo=lo):
                return c.xnT_halo[:, 512 * dt + lo:512 * dt + lo + 128]
        elif lo < 1280:
            def lhs_t(dt, lo=lo):
                o = lo - 256
                return c.xnT_own[:, TOWN * dt + o:TOWN * dt + o + 128]
        else:
            def lhs_t(dt, lo=lo):
                o = 256 + lo - 1280
                return c.xnT_halo[:, 512 * dt + o:512 * dt + o + 128]
        bank = pb[3 + (tt % 2)]
        if tt >= 2:
            P.wait("tensor", c.s_ve, k_done + tt - 1)
        for dt in range(NDT):
            P.op("tensor", "matmul", bank[:, 0:128],
                 lhs_t(dt), wv_bf[:, 128 * dt:128 * (dt + 1)],
                 start=(dt == 0), stop=(dt == NDT - 1),
                 inc=(c.s_mm if dt == NDT - 1 else None))
        P.wait("vector", c.s_mm, c.s_mm.n)
        P.op("vector", "tensor_copy",
             c.v_aug[:, 132 * tt:132 * tt + 128], bank[:, 0:128])
        P.op("vector", "memset",
             c.v_aug[:, 132 * tt + 128:132 * tt + 129], 1.0, inc=c.s_ve)
    ph.close()


WIN_Q = [(0, 256), (0, 512), (256, 768), (512, 1024), (768, 1024)]
WIN_SLOT0 = [2, 0, 0, 0, 0]


def phase_b2(c, attn_outT):
    P, nc, pb = c.P, c.nc, c.pb
    ph = contextlib.ExitStack()
    t = lambda n, s, d: ph.enter_context(nc.sbuf_tensor(n, s, d))
    wq_bf = [t(f"wq_bf{i}", [128, NDT * 128], BF16) for i in range(3)]
    qf_sb = t("qf_sb", [128, TOWN], F32)
    qsw_sb = t("qsw_sb", [128, TOWN], F32)
    qt1_sb = t("qt1_sb", [128, TOWN], F32)
    qrT = [t("qrT0", [128, TOWN], BF16), t("qrT1", [128, TOWN], BF16)]
    attnT = t("attnT", [128, 2048], BF16)
    wbuf = [t("wb0", [128, 4 * 132], F32), t("wb1", [128, 4 * 132], F32)]
    bcol = t("bcol", [128, 4], F32)
    blf = t("blf", [128, 128], F32)

    sig_done = {k: 0 for k in range(4)}
    av_bank_evac = {k: 0 for k in range(4)}
    tr_bank_evac = {0: 0, 1: 0}
    blend_prog = 0
    av_mm_done = 0
    gq_val = {}
    rope_done = {}
    qmm_done = {}

    def q_chunks(h):
        par = h % 3
        chunks = []
        for th in range(2):
            for cb in range(8):
                def f(th=th, cb=cb, par=par, h=h):
                    if th == 0 and cb == 0:
                        P.wait("tensor", c.gq[par], gq_val[h])
                        if h >= 1:
                            P.wait("tensor", c.s_ve, rope_done[h - 1])
                    for dt in range(4 * cb, 4 * cb + 4):
                        P.op("tensor", "matmul", pb[4 + th][:, 0:512],
                             wq_bf[par][:, 128 * dt:128 * (dt + 1)],
                             c.xnT_own[:, TOWN * dt + 512 * th:
                                       TOWN * dt + 512 * th + 512],
                             start=(dt == 0), stop=(dt == NDT - 1),
                             inc=(c.s_mm if dt == NDT - 1 and th == 1
                                  else None))
                    if th == 1 and cb == 7:
                        qmm_done[h] = c.s_mm.n
                chunks.append(f)
        return chunks

    def emit_rope(h):
        par = h % 2
        P.wait("vector", c.s_mm, qmm_done[h])
        P.op("vector", "tensor_copy", qf_sb[:, 0:512], pb[4][:])
        P.op("vector", "tensor_copy", qf_sb[:, 512:1024], pb[5][:])
        P.op("vector", "tensor_copy", qsw_sb[0:64, :], qf_sb[64:128, :])
        P.op("vector", "tensor_copy", qsw_sb[64:128, :], qf_sb[0:64, :])
        P.op("vector", "tensor_mul", qt1_sb[:], qf_sb[:], c.cq_sb[:])
        P.op("vector", "tensor_mul", qsw_sb[:], qsw_sb[:], c.sq_sb[:])
        P.op("vector", "tensor_add", qrT[par][:], qt1_sb[:], qsw_sb[:],
             inc=c.s_ve)
        rope_done[h] = c.s_ve.n

    for hh in range(3):
        P.dma("gpsimd", wq_bf[hh][:], c.wqt[128 * hh:128 * (hh + 1), :],
              c.gq[hh])
        gq_val[hh] = c.gq[hh].n
    for f in q_chunks(0):
        f()
    emit_rope(0)

    for h in range(H):
        par = h % 3
        pump = q_chunks(h + 1) if h + 1 < H else []
        pump_i = [0]

        def do_pump(n=1):
            for _ in range(n):
                if pump_i[0] < len(pump):
                    pump[pump_i[0]]()
                    pump_i[0] += 1

        if h + 3 < H:
            P.wait("gpsimd", c.s_mm, qmm_done[h])
            P.dma("gpsimd", wq_bf[par][:],
                  c.wqt[128 * (h + 3):128 * (h + 4), :], c.gq[par])
            gq_val[h + 3] = c.gq[par].n

        for w in range(5):
            qlo, qhi = WIN_Q[w]
            qn = qhi - qlo
            wslot = wbuf[w % 2]
            for kt in range(4):
                if kt == 0 and w == 0:
                    P.wait("tensor", c.s_ve, rope_done[h])
                P.wait("tensor", c.s_ac, av_bank_evac[kt])
                P.wait("tensor", c.s_ac, sig_done[kt])
                P.op("tensor", "matmul", pb[kt][:, 0:qn],
                     c.krT[:, 256 * w + 128 * kt:256 * w + 128 * (kt + 1)],
                     qrT[h % 2][:, qlo:qhi], start=True, stop=True,
                     inc=c.s_mm)
            sc_done = c.s_mm.n
            do_pump(2)
            for kt in range(4):
                P.wait("scalar", c.s_mm, sc_done - 3 + kt)
                P.wait("scalar", c.s_mm, av_mm_done)
                P.wait("scalar", c.s_ve, blend_prog)
                P.op("scalar", "activation", attnT[:, 512 * kt:512 * kt + qn],
                     pb[kt][:, 0:qn], AF.Sigmoid,
                     bias=c.msk_sb[:, 4 * w + kt:4 * w + kt + 1],
                     scale=SCALE, inc=c.s_ac)
                sig_done[kt] = c.s_ac.n
            nq = qn // 128
            for qs in range(nq):
                bank = pb[qs]
                P.wait("tensor", c.s_ac, sig_done[3])
                for kt in range(4):
                    P.op("tensor", "matmul", bank[:, 0:129],
                         attnT[:, 512 * kt + 128 * qs:512 * kt + 128 * (qs + 1)],
                         c.v_aug[:, 132 * (2 * w + kt):132 * (2 * w + kt) + 129],
                         start=(kt == 0), stop=(kt == 3),
                         inc=(c.s_mm if kt == 3 else None))
                av_mm_done = c.s_mm.n
                do_pump(1)
                P.wait("scalar", c.s_mm, av_mm_done)
                slot = WIN_SLOT0[w] + qs
                P.op("scalar", "copy", wslot[:, 132 * slot:132 * slot + 129],
                     bank[:, 0:129], inc=c.s_ac)
                av_bank_evac[qs] = c.s_ac.n

            if w == 2 and h + 1 < H:
                emit_rope(h + 1)

            if w >= 1:
                k = w - 1
                A = wbuf[(w - 1) % 2]
                B = wbuf[w % 2]
                for qs in range(2):
                    aslot = 2 + qs
                    bslot = qs
                    col = 2 * k + qs
                    P.wait("vector", c.s_ac, av_bank_evac[bslot])
                    P.op("vector", "reciprocal", bcol[:, 0:1],
                         A[:, 132 * aslot + 128:132 * aslot + 129])
                    P.op("vector", "reciprocal", bcol[:, 1:2],
                         B[:, 132 * bslot + 128:132 * bslot + 129],
                         inc=c.s_vsy)
                    P.vsy("vector", c.s_vsy)
                    P.op("vector", "tensor_mul", bcol[:, 2:3],
                         bcol[:, 0:1], c.one_sb[:, col:col + 1])
                    P.op("vector", "tensor_mul", bcol[:, 3:4],
                         bcol[:, 1:2], c.alp_sb[:, col:col + 1], inc=c.s_vsy)
                    P.vsy("vector", c.s_vsy)
                    P.op("vector", "tensor_scalar", blf[:],
                         A[:, 132 * aslot:132 * aslot + 128],
                         bcol[:, 2:3], None, ALU.mult)
                    P.op("vector", "scalar_tensor_tensor", blf[:],
                         B[:, 132 * bslot:132 * bslot + 128],
                         bcol[:, 3:4], blf[:], ALU.mult, ALU.add, inc=c.s_ve)
                    tbank = 6 + (qs % 2)
                    P.wait("tensor", c.s_ve, c.s_ve.n)
                    P.wait("tensor", c.s_ve, tr_bank_evac[qs % 2])
                    P.op("tensor", "transpose", pb[tbank][:, 0:128], blf[:],
                         c.idf_sb[:], inc=c.s_mm)
                    P.wait("vector", c.s_mm, c.s_mm.n)
                    P.op("vector", "tensor_copy",
                         attn_outT[:, TOWN * h + 128 * col:
                                   TOWN * h + 128 * (col + 1)],
                         pb[tbank][:, 0:128], inc=c.s_ve)
                    tr_bank_evac[qs % 2] = c.s_ve.n
                blend_prog = c.s_ve.n
        do_pump(16)
    ph.close()


def phase_b3(c, attn_outT, dq):
    P, nc, pb = c.P, c.nc, c.pb
    ph = contextlib.ExitStack()
    t = lambda n, s, d: ph.enter_context(nc.sbuf_tensor(n, s, d))
    wo_bf = [t(f"wo_bf{i}", [128, NDT * 128], BF16) for i in range(3)]
    pr_sb = [t("pr_sb0", [128, TOWN], F32), t("pr_sb1", [128, TOWN], F32)]
    wo_mm = {}
    gq_val = {}
    for hh in range(3):
        P.dma("gpsimd", wo_bf[hh][:], c.wot[128 * hh:128 * (hh + 1), :],
              c.gq[hh])
        gq_val[hh] = c.gq[hh].n
    for i in range(NDT):
        par = i % 2
        wpar = i % 3
        P.wait("tensor", c.gq[wpar], gq_val[i])
        if i >= 2:
            P.wait("tensor", c.pj[par], 16 * (i // 2))
        for th in range(2):
            bank = pb[2 * (i % 2) + th]
            for ot in range(NDT):
                P.op("tensor", "matmul", bank[:, 0:512],
                     wo_bf[wpar][:, 128 * ot:128 * (ot + 1)],
                     attn_outT[:, TOWN * ot + 512 * th:
                               TOWN * ot + 512 * (th + 1)],
                     start=(ot == 0), stop=(ot == NDT - 1),
                     inc=(c.s_mm if ot == NDT - 1 else None))
        wo_mm[i] = c.s_mm.n
        if i + 3 < NDT:
            P.wait("gpsimd", c.s_mm, wo_mm[i])
            P.dma("gpsimd", wo_bf[wpar][:],
                  c.wot[128 * (i + 3):128 * (i + 4), :], c.gq[wpar])
            gq_val[i + 3] = c.gq[wpar].n
        P.wait("vector", c.s_mm, c.s_mm.n)
        if i >= 2:
            P.wait("vector", c.pj[par], 16 * (i // 2))
        P.op("vector", "tensor_copy", pr_sb[par][:, 0:512], pb[2 * (i % 2)][:])
        P.op("vector", "tensor_copy", pr_sb[par][:, 512:1024],
             pb[2 * (i % 2) + 1][:], inc_drain=c.s_ve)
        P.wait("gpsimd", c.s_ve, c.s_ve.n)
        P.dma("gpsimd", c.projT_stash[128 * i:128 * (i + 1), :], pr_sb[par][:],
              c.pj[par])
        if i < 16:
            dq.load(i)
        if i % 2 == 1 and i // 2 < 12:
            dq.compute(i // 2)
    ph.close()


class DqGU:

    def __init__(self, c, scope):
        nc = c.nc
        t = lambda n, s, d: scope.enter_context(
            nc.sbuf_tensor(n, s, d, side="right"))
        self.qin = [t(f"qin{i}", [128, D], U8) for i in range(4)]
        self.qo = [t(f"qo{i}", [128, D], BF16) for i in range(2)]
        self.zin = t("zin", [128, 4 * 32], U8)
        self.sin_ = t("sin_", [128, 4 * 32], F32)
        self.zf = t("zf", [128, 4 * 32], F32)
        self.c = c
        self.ld = {}

    def load(self, q):
        c, P = self.c, self.c.P
        jj, kind = q // 2, q % 2
        lane = q % 4
        qw = c.qg if kind == 0 else c.qu
        zt = c.zg if kind == 0 else c.zu
        st_ = c.sg if kind == 0 else c.su
        r0, r1 = 128 * jj, 128 * (jj + 1)
        if q >= 4:
            P.wait("sync", c.swl[q % 2], 16 * (q // 2 - 1))
        P.dma("sync", self.qin[lane][:], qw[r0:r1, :], c.lq[lane])
        P.dma("sync", self.zin[:, 32 * lane:32 * (lane + 1)], zt[r0:r1, :],
              c.lq[lane])
        P.dma("sync", self.sin_[:, 32 * lane:32 * (lane + 1)], st_[r0:r1, :],
              c.lq[lane])
        self.ld[q] = c.lq[lane].n

    def compute(self, q):
        c, P = self.c, self.c.P
        jj, kind = q // 2, q % 2
        lane = q % 4
        olane = q % 2
        s0 = 256 * jj + 128 * kind
        P.wait("vector", c.lq[lane], self.ld[q])
        if q >= 2:
            P.wait("vector", c.swl[olane], 16 * (q // 2))
        P.op("vector", "tensor_copy", self.zf[:, 32 * lane:32 * (lane + 1)],
             self.zin[:, 32 * lane:32 * (lane + 1)], inc=c.s_vsy)
        P.vsy("vector", c.s_vsy)
        for g in range(NG_IN):
            csl = slice(128 * g, 128 * (g + 1))
            zc = slice(32 * lane + g, 32 * lane + g + 1)
            P.op("vector", "tensor_scalar", self.qo[olane][:, csl],
                 self.qin[lane][:, csl], self.zf[:, zc], self.sin_[:, zc],
                 ALU.subtract, ALU.mult,
                 inc_drain=(c.s_ve if g == NG_IN - 1 else None))
        P.wait("gpsimd", c.s_ve, c.s_ve.n)
        P.dma("gpsimd", c.gu_stash[s0:s0 + 128, :], self.qo[olane][:],
              c.swl[olane])


class DqDown:

    def __init__(self, c, scope):
        nc = c.nc
        HH = HFF // 2
        t = lambda n, s, d: scope.enter_context(nc.sbuf_tensor(n, s, d))
        self.qind = [t(f"qind{i}", [128, HH], U8) for i in range(2)]
        self.qodd = [t(f"qodd{i}", [128, HH], BF16) for i in range(2)]
        self.zind = t("zind", [128, 2 * NG_OUT], U8)
        self.sind = t("sind", [128, 2 * NG_OUT], F32)
        self.zfd = t("zfd", [128, 2 * NG_OUT], F32)
        self.c = c
        self.ld = {}

    def load(self, dj):
        c, P = self.c, self.c.P
        HH = HFF // 2
        dt, half = dj // 2, dj % 2
        lane = dj % 2
        r0, r1 = 128 * dt, 128 * (dt + 1)
        h0 = HH * half
        if dj >= 2:
            P.wait("sync", c.swd[lane], 16 * (dj // 2))
        P.dma("sync", self.zind[:, NG_OUT * lane:NG_OUT * (lane + 1)],
              c.zd[r0:r1, :], c.ldq[lane])
        P.dma("sync", self.sind[:, NG_OUT * lane:NG_OUT * (lane + 1)],
              c.sd[r0:r1, :], c.ldq[lane])
        P.dma("sync", self.qind[lane][:], c.qd[r0:r1, h0:h0 + HH],
              c.ldq[lane])
        self.ld[dj] = c.ldq[lane].n

    def compute(self, dj):
        c, P = self.c, self.c.P
        HH = HFF // 2
        NGH = NG_OUT // 2
        dt, half = dj // 2, dj % 2
        lane = dj % 2
        r0, r1 = 128 * dt, 128 * (dt + 1)
        h0 = HH * half
        P.wait("vector", c.ldq[lane], self.ld[dj])
        P.op("vector", "tensor_copy",
             self.zfd[:, NG_OUT * lane:NG_OUT * (lane + 1)],
             self.zind[:, NG_OUT * lane:NG_OUT * (lane + 1)], inc=c.s_vsy)
        P.vsy("vector", c.s_vsy)
        for gg in range(NGH):
            g = NGH * half + gg
            csl = slice(128 * gg, 128 * (gg + 1))
            zc = slice(NG_OUT * lane + g, NG_OUT * lane + g + 1)
            P.op("vector", "tensor_scalar", self.qodd[lane][:, csl],
                 self.qind[lane][:, csl], self.zfd[:, zc], self.sind[:, zc],
                 ALU.subtract, ALU.mult,
                 inc_drain=(c.s_ve if gg == NGH - 1 else None))
        P.wait("gpsimd", c.s_ve, c.s_ve.n)
        P.dma("gpsimd", c.wd_stash[r0:r1, h0:h0 + HH], self.qodd[lane][:],
              c.swd[lane])


def phase_c1(c, dq, dqd):
    P, nc, pb = c.P, c.nc, c.pb
    ph = contextlib.ExitStack()
    t = lambda n, s, d: ph.enter_context(nc.sbuf_tensor(n, s, d))
    pan = [t("pan0", [128, NDT * 512], BF16),
           t("pan1", [128, NDT * 512], BF16)]
    sg_sb = [t("sg_sb0", [128, TOWN], BF16), t("sg_sb1", [128, TOWN], BF16)]
    h_sb = [t("h_sb0", [128, TOWN], BF16), t("h_sb1", [128, TOWN], BF16)]
    NPAN = NHT // 2
    pan_ld = {}
    gu_mm = {}
    h_ve = {}
    NGU = 2 * NHT
    NDJ = 2 * NDT
    for dj in range(2):
        dqd.load(dj)
    for J in range(NPAN):
        ppar = J % 2
        for q in range(4 * J + 12, min(4 * J + 16, NGU)):
            dq.compute(q)
        for dj in range(2 * J, min(2 * J + 2, NDJ)):
            dqd.compute(dj)
        if J >= 2:
            P.wait("sync", c.s_mm, gu_mm[2 * (J - 2) + 1])
        P.wait("sync", c.swl[0], 16 * min(2 * J + 2, NHT))
        P.wait("sync", c.swl[1], 16 * min(2 * J + 2, NHT))
        for dt in range(NDT):
            P.dma("sync", pan[ppar][:, 512 * dt:512 * (dt + 1)],
                  c.gu_stash[512 * J:512 * (J + 1), 128 * dt:128 * (dt + 1)],
                  c.pan[ppar], transpose=True)
        pan_ld[J] = c.pan[ppar].n
        for r in range(2):
            jj = 2 * J + r
            jpar = jj % 2
            P.wait("tensor", c.pan[ppar], pan_ld[J])
            if jj >= 2:
                P.wait("tensor", c.s_ve, h_ve[jj - 2])
            for th in range(2):
                for dt in range(NDT):
                    P.op("tensor", "matmul", pb[4 * jpar + th][:, 0:512],
                         pan[ppar][:, 512 * dt + 256 * r:
                                    512 * dt + 256 * r + 128],
                         c.xnT_own[:, TOWN * dt + 512 * th:
                                   TOWN * dt + 512 * (th + 1)],
                         start=(dt == 0), stop=(dt == NDT - 1),
                         inc=(c.s_mm if dt == NDT - 1 else None))
            for th in range(2):
                for dt in range(NDT):
                    P.op("tensor", "matmul", pb[4 * jpar + 2 + th][:, 0:512],
                         pan[ppar][:, 512 * dt + 256 * r + 128:
                                    512 * dt + 256 * r + 256],
                         c.xnT_own[:, TOWN * dt + 512 * th:
                                   TOWN * dt + 512 * (th + 1)],
                         start=(dt == 0), stop=(dt == NDT - 1),
                         inc=(c.s_mm if dt == NDT - 1 else None))
            gu_mm[jj] = c.s_mm.n
            P.wait("scalar", c.s_mm, gu_mm[jj] - 2)
            if jj >= 2:
                P.wait("scalar", c.s_ve, h_ve[jj - 2])
            P.op("scalar", "activation", sg_sb[jpar][:, 0:512],
                 pb[4 * jpar + 0][:], AF.Silu)
            P.op("scalar", "activation", sg_sb[jpar][:, 512:1024],
                 pb[4 * jpar + 1][:], AF.Silu, inc=c.s_ac)
            P.wait("vector", c.s_ac, c.s_ac.n)
            P.wait("vector", c.s_mm, gu_mm[jj])
            if jj >= 2:
                P.wait("vector", c.hw[jpar], 16 * (jj // 2))
            P.op("vector", "tensor_mul", h_sb[jpar][:, 0:512],
                 sg_sb[jpar][:, 0:512], pb[4 * jpar + 2][:])
            P.op("vector", "tensor_mul", h_sb[jpar][:, 512:1024],
                 sg_sb[jpar][:, 512:1024], pb[4 * jpar + 3][:],
                 inc_drain=c.s_ve)
            h_ve[jj] = c.s_ve.n
            P.wait("gpsimd", c.s_ve, c.s_ve.n)
            P.dma("gpsimd", c.h_stash[128 * jj:128 * (jj + 1), :],
                  h_sb[jpar][:], c.hw[jpar])
        for q in range(4 * J + 16, min(4 * J + 20, NGU)):
            dq.load(q)
        for dj in range(2 * J + 2, min(2 * J + 4, NDJ)):
            dqd.load(dj)
    ph.close()


def phase_c2(c):
    P, nc, pb = c.P, c.nc, c.pb
    ph = contextlib.ExitStack()
    t = lambda n, s, d: ph.enter_context(nc.sbuf_tensor(n, s, d))
    wdt = [t(f"wdt{i}", [128, 512], BF16) for i in range(4)]
    hst = [t(f"hst{i}", [128, TOWN], BF16) for i in range(4)]
    prt = t("prt", [128, 4 * TOWN], F32)
    smt = [t("smt0", [128, TOWN], F32), t("smt1", [128, TOWN], F32)]
    for s in (c.hw[0], c.hw[1], c.pj[0], c.pj[1], c.swd[0], c.swd[1]):
        P.wait("sync", s, s.n)
        P.wait("gpsimd", s, s.n)
    dn_mm = {}
    sum_ve = 0
    for ch in range(8):
        P.wait("sync", c.s_ve, sum_ve)
        for ic in range(4):
            i = 4 * ch + ic
            P.dma("sync", prt[:, TOWN * ic:TOWN * (ic + 1)],
                  c.projT_stash[128 * i:128 * (i + 1), :], c.spr)
        prj_ld = c.spr.n
        for jj in range(NHT):
            lane = jj % 4
            prev = (ch, jj - 4) if jj >= 4 else (ch - 1, 82 + (jj + 2) % 4)
            if prev in dn_mm:
                P.wait("sync", c.s_mm, dn_mm[prev])
                P.wait("gpsimd", c.s_mm, dn_mm[prev])
            P.dma("sync", wdt[lane][:],
                  c.wd_stash[512 * ch:512 * (ch + 1),
                             128 * jj:128 * (jj + 1)], c.wd[lane],
                  transpose=True)
            P.dma("gpsimd", hst[lane][:],
                  c.h_stash[128 * jj:128 * (jj + 1), :], c.hh[lane])
            P.wait("tensor", c.wd[lane], c.wd[lane].n)
            P.wait("tensor", c.hh[lane], c.hh[lane].n)
            if jj == 0 and ch > 0:
                P.wait("tensor", c.s_ve, sum_ve)
            for ic in range(4):
                for th in range(2):
                    P.op("tensor", "matmul", pb[2 * ic + th][:, 0:512],
                         wdt[lane][:, 128 * ic:128 * (ic + 1)],
                         hst[lane][:, 512 * th:512 * (th + 1)],
                         start=(jj == 0), stop=(jj == NHT - 1),
                         inc=(c.s_mm if (th == 1 and ic == 3) else None))
            dn_mm[(ch, jj)] = c.s_mm.n
        P.wait("vector", c.s_mm, dn_mm[(ch, NHT - 1)])
        P.wait("vector", c.spr, prj_ld)
        for ic in range(4):
            i = 4 * ch + ic
            k = 4 * ch + ic
            sl = ic % 2
            if k >= 2:
                P.wait("vector", c.sm[sl], 16 * (k // 2))
            for th in range(2):
                P.op("vector", "tensor_add",
                     smt[sl][:, 512 * th:512 * (th + 1)],
                     pb[2 * ic + th][:],
                     prt[:, TOWN * ic + 512 * th:TOWN * ic + 512 * (th + 1)],
                     inc_drain=(c.s_ve if th == 1 else None))
            P.wait("gpsimd", c.s_ve, c.s_ve.n)
            P.dma("gpsimd", c.sum_stash[128 * i:128 * (i + 1), :],
                  smt[sl][:], c.sm[sl])
        sum_ve = c.s_ve.n
    ph.close()


def phase_d(c):
    P, nc, pb = c.P, c.nc, c.pb
    ph = contextlib.ExitStack()
    t = lambda n, s, d: ph.enter_context(nc.sbuf_tensor(n, s, d))
    xs = [t("xs0", [128, D], F32), t("xs1", [128, D], F32)]
    stile = [t("stile0", [128, 128], F32), t("stile1", [128, 128], F32)]
    ostrip = [t("ostrip0", [128, D], F32), t("ostrip1", [128, D], F32)]
    for s in (c.sm[0], c.sm[1]):
        P.wait("sync", s, s.n)
    tr_mm = {0: 0, 1: 0}
    ev_ve = {0: 0, 1: 0}
    dx_val = {}
    for tt in range(8):
        par = tt % 2
        if tt >= 2:
            P.wait("sync", c.ou[par], 16 * (tt // 2))
        P.dma("sync", xs[par][:],
              c.x_ext[HALO + 128 * tt:HALO + 128 * (tt + 1), :], c.dx[par])
        dx_val[tt] = c.dx[par].n
        for i in range(NDT):
            sl = i % 2
            P.wait("sync", c.s_mm, tr_mm[sl])
            P.dma("sync", stile[sl][:],
                  c.sum_stash[128 * i:128 * (i + 1), 128 * tt:128 * (tt + 1)],
                  c.stl[sl])
            P.wait("tensor", c.stl[sl], c.stl[sl].n)
            P.wait("tensor", c.s_ve, ev_ve[sl])
            P.op("tensor", "transpose", pb[sl][:, 0:128], stile[sl][:],
                 c.idf_sb[:], inc=c.s_mm)
            tr_mm[sl] = c.s_mm.n
            P.wait("vector", c.s_mm, c.s_mm.n)
            if i == 0:
                P.wait("vector", c.dx[par], dx_val[tt])
                if tt >= 2:
                    P.wait("vector", c.ou[par], 16 * (tt // 2))
            P.op("vector", "tensor_add",
                 ostrip[par][:, 128 * i:128 * (i + 1)],
                 pb[sl][:, 0:128], xs[par][:, 128 * i:128 * (i + 1)],
                 **({"inc_drain": c.s_ve} if i == NDT - 1 else {"inc": c.s_ve}))
            ev_ve[sl] = c.s_ve.n
        P.wait("gpsimd", c.s_ve, c.s_ve.n)
        P.dma("gpsimd", c.out[128 * tt:128 * (tt + 1), :], ostrip[par][:],
              c.ou[par])
    P.wait("gpsimd", c.ou[0], c.ou[0].n)
    P.wait("gpsimd", c.ou[1], c.ou[1].n)
    ph.close()


def build_program():
    c = Ctx()
    c.nc = bass.Bass("TRN2", target_bir_lowering=False, debug=False)
    c.P = Plan()
    c.stack = contextlib.ExitStack()
    c.sb = lambda n, s, d: c.stack.enter_context(c.nc.sbuf_tensor(n, s, d))
    _mk(c)

    phase_a(c)
    barrier(c)
    phase_b1(c)
    c.g_halo.__exit__(None, None, None)
    barrier(c)
    g_attn = c.nc.sbuf_tensor("attn_outT", [128, H * TOWN], BF16)
    attn_outT = g_attn.__enter__()
    phase_b2(c, attn_outT)
    c.bstack.close()
    barrier(c)
    dq_scope = contextlib.ExitStack()
    dq = DqGU(c, dq_scope)
    phase_b3(c, attn_outT, dq)
    g_attn.__exit__(None, None, None)
    barrier(c)
    dqd_scope = contextlib.ExitStack()
    dqd = DqDown(c, dqd_scope)
    phase_c1(c, dq, dqd)
    dq_scope.close()
    dqd_scope.close()
    barrier(c)
    phase_c2(c)
    barrier(c)
    phase_d(c)

    with c.nc.Block() as block:
        @block.sync
        def _(e):
            for fn in c.P.q["sync"]:
                fn(e)

        @block.tensor
        def _(e):
            for fn in c.P.q["tensor"]:
                fn(e)

        @block.vector
        def _(e):
            for fn in c.P.q["vector"]:
                fn(e)

        @block.scalar
        def _(e):
            for fn in c.P.q["scalar"]:
                fn(e)

        @block.gpsimd
        def _(e):
            for fn in c.P.q["gpsimd"]:
                fn(e)

    c.stack.close()
    return c.nc


def _rope_tables():
    inv_freq = 1.0 / 10000.0 ** (np.arange(0, DH, 2, dtype=np.float32) / DH)
    t = np.arange(S, dtype=np.float32)
    f = np.outer(t, inv_freq)
    cos = np.zeros((S, DH), dtype=np.float32)
    sin = np.zeros((S, DH), dtype=np.float32)
    cos[:, 0::2] = np.cos(f)
    cos[:, 1::2] = np.cos(f)
    sin[:, 0::2] = np.sin(f)
    sin[:, 1::2] = np.sin(f)
    sinS = sin.copy()
    sinS[:, :DH // 2] *= -1.0
    return cos, sinS


def _u8view(a_int32):
    a = np.ascontiguousarray(a_int32)
    return np.ascontiguousarray(a.view(np.uint8)[:, ::4])


def kernel(L0_attn_x, L0_attn_Wq, L0_attn_Wk, L0_attn_Wv, L0_attn_Wo,
           L0_ffn_qw_gate, L0_ffn_z_gate, L0_ffn_s_gate,
           L0_ffn_qw_up, L0_ffn_z_up, L0_ffn_s_up,
           L0_ffn_qw_down, L0_ffn_z_down, L0_ffn_s_down):
    x = np.ascontiguousarray(
        np.asarray(L0_attn_x, dtype=np.float32).reshape(S, D))
    Wq = np.asarray(L0_attn_Wq, dtype=np.float32)
    Wk = np.asarray(L0_attn_Wk, dtype=np.float32)
    Wv = np.asarray(L0_attn_Wv, dtype=np.float32)
    Wo = np.asarray(L0_attn_Wo, dtype=np.float32)
    wqt = np.ascontiguousarray(
        Wq.reshape(H, 128, NDT, 128).transpose(0, 3, 2, 1)).reshape(H * 128, -1)
    wot = np.ascontiguousarray(
        Wo.reshape(NDT, 128, NDT, 128).transpose(0, 3, 2, 1)).reshape(D, -1)
    wkt = np.ascontiguousarray(
        Wk.reshape(DH, NDT, 128).transpose(2, 1, 0)).reshape(128, -1)
    wvt = np.ascontiguousarray(
        Wv.reshape(DH, NDT, 128).transpose(2, 1, 0)).reshape(128, -1)
    qgT = _u8view(np.ascontiguousarray(np.asarray(L0_ffn_qw_gate).T))
    quT = _u8view(np.ascontiguousarray(np.asarray(L0_ffn_qw_up).T))
    qdT = _u8view(np.ascontiguousarray(np.asarray(L0_ffn_qw_down).T))
    zgT = _u8view(np.ascontiguousarray(np.asarray(L0_ffn_z_gate).T))
    sgT = np.ascontiguousarray(np.asarray(L0_ffn_s_gate, dtype=np.float32).T)
    zuT = _u8view(np.ascontiguousarray(np.asarray(L0_ffn_z_up).T))
    suT = np.ascontiguousarray(np.asarray(L0_ffn_s_up, dtype=np.float32).T)
    zdT = _u8view(np.ascontiguousarray(np.asarray(L0_ffn_z_down).T))
    sdT = np.ascontiguousarray(np.asarray(L0_ffn_s_down, dtype=np.float32).T)

    cos, sinS = _rope_tables()
    alpha = np.linspace(0.0, 1.0, 256, dtype=np.float32)
    idf = np.eye(128, dtype=np.float32)

    in_maps = []
    for cid in range(NCORE):
        q0 = TOWN * cid
        lo, hi = q0 - HALO, q0 + TOWN + HALO
        xe = np.zeros((TEXT, D), dtype=np.float32)
        slo, shi = max(lo, 0), min(hi, S)
        xe[slo - lo:shi - lo, :] = x[slo:shi]
        cq = np.ascontiguousarray(cos[q0:q0 + TOWN].T)
        sq = np.ascontiguousarray(sinS[q0:q0 + TOWN].T)
        ck = np.zeros((128, TEXT), dtype=np.float32)
        sk = np.zeros((128, TEXT), dtype=np.float32)
        ck[:, slo - lo:shi - lo] = cos[slo:shi].T
        sk[:, slo - lo:shi - lo] = sinS[slo:shi].T
        alc = np.empty((128, 8), dtype=np.float32)
        onc = np.empty((128, 8), dtype=np.float32)
        for j in range(8):
            a = alpha[0:128] if j % 2 == 0 else alpha[128:256]
            if cid == 0 and j < 2:
                alc[:, j] = 1.0
                onc[:, j] = 0.0
            else:
                alc[:, j] = a
                onc[:, j] = 1.0 - a
        mkc = np.zeros((128, 20), dtype=np.float32)
        p = np.arange(128)
        for w in range(5):
            for kt in range(4):
                kp = q0 - HALO + 256 * w + 128 * kt + p
                mkc[:, 4 * w + kt] = np.where((kp >= 0) & (kp < S), 0.0, NEG)
        in_maps.append(dict(
            x_ext=xe, wqt=wqt, wot=wot, wkt=wkt, wvt=wvt,
            qg=qgT, qu=quT, qd=qdT, zg=zgT, sg=sgT, zu=zuT, su=suT,
            zd=zdT, sd=sdT,
            cosq=cq, sinq=sq, cosk=ck, sink=sk,
            alpc=alc, onec=onc, maskc=mkc, idf=idf,
        ))

    nc = build_program()
    res = run_bass_kernel_spmd(nc, in_maps, core_ids=list(range(NCORE)))
    outs = [np.asarray(res.results[cid]["out"], dtype=np.float32)
            for cid in range(NCORE)]
    return np.concatenate(outs, axis=0).reshape(1, S, D)


# version 18
# speedup vs baseline: 1.0024x; 1.0024x over previous
import contextlib
import numpy as np
import ml_dtypes
import concourse.bass as bass
import concourse.mybir as mybir
from concourse.bass_utils import run_bass_kernel_spmd

F32 = mybir.dt.float32
BF16 = mybir.dt.bfloat16
U8 = mybir.dt.uint8
AF = mybir.ActivationFunctionType
ALU = mybir.AluOpType
AXX = mybir.AxisListType.X

S = 8192
D = 4096
H = 32
DH = 128
HFF = 11008
NCORE = 8
TOWN = S // NCORE
HALO = 256
TEXT = TOWN + 2 * HALO
NDT = D // 128
NHT = HFF // 128
NG_IN = 32
NG_OUT = 86
SCALE = 1.0 / float(np.sqrt(DH))
NEG = -1e9

ENGS = ("sync", "tensor", "vector", "scalar", "gpsimd")


class Sem:
    def __init__(self, h):
        self.h = h
        self.n = 0


class Plan:
    def __init__(self):
        self.q = {e: [] for e in ENGS}

    def emit(self, eng, fn):
        self.q[eng].append(fn)

    def wait(self, eng, sem, val):
        if val > 0:
            self.emit(eng, lambda e, s=sem.h, v=val: e.wait_ge(s, v))

    def dma(self, eng, out, in_, sem, transpose=False):
        def f(e, o=out, i=in_, s=sem.h, t=transpose):
            e.dma_start(out=o, in_=i, transpose=t).then_inc(s, 16)
        self.emit(eng, f)
        sem.n += 16
        return sem.n

    def vsy(self, eng, sem):
        self.wait(eng, sem, sem.n)

    def op(self, eng, meth, *args, inc=None, inc_drain=None, **kw):
        def f(e, m=meth, a=args, k=kw, i=inc, d=inc_drain):
            ins = getattr(e, m)(*a, **k)
            if i is not None:
                ins.then_inc(i.h, 1)
            if d is not None:
                e.drain().then_inc(d.h, 1)
        self.emit(eng, f)
        if inc is not None:
            inc.n += 1
        if inc_drain is not None:
            inc_drain.n += 1
        if inc is not None or inc_drain is not None:
            return (inc or inc_drain).n


class Ctx:
    pass


DBG_OUT = ()


def _mk(c: Ctx):
    nc = c.nc
    ap = lambda *a, **k: nc.dram_tensor(*a, **k).ap()
    c.x_ext = ap("x_ext", [TEXT, D], F32, kind="ExternalInput")
    c.wqt = ap("wqt", [H * 128, NDT * 128], F32, kind="ExternalInput")
    c.wot = ap("wot", [NDT * 128, NDT * 128], F32, kind="ExternalInput")
    c.wkt = ap("wkt", [128, NDT * DH], F32, kind="ExternalInput")
    c.wvt = ap("wvt", [128, NDT * DH], F32, kind="ExternalInput")
    c.qg = ap("qg", [HFF, D], U8, kind="ExternalInput")
    c.qu = ap("qu", [HFF, D], U8, kind="ExternalInput")
    c.qd = ap("qd", [D, HFF], U8, kind="ExternalInput")
    c.zg = ap("zg", [HFF, NG_IN], U8, kind="ExternalInput")
    c.sg = ap("sg", [HFF, NG_IN], F32, kind="ExternalInput")
    c.zu = ap("zu", [HFF, NG_IN], U8, kind="ExternalInput")
    c.su = ap("su", [HFF, NG_IN], F32, kind="ExternalInput")
    c.zd = ap("zd", [D, NG_OUT], U8, kind="ExternalInput")
    c.sd = ap("sd", [D, NG_OUT], F32, kind="ExternalInput")
    c.cosq = ap("cosq", [128, TOWN], F32, kind="ExternalInput")
    c.sinq = ap("sinq", [128, TOWN], F32, kind="ExternalInput")
    c.cosk = ap("cosk", [128, TEXT], F32, kind="ExternalInput")
    c.sink = ap("sink", [128, TEXT], F32, kind="ExternalInput")
    c.alpc = ap("alpc", [128, 8], F32, kind="ExternalInput")
    c.onec = ap("onec", [128, 8], F32, kind="ExternalInput")
    c.maskc = ap("maskc", [128, 20], F32, kind="ExternalInput")
    c.idf = ap("idf", [128, 128], F32, kind="ExternalInput")
    c.out = ap("out", [TOWN, D], F32, kind="ExternalOutput")
    kk = lambda n: "ExternalOutput" if n in DBG_OUT else "Internal"
    c.xn_stash = ap("xn_stash", [TEXT, D], BF16, kind=kk("xn_stash"))
    c.gu_stash = ap("gu_stash", [2 * HFF, D], BF16, kind=kk("gu_stash"))
    c.wd_stash = ap("wd_stash", [D, HFF], BF16, kind=kk("wd_stash"))
    c.h_stash = ap("h_stash", [HFF, TOWN], BF16, kind=kk("h_stash"))
    c.projT_stash = ap("projT_st", [D, TOWN], F32, kind=kk("projT_st"))
    c.sum_stash = ap("sum_st", [D, TOWN], F32, kind=kk("sum_st"))

    sb = c.sb
    c.xnT_own = sb("xnT_own", [128, NDT * TOWN], BF16)
    c.alp_sb = sb("alp_sb", [128, 8], F32)
    c.one_sb = sb("one_sb", [128, 8], F32)
    c.msk_sb = sb("msk_sb", [128, 20], F32)
    c.idf_sb = sb("idf_sb", [128, 128], F32)
    c.bstack = contextlib.ExitStack()
    bsb = lambda n, s, d: c.bstack.enter_context(
        nc.sbuf_tensor(n, s, d, side="right"))
    c.v_aug = bsb("v_aug", [128, 12 * 132], BF16)
    c.krT = bsb("krT", [128, TEXT], BF16)
    c.cq_sb = bsb("cq_sb", [128, TOWN], F32)
    c.sq_sb = bsb("sq_sb", [128, TOWN], F32)
    c.g_halo = nc.sbuf_tensor("xnT_halo", [128, NDT * 512], BF16, side="right")
    c.xnT_halo = c.g_halo.__enter__()
    c.pb = [c.stack.enter_context(nc.psum_tensor(f"pb{i}", [128, 512], F32))
            for i in range(8)]

    for name in ("s_ld", "s_xt", "s_mm", "s_ve", "s_ac", "s_vsy"):
        setattr(c, name, Sem(c.stack.enter_context(nc.semaphore(name))))

    def sems(name, k):
        return [Sem(c.stack.enter_context(nc.semaphore(f"{name}{i}")))
                for i in range(k)]
    c.ldx = sems("ldx", 2)
    c.xw = sems("xw", 2)
    c.gq = sems("gq", 3)
    c.lq = sems("lq", 4)
    c.swl = sems("swl", 4)
    c.ldq = sems("ldq", 2)
    c.swd = sems("swd", 2)
    c.pan = sems("pan", 2)
    c.pans = sems("pans", 2)
    c.hw = sems("hw", 2)
    c.wd = sems("wd", 4)
    c.hh = sems("hh", 4)
    c.spr = sems("spr", 1)[0]
    c.sm = sems("sm", 2)
    c.pj = sems("pj", 2)
    c.dx = sems("dx", 2)
    c.stl = sems("stl", 2)
    c.ou = sems("ou", 2)
    c.bars = [Sem(c.stack.enter_context(nc.semaphore(f"bar{i}")))
              for i in range(10)]
    c.bar_i = 0


def barrier(c):
    sem = c.bars[c.bar_i]
    c.bar_i += 1
    for e in ENGS:
        def f(en, h=sem.h):
            en.drain()
            en.sem_inc(h, 1)
        c.P.emit(e, f)
    sem.n += 5
    for e in ENGS:
        c.P.wait(e, sem, sem.n)


def phase_a(c):
    P, nc = c.P, c.nc
    P.dma("sync", c.cq_sb[:], c.cosq, c.s_ld)
    P.dma("sync", c.sq_sb[:], c.sinq, c.s_ld)
    P.dma("sync", c.alp_sb[:], c.alpc, c.s_ld)
    P.dma("sync", c.one_sb[:], c.onec, c.s_ld)
    P.dma("sync", c.msk_sb[:], c.maskc, c.s_ld)
    P.dma("sync", c.idf_sb[:], c.idf, c.s_ld)

    ph = contextlib.ExitStack()
    t = lambda n, s, d: ph.enter_context(nc.sbuf_tensor(n, s, d))
    x_sb = [t("x_sb0", [128, D], F32), t("x_sb1", [128, D], F32)]
    xc_sb = t("xc_sb", [128, D], F32)
    sqr_sb = t("sqr_sb", [128, D], F32)
    xnb = [t("xnb0", [128, D], BF16), t("xnb1", [128, D], BF16)]
    lncol = t("lncol", [128, 4], F32)

    NSTRIP = TEXT // 128
    for st in range(NSTRIP):
        par = st % 2
        if st >= 2:
            P.wait("sync", c.xw[par], 16 * (st // 2))
        P.dma("sync", x_sb[par][:], c.x_ext[128 * st:128 * (st + 1), :],
              c.ldx[par])
        P.wait("vector", c.ldx[par], c.ldx[par].n)
        P.wait("vector", c.s_ac, 2 * st)
        if st >= 2:
            P.wait("vector", c.xw[par], 16 * (st // 2))
        P.op("vector", "reduce_sum", lncol[:, 0:1], x_sb[par][:], AXX,
             inc=c.s_vsy)
        P.vsy("vector", c.s_vsy)
        P.op("vector", "tensor_scalar", lncol[:, 1:2], lncol[:, 0:1],
             -1.0 / D, None, ALU.mult, inc=c.s_vsy)
        P.vsy("vector", c.s_vsy)
        P.op("vector", "tensor_scalar", xc_sb[:], x_sb[par][:],
             lncol[:, 1:2], None, ALU.add, inc=c.s_ve)
        P.wait("scalar", c.s_ve, c.s_ve.n)
        P.op("scalar", "activation", sqr_sb[:], xc_sb[:], AF.Square,
             inc=c.s_ac)
        P.wait("vector", c.s_ac, c.s_ac.n)
        P.op("vector", "reduce_sum", lncol[:, 2:3], sqr_sb[:], AXX,
             inc=c.s_vsy)
        P.vsy("vector", c.s_vsy)
        P.op("vector", "tensor_scalar", lncol[:, 2:3], lncol[:, 2:3],
             1.0 / D, 1e-5, ALU.mult, ALU.add, inc=c.s_ve)
        P.wait("scalar", c.s_ve, c.s_ve.n)
        P.op("scalar", "activation", lncol[:, 3:4], lncol[:, 2:3], AF.Sqrt,
             inc=c.s_ac)
        P.wait("vector", c.s_ac, c.s_ac.n)
        P.op("vector", "reciprocal", lncol[:, 0:1], lncol[:, 3:4],
             inc=c.s_vsy)
        P.vsy("vector", c.s_vsy)
        P.op("vector", "tensor_scalar", xnb[par][:], xc_sb[:],
             lncol[:, 0:1], None, ALU.mult, inc_drain=c.s_ve)
        P.wait("gpsimd", c.s_ve, c.s_ve.n)
        P.dma("gpsimd", c.xn_stash[128 * st:128 * (st + 1), :], xnb[par][:],
              c.xw[par])

    P.wait("sync", c.xw[0], 16 * (NSTRIP // 2))
    P.wait("sync", c.xw[1], 16 * (NSTRIP // 2))
    for dt in range(NDT):
        c0, c1 = 128 * dt, 128 * (dt + 1)
        P.dma("sync", c.xnT_own[:, TOWN * dt:TOWN * (dt + 1)],
              c.xn_stash[HALO:HALO + TOWN, c0:c1], c.s_xt, transpose=True)
        P.dma("sync", c.xnT_halo[:, 512 * dt:512 * dt + 256],
              c.xn_stash[0:HALO, c0:c1], c.s_xt, transpose=True)
        P.dma("sync", c.xnT_halo[:, 512 * dt + 256:512 * dt + 512],
              c.xn_stash[TEXT - HALO:TEXT, c0:c1], c.s_xt, transpose=True)
    ph.close()


def phase_b1(c):
    P, nc, pb = c.P, c.nc, c.pb
    ph = contextlib.ExitStack()
    t = lambda n, s, d: ph.enter_context(nc.sbuf_tensor(n, s, d))
    ck_sb = t("ck_sb", [128, TEXT], F32)
    sk_sb = t("sk_sb", [128, TEXT], F32)
    wk_bf = t("wk_bf", [128, NDT * DH], BF16)
    wv_bf = t("wv_bf", [128, NDT * DH], BF16)
    kf_sb = t("kf_sb", [128, 512], F32)
    ksw_sb = t("ksw_sb", [128, 512], F32)
    kt1_sb = t("kt1_sb", [128, 512], F32)

    P.dma("sync", ck_sb[:], c.cosk, c.s_ld)
    P.dma("sync", sk_sb[:], c.sink, c.s_ld)
    P.dma("gpsimd", wk_bf[:], c.wkt, c.s_ld)
    P.dma("gpsimd", wv_bf[:], c.wvt, c.s_ld)
    P.wait("tensor", c.s_ld, c.s_ld.n)
    P.wait("vector", c.s_ld, c.s_ld.n)
    P.wait("scalar", c.s_ld, c.s_ld.n)
    P.wait("tensor", c.s_xt, c.s_xt.n)

    def ext_rhs_pieces(lo, hi):
        segs = []
        if lo < 256:
            e = min(hi, 256)
            segs.append(("halo", lo, e - lo))
            lo = e
        if lo < hi and lo < 1280:
            e = min(hi, 1280)
            segs.append(("own", lo - 256, e - lo))
            lo = e
        if lo < hi:
            segs.append(("halo", 256 + (lo - 1280), hi - lo))
        return segs

    for ch in range(3):
        segs = ext_rhs_pieces(512 * ch, 512 * (ch + 1))
        off = 0
        for kind, start, width in segs:
            for dt in range(NDT):
                if kind == "own":
                    rhs = c.xnT_own[:, TOWN * dt + start:TOWN * dt + start + width]
                else:
                    rhs = c.xnT_halo[:, 512 * dt + start:512 * dt + start + width]
                P.op("tensor", "matmul", pb[ch][:, off:off + width],
                     wk_bf[:, 128 * dt:128 * (dt + 1)], rhs,
                     start=(dt == 0), stop=(dt == NDT - 1),
                     inc=(c.s_mm if dt == NDT - 1 else None))
            off += width
        P.wait("vector", c.s_mm, c.s_mm.n)
        cc = ck_sb[:, 512 * ch:512 * (ch + 1)]
        scs = sk_sb[:, 512 * ch:512 * (ch + 1)]
        P.op("vector", "tensor_copy", kf_sb[:], pb[ch][:])
        P.op("vector", "tensor_copy", ksw_sb[0:64, :], kf_sb[64:128, :])
        P.op("vector", "tensor_copy", ksw_sb[64:128, :], kf_sb[0:64, :])
        P.op("vector", "tensor_mul", kt1_sb[:], kf_sb[:], cc)
        P.op("vector", "tensor_mul", ksw_sb[:], ksw_sb[:], scs)
        P.op("vector", "tensor_add", c.krT[:, 512 * ch:512 * (ch + 1)],
             kt1_sb[:], ksw_sb[:], inc=c.s_ve)
    k_done = c.s_ve.n

    for tt in range(12):
        lo = 128 * tt
        if lo < 256:
            def lhs_t(dt, lo=lo):
                return c.xnT_halo[:, 512 * dt + lo:512 * dt + lo + 128]
        elif lo < 1280:
            def lhs_t(dt, lo=lo):
                o = lo - 256
                return c.xnT_own[:, TOWN * dt + o:TOWN * dt + o + 128]
        else:
            def lhs_t(dt, lo=lo):
                o = 256 + lo - 1280
                return c.xnT_halo[:, 512 * dt + o:512 * dt + o + 128]
        bank = pb[3 + (tt % 2)]
        if tt >= 2:
            P.wait("tensor", c.s_ve, k_done + tt - 1)
        for dt in range(NDT):
            P.op("tensor", "matmul", bank[:, 0:128],
                 lhs_t(dt), wv_bf[:, 128 * dt:128 * (dt + 1)],
                 start=(dt == 0), stop=(dt == NDT - 1),
                 inc=(c.s_mm if dt == NDT - 1 else None))
        P.wait("vector", c.s_mm, c.s_mm.n)
        P.op("vector", "tensor_copy",
             c.v_aug[:, 132 * tt:132 * tt + 128], bank[:, 0:128])
        P.op("vector", "memset",
             c.v_aug[:, 132 * tt + 128:132 * tt + 129], 1.0, inc=c.s_ve)
    ph.close()


WIN_Q = [(0, 256), (0, 512), (256, 768), (512, 1024), (768, 1024)]
WIN_SLOT0 = [2, 0, 0, 0, 0]


def phase_b2(c, attn_outT):
    P, nc, pb = c.P, c.nc, c.pb
    ph = contextlib.ExitStack()
    t = lambda n, s, d: ph.enter_context(nc.sbuf_tensor(n, s, d))
    wq_bf = [t(f"wq_bf{i}", [128, NDT * 128], BF16) for i in range(3)]
    qf_sb = t("qf_sb", [128, TOWN], F32)
    qsw_sb = t("qsw_sb", [128, TOWN], F32)
    qt1_sb = t("qt1_sb", [128, TOWN], F32)
    qrT = [t("qrT0", [128, TOWN], BF16), t("qrT1", [128, TOWN], BF16)]
    attnT = t("attnT", [128, 2048], BF16)
    wbuf = [t("wb0", [128, 4 * 132], F32), t("wb1", [128, 4 * 132], F32)]
    bcol = t("bcol", [128, 4], F32)
    blf = t("blf", [128, 128], F32)

    sig_done = {k: 0 for k in range(4)}
    av_bank_evac = {k: 0 for k in range(4)}
    tr_bank_evac = {0: 0, 1: 0}
    blend_prog = 0
    av_mm_done = 0
    gq_val = {}
    rope_done = {}
    qmm_done = {}

    def q_chunks(h):
        par = h % 3
        chunks = []
        for th in range(2):
            for cb in range(8):
                def f(th=th, cb=cb, par=par, h=h):
                    if th == 0 and cb == 0:
                        P.wait("tensor", c.gq[par], gq_val[h])
                        if h >= 1:
                            P.wait("tensor", c.s_ve, rope_done[h - 1])
                    for dt in range(4 * cb, 4 * cb + 4):
                        P.op("tensor", "matmul", pb[4 + th][:, 0:512],
                             wq_bf[par][:, 128 * dt:128 * (dt + 1)],
                             c.xnT_own[:, TOWN * dt + 512 * th:
                                       TOWN * dt + 512 * th + 512],
                             start=(dt == 0), stop=(dt == NDT - 1),
                             inc=(c.s_mm if dt == NDT - 1 and th == 1
                                  else None))
                    if th == 1 and cb == 7:
                        qmm_done[h] = c.s_mm.n
                chunks.append(f)
        return chunks

    def emit_rope(h):
        par = h % 2
        P.wait("vector", c.s_mm, qmm_done[h])
        P.op("vector", "tensor_copy", qf_sb[:, 0:512], pb[4][:])
        P.op("vector", "tensor_copy", qf_sb[:, 512:1024], pb[5][:])
        P.op("vector", "tensor_copy", qsw_sb[0:64, :], qf_sb[64:128, :])
        P.op("vector", "tensor_copy", qsw_sb[64:128, :], qf_sb[0:64, :])
        P.op("vector", "tensor_mul", qt1_sb[:], qf_sb[:], c.cq_sb[:])
        P.op("vector", "tensor_mul", qsw_sb[:], qsw_sb[:], c.sq_sb[:])
        P.op("vector", "tensor_add", qrT[par][:], qt1_sb[:], qsw_sb[:],
             inc=c.s_ve)
        rope_done[h] = c.s_ve.n

    for hh in range(3):
        P.dma("gpsimd", wq_bf[hh][:], c.wqt[128 * hh:128 * (hh + 1), :],
              c.gq[hh])
        gq_val[hh] = c.gq[hh].n
    for f in q_chunks(0):
        f()
    emit_rope(0)

    for h in range(H):
        par = h % 3
        pump = q_chunks(h + 1) if h + 1 < H else []
        pump_i = [0]

        def do_pump(n=1):
            for _ in range(n):
                if pump_i[0] < len(pump):
                    pump[pump_i[0]]()
                    pump_i[0] += 1

        if h + 3 < H:
            P.wait("gpsimd", c.s_mm, qmm_done[h])
            P.dma("gpsimd", wq_bf[par][:],
                  c.wqt[128 * (h + 3):128 * (h + 4), :], c.gq[par])
            gq_val[h + 3] = c.gq[par].n

        for w in range(5):
            qlo, qhi = WIN_Q[w]
            qn = qhi - qlo
            wslot = wbuf[w % 2]
            for kt in range(4):
                if kt == 0 and w == 0:
                    P.wait("tensor", c.s_ve, rope_done[h])
                P.wait("tensor", c.s_ac, av_bank_evac[kt])
                P.wait("tensor", c.s_ac, sig_done[kt])
                P.op("tensor", "matmul", pb[kt][:, 0:qn],
                     c.krT[:, 256 * w + 128 * kt:256 * w + 128 * (kt + 1)],
                     qrT[h % 2][:, qlo:qhi], start=True, stop=True,
                     inc=c.s_mm)
            sc_done = c.s_mm.n
            do_pump(2)
            for kt in range(4):
                P.wait("scalar", c.s_mm, sc_done - 3 + kt)
                P.wait("scalar", c.s_mm, av_mm_done)
                P.wait("scalar", c.s_ve, blend_prog)
                P.op("scalar", "activation", attnT[:, 512 * kt:512 * kt + qn],
                     pb[kt][:, 0:qn], AF.Sigmoid,
                     bias=c.msk_sb[:, 4 * w + kt:4 * w + kt + 1],
                     scale=SCALE, inc=c.s_ac)
                sig_done[kt] = c.s_ac.n
            nq = qn // 128
            for qs in range(nq):
                bank = pb[qs]
                P.wait("tensor", c.s_ac, sig_done[3])
                for kt in range(4):
                    P.op("tensor", "matmul", bank[:, 0:129],
                         attnT[:, 512 * kt + 128 * qs:512 * kt + 128 * (qs + 1)],
                         c.v_aug[:, 132 * (2 * w + kt):132 * (2 * w + kt) + 129],
                         start=(kt == 0), stop=(kt == 3),
                         inc=(c.s_mm if kt == 3 else None))
                av_mm_done = c.s_mm.n
                do_pump(1)
                P.wait("scalar", c.s_mm, av_mm_done)
                slot = WIN_SLOT0[w] + qs
                P.op("scalar", "copy", wslot[:, 132 * slot:132 * slot + 129],
                     bank[:, 0:129], inc=c.s_ac)
                av_bank_evac[qs] = c.s_ac.n

            if w == 2 and h + 1 < H:
                emit_rope(h + 1)

            if w >= 1:
                k = w - 1
                A = wbuf[(w - 1) % 2]
                B = wbuf[w % 2]
                for qs in range(2):
                    aslot = 2 + qs
                    bslot = qs
                    col = 2 * k + qs
                    P.wait("vector", c.s_ac, av_bank_evac[bslot])
                    P.op("vector", "reciprocal", bcol[:, 0:1],
                         A[:, 132 * aslot + 128:132 * aslot + 129])
                    P.op("vector", "reciprocal", bcol[:, 1:2],
                         B[:, 132 * bslot + 128:132 * bslot + 129],
                         inc=c.s_vsy)
                    P.vsy("vector", c.s_vsy)
                    P.op("vector", "tensor_mul", bcol[:, 2:3],
                         bcol[:, 0:1], c.one_sb[:, col:col + 1])
                    P.op("vector", "tensor_mul", bcol[:, 3:4],
                         bcol[:, 1:2], c.alp_sb[:, col:col + 1], inc=c.s_vsy)
                    P.vsy("vector", c.s_vsy)
                    P.op("vector", "tensor_scalar", blf[:],
                         A[:, 132 * aslot:132 * aslot + 128],
                         bcol[:, 2:3], None, ALU.mult)
                    P.op("vector", "scalar_tensor_tensor", blf[:],
                         B[:, 132 * bslot:132 * bslot + 128],
                         bcol[:, 3:4], blf[:], ALU.mult, ALU.add, inc=c.s_ve)
                    tbank = 6 + (qs % 2)
                    P.wait("tensor", c.s_ve, c.s_ve.n)
                    P.wait("tensor", c.s_ve, tr_bank_evac[qs % 2])
                    P.op("tensor", "transpose", pb[tbank][:, 0:128], blf[:],
                         c.idf_sb[:], inc=c.s_mm)
                    P.wait("vector", c.s_mm, c.s_mm.n)
                    P.op("vector", "tensor_copy",
                         attn_outT[:, TOWN * h + 128 * col:
                                   TOWN * h + 128 * (col + 1)],
                         pb[tbank][:, 0:128], inc=c.s_ve)
                    tr_bank_evac[qs % 2] = c.s_ve.n
                blend_prog = c.s_ve.n
        do_pump(16)
    ph.close()


def phase_b3(c, attn_outT, dq):
    P, nc, pb = c.P, c.nc, c.pb
    ph = contextlib.ExitStack()
    t = lambda n, s, d: ph.enter_context(nc.sbuf_tensor(n, s, d))
    wo_bf = [t(f"wo_bf{i}", [128, NDT * 128], BF16) for i in range(3)]
    pr_sb = [t("pr_sb0", [128, TOWN], F32), t("pr_sb1", [128, TOWN], F32)]
    wo_mm = {}
    gq_val = {}
    for hh in range(3):
        P.dma("gpsimd", wo_bf[hh][:], c.wot[128 * hh:128 * (hh + 1), :],
              c.gq[hh])
        gq_val[hh] = c.gq[hh].n
    for i in range(NDT):
        par = i % 2
        wpar = i % 3
        P.wait("tensor", c.gq[wpar], gq_val[i])
        if i >= 2:
            P.wait("tensor", c.pj[par], 16 * (i // 2))
        for th in range(2):
            bank = pb[2 * (i % 2) + th]
            for ot in range(NDT):
                P.op("tensor", "matmul", bank[:, 0:512],
                     wo_bf[wpar][:, 128 * ot:128 * (ot + 1)],
                     attn_outT[:, TOWN * ot + 512 * th:
                               TOWN * ot + 512 * (th + 1)],
                     start=(ot == 0), stop=(ot == NDT - 1),
                     inc=(c.s_mm if ot == NDT - 1 else None))
        wo_mm[i] = c.s_mm.n
        if i + 3 < NDT:
            P.wait("gpsimd", c.s_mm, wo_mm[i])
            P.dma("gpsimd", wo_bf[wpar][:],
                  c.wot[128 * (i + 3):128 * (i + 4), :], c.gq[wpar])
            gq_val[i + 3] = c.gq[wpar].n
        P.wait("vector", c.s_mm, c.s_mm.n)
        if i >= 2:
            P.wait("vector", c.pj[par], 16 * (i // 2))
        P.op("vector", "tensor_copy", pr_sb[par][:, 0:512], pb[2 * (i % 2)][:])
        P.op("vector", "tensor_copy", pr_sb[par][:, 512:1024],
             pb[2 * (i % 2) + 1][:], inc_drain=c.s_ve)
        P.wait("gpsimd", c.s_ve, c.s_ve.n)
        P.dma("gpsimd", c.projT_stash[128 * i:128 * (i + 1), :], pr_sb[par][:],
              c.pj[par])
        if i < 16:
            dq.load(i)
        if i % 2 == 1 and i // 2 < 12:
            dq.compute(i // 2)
    ph.close()


class DqGU:

    def __init__(self, c, scope):
        nc = c.nc
        t = lambda n, s, d: scope.enter_context(
            nc.sbuf_tensor(n, s, d, side="right"))
        self.qin = [t(f"qin{i}", [128, D], U8) for i in range(4)]
        self.qo = [t(f"qo{i}", [128, D], BF16) for i in range(2)]
        self.zin = t("zin", [128, 4 * 32], U8)
        self.sin_ = t("sin_", [128, 4 * 32], F32)
        self.zf = t("zf", [128, 4 * 32], F32)
        self.c = c
        self.ld = {}

    def load(self, q):
        c, P = self.c, self.c.P
        jj, kind = q // 2, q % 2
        lane = q % 4
        qw = c.qg if kind == 0 else c.qu
        zt = c.zg if kind == 0 else c.zu
        st_ = c.sg if kind == 0 else c.su
        r0, r1 = 128 * jj, 128 * (jj + 1)
        if q >= 4:
            P.wait("sync", c.swl[q % 2], 16 * (q // 2 - 1))
        P.dma("sync", self.qin[lane][:], qw[r0:r1, :], c.lq[lane])
        P.dma("sync", self.zin[:, 32 * lane:32 * (lane + 1)], zt[r0:r1, :],
              c.lq[lane])
        P.dma("sync", self.sin_[:, 32 * lane:32 * (lane + 1)], st_[r0:r1, :],
              c.lq[lane])
        self.ld[q] = c.lq[lane].n

    def compute(self, q):
        c, P = self.c, self.c.P
        jj, kind = q // 2, q % 2
        lane = q % 4
        olane = q % 2
        s0 = 256 * jj + 128 * kind
        P.wait("vector", c.lq[lane], self.ld[q])
        if q >= 2:
            P.wait("vector", c.swl[olane], 16 * (q // 2))
        P.op("vector", "tensor_copy", self.zf[:, 32 * lane:32 * (lane + 1)],
             self.zin[:, 32 * lane:32 * (lane + 1)], inc=c.s_vsy)
        P.vsy("vector", c.s_vsy)
        for g in range(NG_IN):
            csl = slice(128 * g, 128 * (g + 1))
            zc = slice(32 * lane + g, 32 * lane + g + 1)
            P.op("vector", "tensor_scalar", self.qo[olane][:, csl],
                 self.qin[lane][:, csl], self.zf[:, zc], self.sin_[:, zc],
                 ALU.subtract, ALU.mult,
                 inc_drain=(c.s_ve if g == NG_IN - 1 else None))
        P.wait("gpsimd", c.s_ve, c.s_ve.n)
        P.dma("gpsimd", c.gu_stash[s0:s0 + 128, :], self.qo[olane][:],
              c.swl[olane])


class DqDown:

    def __init__(self, c, scope):
        nc = c.nc
        HH = HFF // 2
        t = lambda n, s, d: scope.enter_context(nc.sbuf_tensor(n, s, d))
        self.qind = [t(f"qind{i}", [128, HH], U8) for i in range(2)]
        self.qodd = [t(f"qodd{i}", [128, HH], BF16) for i in range(2)]
        self.zind = t("zind", [128, 2 * NG_OUT], U8)
        self.sind = t("sind", [128, 2 * NG_OUT], F32)
        self.zfd = t("zfd", [128, 2 * NG_OUT], F32)
        self.c = c
        self.ld = {}

    def load(self, dj):
        c, P = self.c, self.c.P
        HH = HFF // 2
        dt, half = dj // 2, dj % 2
        lane = dj % 2
        r0, r1 = 128 * dt, 128 * (dt + 1)
        h0 = HH * half
        if dj >= 2:
            P.wait("sync", c.swd[lane], 16 * (dj // 2))
        P.dma("sync", self.zind[:, NG_OUT * lane:NG_OUT * (lane + 1)],
              c.zd[r0:r1, :], c.ldq[lane])
        P.dma("sync", self.sind[:, NG_OUT * lane:NG_OUT * (lane + 1)],
              c.sd[r0:r1, :], c.ldq[lane])
        P.dma("sync", self.qind[lane][:], c.qd[r0:r1, h0:h0 + HH],
              c.ldq[lane])
        self.ld[dj] = c.ldq[lane].n

    def compute(self, dj):
        c, P = self.c, self.c.P
        HH = HFF // 2
        NGH = NG_OUT // 2
        dt, half = dj // 2, dj % 2
        lane = dj % 2
        r0, r1 = 128 * dt, 128 * (dt + 1)
        h0 = HH * half
        P.wait("vector", c.ldq[lane], self.ld[dj])
        P.op("vector", "tensor_copy",
             self.zfd[:, NG_OUT * lane:NG_OUT * (lane + 1)],
             self.zind[:, NG_OUT * lane:NG_OUT * (lane + 1)], inc=c.s_vsy)
        P.vsy("vector", c.s_vsy)
        for gg in range(NGH):
            g = NGH * half + gg
            csl = slice(128 * gg, 128 * (gg + 1))
            zc = slice(NG_OUT * lane + g, NG_OUT * lane + g + 1)
            P.op("vector", "tensor_scalar", self.qodd[lane][:, csl],
                 self.qind[lane][:, csl], self.zfd[:, zc], self.sind[:, zc],
                 ALU.subtract, ALU.mult,
                 inc_drain=(c.s_ve if gg == NGH - 1 else None))
        P.wait("gpsimd", c.s_ve, c.s_ve.n)
        P.dma("gpsimd", c.wd_stash[r0:r1, h0:h0 + HH], self.qodd[lane][:],
              c.swd[lane])


def phase_c1(c, dq, dqd):
    P, nc, pb = c.P, c.nc, c.pb
    ph = contextlib.ExitStack()
    t = lambda n, s, d: ph.enter_context(nc.sbuf_tensor(n, s, d))
    pan = [t("pan0", [128, NDT * 512], BF16),
           t("pan1", [128, NDT * 512], BF16)]
    sg_sb = [t("sg_sb0", [128, TOWN], BF16), t("sg_sb1", [128, TOWN], BF16)]
    h_sb = [t("h_sb0", [128, TOWN], BF16), t("h_sb1", [128, TOWN], BF16)]
    NPAN = NHT // 2
    pan_ld = {}
    gu_mm = {}
    h_ve = {}
    NGU = 2 * NHT
    NDJ = 2 * NDT
    for dj in range(2):
        dqd.load(dj)
    for J in range(NPAN):
        ppar = J % 2
        for q in range(4 * J + 12, min(4 * J + 16, NGU)):
            dq.compute(q)
        for dj in range(2 * J, min(2 * J + 2, NDJ)):
            dqd.compute(dj)
        if J >= 2:
            P.wait("sync", c.s_mm, gu_mm[2 * (J - 2) + 1])
        P.wait("sync", c.swl[0], 16 * min(2 * J + 2, NHT))
        P.wait("sync", c.swl[1], 16 * min(2 * J + 2, NHT))
        for dt in range(NDT):
            P.dma("sync", pan[ppar][:, 512 * dt:512 * (dt + 1)],
                  c.gu_stash[512 * J:512 * (J + 1), 128 * dt:128 * (dt + 1)],
                  c.pan[ppar], transpose=True)
        pan_ld[J] = c.pan[ppar].n
        for r in range(2):
            jj = 2 * J + r
            jpar = jj % 2
            P.wait("tensor", c.pan[ppar], pan_ld[J])
            if jj >= 2:
                P.wait("tensor", c.s_ve, h_ve[jj - 2])
            for dt in range(NDT):
                glhs = pan[ppar][:, 512 * dt + 256 * r:
                                512 * dt + 256 * r + 128]
                ulhs = pan[ppar][:, 512 * dt + 256 * r + 128:
                                512 * dt + 256 * r + 256]
                for th in range(2):
                    P.op("tensor", "matmul", pb[4 * jpar + th][:, 0:512],
                         glhs,
                         c.xnT_own[:, TOWN * dt + 512 * th:
                                   TOWN * dt + 512 * (th + 1)],
                         start=(dt == 0), stop=(dt == NDT - 1),
                         inc=(c.s_mm if dt == NDT - 1 and th == 1 else None))
                for th in range(2):
                    P.op("tensor", "matmul",
                         pb[4 * jpar + 2 + th][:, 0:512],
                         ulhs,
                         c.xnT_own[:, TOWN * dt + 512 * th:
                                   TOWN * dt + 512 * (th + 1)],
                         start=(dt == 0), stop=(dt == NDT - 1),
                         inc=(c.s_mm if dt == NDT - 1 and th == 1 else None))
            gu_mm[jj] = c.s_mm.n
            P.wait("scalar", c.s_mm, gu_mm[jj] - 1)
            if jj >= 2:
                P.wait("scalar", c.s_ve, h_ve[jj - 2])
            P.op("scalar", "activation", sg_sb[jpar][:, 0:512],
                 pb[4 * jpar + 0][:], AF.Silu)
            P.op("scalar", "activation", sg_sb[jpar][:, 512:1024],
                 pb[4 * jpar + 1][:], AF.Silu, inc=c.s_ac)
            P.wait("vector", c.s_ac, c.s_ac.n)
            P.wait("vector", c.s_mm, gu_mm[jj])
            if jj >= 2:
                P.wait("vector", c.hw[jpar], 16 * (jj // 2))
            P.op("vector", "tensor_mul", h_sb[jpar][:, 0:512],
                 sg_sb[jpar][:, 0:512], pb[4 * jpar + 2][:])
            P.op("vector", "tensor_mul", h_sb[jpar][:, 512:1024],
                 sg_sb[jpar][:, 512:1024], pb[4 * jpar + 3][:],
                 inc_drain=c.s_ve)
            h_ve[jj] = c.s_ve.n
            P.wait("gpsimd", c.s_ve, c.s_ve.n)
            P.dma("gpsimd", c.h_stash[128 * jj:128 * (jj + 1), :],
                  h_sb[jpar][:], c.hw[jpar])
        for q in range(4 * J + 16, min(4 * J + 20, NGU)):
            dq.load(q)
        for dj in range(2 * J + 2, min(2 * J + 4, NDJ)):
            dqd.load(dj)
    ph.close()


def phase_c2(c):
    P, nc, pb = c.P, c.nc, c.pb
    ph = contextlib.ExitStack()
    t = lambda n, s, d: ph.enter_context(nc.sbuf_tensor(n, s, d))
    wdt = [t(f"wdt{i}", [128, 512], BF16) for i in range(4)]
    hst = [t(f"hst{i}", [128, TOWN], BF16) for i in range(4)]
    prt = t("prt", [128, 4 * TOWN], F32)
    smt = [t("smt0", [128, TOWN], F32), t("smt1", [128, TOWN], F32)]
    for s in (c.hw[0], c.hw[1], c.pj[0], c.pj[1], c.swd[0], c.swd[1]):
        P.wait("sync", s, s.n)
        P.wait("gpsimd", s, s.n)
    dn_mm = {}
    sum_ve = 0
    for ch in range(8):
        P.wait("sync", c.s_ve, sum_ve)
        for ic in range(4):
            i = 4 * ch + ic
            P.dma("sync", prt[:, TOWN * ic:TOWN * (ic + 1)],
                  c.projT_stash[128 * i:128 * (i + 1), :], c.spr)
        prj_ld = c.spr.n
        for jj in range(NHT):
            lane = jj % 4
            prev = (ch, jj - 4) if jj >= 4 else (ch - 1, 82 + (jj + 2) % 4)
            if prev in dn_mm:
                P.wait("sync", c.s_mm, dn_mm[prev])
                P.wait("gpsimd", c.s_mm, dn_mm[prev])
            P.dma("sync", wdt[lane][:],
                  c.wd_stash[512 * ch:512 * (ch + 1),
                             128 * jj:128 * (jj + 1)], c.wd[lane],
                  transpose=True)
            P.dma("gpsimd", hst[lane][:],
                  c.h_stash[128 * jj:128 * (jj + 1), :], c.hh[lane])
            P.wait("tensor", c.wd[lane], c.wd[lane].n)
            P.wait("tensor", c.hh[lane], c.hh[lane].n)
            if jj == 0 and ch > 0:
                P.wait("tensor", c.s_ve, sum_ve)
            for ic in range(4):
                for th in range(2):
                    P.op("tensor", "matmul", pb[2 * ic + th][:, 0:512],
                         wdt[lane][:, 128 * ic:128 * (ic + 1)],
                         hst[lane][:, 512 * th:512 * (th + 1)],
                         start=(jj == 0), stop=(jj == NHT - 1),
                         inc=(c.s_mm if (th == 1 and ic == 3) else None))
            dn_mm[(ch, jj)] = c.s_mm.n
        P.wait("vector", c.s_mm, dn_mm[(ch, NHT - 1)])
        P.wait("vector", c.spr, prj_ld)
        for ic in range(4):
            i = 4 * ch + ic
            k = 4 * ch + ic
            sl = ic % 2
            if k >= 2:
                P.wait("vector", c.sm[sl], 16 * (k // 2))
            for th in range(2):
                P.op("vector", "tensor_add",
                     smt[sl][:, 512 * th:512 * (th + 1)],
                     pb[2 * ic + th][:],
                     prt[:, TOWN * ic + 512 * th:TOWN * ic + 512 * (th + 1)],
                     inc_drain=(c.s_ve if th == 1 else None))
            P.wait("gpsimd", c.s_ve, c.s_ve.n)
            P.dma("gpsimd", c.sum_stash[128 * i:128 * (i + 1), :],
                  smt[sl][:], c.sm[sl])
        sum_ve = c.s_ve.n
    ph.close()


def phase_d(c):
    P, nc, pb = c.P, c.nc, c.pb
    ph = contextlib.ExitStack()
    t = lambda n, s, d: ph.enter_context(nc.sbuf_tensor(n, s, d))
    xs = [t("xs0", [128, D], F32), t("xs1", [128, D], F32)]
    stile = [t("stile0", [128, 128], F32), t("stile1", [128, 128], F32)]
    ostrip = [t("ostrip0", [128, D], F32), t("ostrip1", [128, D], F32)]
    for s in (c.sm[0], c.sm[1]):
        P.wait("sync", s, s.n)
    tr_mm = {0: 0, 1: 0}
    ev_ve = {0: 0, 1: 0}
    dx_val = {}
    for tt in range(8):
        par = tt % 2
        if tt >= 2:
            P.wait("sync", c.ou[par], 16 * (tt // 2))
        P.dma("sync", xs[par][:],
              c.x_ext[HALO + 128 * tt:HALO + 128 * (tt + 1), :], c.dx[par])
        dx_val[tt] = c.dx[par].n
        for i in range(NDT):
            sl = i % 2
            P.wait("sync", c.s_mm, tr_mm[sl])
            P.dma("sync", stile[sl][:],
                  c.sum_stash[128 * i:128 * (i + 1), 128 * tt:128 * (tt + 1)],
                  c.stl[sl])
            P.wait("tensor", c.stl[sl], c.stl[sl].n)
            P.wait("tensor", c.s_ve, ev_ve[sl])
            P.op("tensor", "transpose", pb[sl][:, 0:128], stile[sl][:],
                 c.idf_sb[:], inc=c.s_mm)
            tr_mm[sl] = c.s_mm.n
            P.wait("vector", c.s_mm, c.s_mm.n)
            if i == 0:
                P.wait("vector", c.dx[par], dx_val[tt])
                if tt >= 2:
                    P.wait("vector", c.ou[par], 16 * (tt // 2))
            P.op("vector", "tensor_add",
                 ostrip[par][:, 128 * i:128 * (i + 1)],
                 pb[sl][:, 0:128], xs[par][:, 128 * i:128 * (i + 1)],
                 **({"inc_drain": c.s_ve} if i == NDT - 1 else {"inc": c.s_ve}))
            ev_ve[sl] = c.s_ve.n
        P.wait("gpsimd", c.s_ve, c.s_ve.n)
        P.dma("gpsimd", c.out[128 * tt:128 * (tt + 1), :], ostrip[par][:],
              c.ou[par])
    P.wait("gpsimd", c.ou[0], c.ou[0].n)
    P.wait("gpsimd", c.ou[1], c.ou[1].n)
    ph.close()


def build_program():
    c = Ctx()
    c.nc = bass.Bass("TRN2", target_bir_lowering=False, debug=False)
    c.P = Plan()
    c.stack = contextlib.ExitStack()
    c.sb = lambda n, s, d: c.stack.enter_context(c.nc.sbuf_tensor(n, s, d))
    _mk(c)

    phase_a(c)
    barrier(c)
    phase_b1(c)
    c.g_halo.__exit__(None, None, None)
    barrier(c)
    g_attn = c.nc.sbuf_tensor("attn_outT", [128, H * TOWN], BF16)
    attn_outT = g_attn.__enter__()
    phase_b2(c, attn_outT)
    c.bstack.close()
    barrier(c)
    dq_scope = contextlib.ExitStack()
    dq = DqGU(c, dq_scope)
    phase_b3(c, attn_outT, dq)
    g_attn.__exit__(None, None, None)
    barrier(c)
    dqd_scope = contextlib.ExitStack()
    dqd = DqDown(c, dqd_scope)
    phase_c1(c, dq, dqd)
    dq_scope.close()
    dqd_scope.close()
    barrier(c)
    phase_c2(c)
    barrier(c)
    phase_d(c)

    with c.nc.Block() as block:
        @block.sync
        def _(e):
            for fn in c.P.q["sync"]:
                fn(e)

        @block.tensor
        def _(e):
            for fn in c.P.q["tensor"]:
                fn(e)

        @block.vector
        def _(e):
            for fn in c.P.q["vector"]:
                fn(e)

        @block.scalar
        def _(e):
            for fn in c.P.q["scalar"]:
                fn(e)

        @block.gpsimd
        def _(e):
            for fn in c.P.q["gpsimd"]:
                fn(e)

    c.stack.close()
    return c.nc


def _rope_tables():
    inv_freq = 1.0 / 10000.0 ** (np.arange(0, DH, 2, dtype=np.float32) / DH)
    t = np.arange(S, dtype=np.float32)
    f = np.outer(t, inv_freq)
    cos = np.zeros((S, DH), dtype=np.float32)
    sin = np.zeros((S, DH), dtype=np.float32)
    cos[:, 0::2] = np.cos(f)
    cos[:, 1::2] = np.cos(f)
    sin[:, 0::2] = np.sin(f)
    sin[:, 1::2] = np.sin(f)
    sinS = sin.copy()
    sinS[:, :DH // 2] *= -1.0
    return cos, sinS


def _u8view(a_int32):
    a = np.ascontiguousarray(a_int32)
    return np.ascontiguousarray(a.view(np.uint8)[:, ::4])


def kernel(L0_attn_x, L0_attn_Wq, L0_attn_Wk, L0_attn_Wv, L0_attn_Wo,
           L0_ffn_qw_gate, L0_ffn_z_gate, L0_ffn_s_gate,
           L0_ffn_qw_up, L0_ffn_z_up, L0_ffn_s_up,
           L0_ffn_qw_down, L0_ffn_z_down, L0_ffn_s_down):
    x = np.ascontiguousarray(
        np.asarray(L0_attn_x, dtype=np.float32).reshape(S, D))
    Wq = np.asarray(L0_attn_Wq, dtype=np.float32)
    Wk = np.asarray(L0_attn_Wk, dtype=np.float32)
    Wv = np.asarray(L0_attn_Wv, dtype=np.float32)
    Wo = np.asarray(L0_attn_Wo, dtype=np.float32)
    wqt = np.ascontiguousarray(
        Wq.reshape(H, 128, NDT, 128).transpose(0, 3, 2, 1)).reshape(H * 128, -1)
    wot = np.ascontiguousarray(
        Wo.reshape(NDT, 128, NDT, 128).transpose(0, 3, 2, 1)).reshape(D, -1)
    wkt = np.ascontiguousarray(
        Wk.reshape(DH, NDT, 128).transpose(2, 1, 0)).reshape(128, -1)
    wvt = np.ascontiguousarray(
        Wv.reshape(DH, NDT, 128).transpose(2, 1, 0)).reshape(128, -1)
    qgT = _u8view(np.ascontiguousarray(np.asarray(L0_ffn_qw_gate).T))
    quT = _u8view(np.ascontiguousarray(np.asarray(L0_ffn_qw_up).T))
    qdT = _u8view(np.ascontiguousarray(np.asarray(L0_ffn_qw_down).T))
    zgT = _u8view(np.ascontiguousarray(np.asarray(L0_ffn_z_gate).T))
    sgT = np.ascontiguousarray(np.asarray(L0_ffn_s_gate, dtype=np.float32).T)
    zuT = _u8view(np.ascontiguousarray(np.asarray(L0_ffn_z_up).T))
    suT = np.ascontiguousarray(np.asarray(L0_ffn_s_up, dtype=np.float32).T)
    zdT = _u8view(np.ascontiguousarray(np.asarray(L0_ffn_z_down).T))
    sdT = np.ascontiguousarray(np.asarray(L0_ffn_s_down, dtype=np.float32).T)

    cos, sinS = _rope_tables()
    alpha = np.linspace(0.0, 1.0, 256, dtype=np.float32)
    idf = np.eye(128, dtype=np.float32)

    in_maps = []
    for cid in range(NCORE):
        q0 = TOWN * cid
        lo, hi = q0 - HALO, q0 + TOWN + HALO
        xe = np.zeros((TEXT, D), dtype=np.float32)
        slo, shi = max(lo, 0), min(hi, S)
        xe[slo - lo:shi - lo, :] = x[slo:shi]
        cq = np.ascontiguousarray(cos[q0:q0 + TOWN].T)
        sq = np.ascontiguousarray(sinS[q0:q0 + TOWN].T)
        ck = np.zeros((128, TEXT), dtype=np.float32)
        sk = np.zeros((128, TEXT), dtype=np.float32)
        ck[:, slo - lo:shi - lo] = cos[slo:shi].T
        sk[:, slo - lo:shi - lo] = sinS[slo:shi].T
        alc = np.empty((128, 8), dtype=np.float32)
        onc = np.empty((128, 8), dtype=np.float32)
        for j in range(8):
            a = alpha[0:128] if j % 2 == 0 else alpha[128:256]
            if cid == 0 and j < 2:
                alc[:, j] = 1.0
                onc[:, j] = 0.0
            else:
                alc[:, j] = a
                onc[:, j] = 1.0 - a
        mkc = np.zeros((128, 20), dtype=np.float32)
        p = np.arange(128)
        for w in range(5):
            for kt in range(4):
                kp = q0 - HALO + 256 * w + 128 * kt + p
                mkc[:, 4 * w + kt] = np.where((kp >= 0) & (kp < S), 0.0, NEG)
        in_maps.append(dict(
            x_ext=xe, wqt=wqt, wot=wot, wkt=wkt, wvt=wvt,
            qg=qgT, qu=quT, qd=qdT, zg=zgT, sg=sgT, zu=zuT, su=suT,
            zd=zdT, sd=sdT,
            cosq=cq, sinq=sq, cosk=ck, sink=sk,
            alpc=alc, onec=onc, maskc=mkc, idf=idf,
        ))

    nc = build_program()
    res = run_bass_kernel_spmd(nc, in_maps, core_ids=list(range(NCORE)))
    outs = [np.asarray(res.results[cid]["out"], dtype=np.float32)
            for cid in range(NCORE)]
    return np.concatenate(outs, axis=0).reshape(1, S, D)


# version 19
# speedup vs baseline: 1.0410x; 1.0385x over previous
import contextlib
import numpy as np
import ml_dtypes
import concourse.bass as bass
import concourse.mybir as mybir
from concourse.bass_utils import run_bass_kernel_spmd

F32 = mybir.dt.float32
BF16 = mybir.dt.bfloat16
U8 = mybir.dt.uint8
AF = mybir.ActivationFunctionType
ALU = mybir.AluOpType
AXX = mybir.AxisListType.X

S = 8192
D = 4096
H = 32
DH = 128
HFF = 11008
NCORE = 8
TOWN = S // NCORE
HALO = 256
TEXT = TOWN + 2 * HALO
NDT = D // 128
NHT = HFF // 128
NG_IN = 32
NG_OUT = 86
SCALE = 1.0 / float(np.sqrt(DH))
NEG = -1e9

ENGS = ("sync", "tensor", "vector", "scalar", "gpsimd")


class Sem:
    def __init__(self, h):
        self.h = h
        self.n = 0


class Plan:
    def __init__(self):
        self.q = {e: [] for e in ENGS}

    def emit(self, eng, fn):
        self.q[eng].append(fn)

    def wait(self, eng, sem, val):
        if val > 0:
            self.emit(eng, lambda e, s=sem.h, v=val: e.wait_ge(s, v))

    def dma(self, eng, out, in_, sem, transpose=False):
        def f(e, o=out, i=in_, s=sem.h, t=transpose):
            e.dma_start(out=o, in_=i, transpose=t).then_inc(s, 16)
        self.emit(eng, f)
        sem.n += 16
        return sem.n

    def vsy(self, eng, sem):
        self.wait(eng, sem, sem.n)

    def op(self, eng, meth, *args, inc=None, inc_drain=None, **kw):
        def f(e, m=meth, a=args, k=kw, i=inc, d=inc_drain):
            ins = getattr(e, m)(*a, **k)
            if i is not None:
                ins.then_inc(i.h, 1)
            if d is not None:
                e.drain().then_inc(d.h, 1)
        self.emit(eng, f)
        if inc is not None:
            inc.n += 1
        if inc_drain is not None:
            inc_drain.n += 1
        if inc is not None or inc_drain is not None:
            return (inc or inc_drain).n


class Ctx:
    pass


DBG_OUT = ()


def _mk(c: Ctx):
    nc = c.nc
    ap = lambda *a, **k: nc.dram_tensor(*a, **k).ap()
    c.x_ext = ap("x_ext", [TEXT, D], F32, kind="ExternalInput")
    c.wqt = ap("wqt", [H * 128, NDT * 128], F32, kind="ExternalInput")
    c.wot = ap("wot", [NDT * 128, NDT * 128], F32, kind="ExternalInput")
    c.wkt = ap("wkt", [128, NDT * DH], F32, kind="ExternalInput")
    c.wvt = ap("wvt", [128, NDT * DH], F32, kind="ExternalInput")
    c.qg = ap("qg", [HFF, D], U8, kind="ExternalInput")
    c.qu = ap("qu", [HFF, D], U8, kind="ExternalInput")
    c.qd = ap("qd", [D, HFF], U8, kind="ExternalInput")
    c.zg = ap("zg", [HFF, NG_IN], U8, kind="ExternalInput")
    c.sg = ap("sg", [HFF, NG_IN], F32, kind="ExternalInput")
    c.zu = ap("zu", [HFF, NG_IN], U8, kind="ExternalInput")
    c.su = ap("su", [HFF, NG_IN], F32, kind="ExternalInput")
    c.zd = ap("zd", [D, NG_OUT], U8, kind="ExternalInput")
    c.sd = ap("sd", [D, NG_OUT], F32, kind="ExternalInput")
    c.cosq = ap("cosq", [128, TOWN], F32, kind="ExternalInput")
    c.sinq = ap("sinq", [128, TOWN], F32, kind="ExternalInput")
    c.cosk = ap("cosk", [128, TEXT], F32, kind="ExternalInput")
    c.sink = ap("sink", [128, TEXT], F32, kind="ExternalInput")
    c.alpc = ap("alpc", [128, 8], F32, kind="ExternalInput")
    c.onec = ap("onec", [128, 8], F32, kind="ExternalInput")
    c.maskc = ap("maskc", [128, 20], F32, kind="ExternalInput")
    c.idf = ap("idf", [128, 128], F32, kind="ExternalInput")
    c.out = ap("out", [TOWN, D], F32, kind="ExternalOutput")
    kk = lambda n: "ExternalOutput" if n in DBG_OUT else "Internal"
    c.xn_stash = ap("xn_stash", [TEXT, D], BF16, kind=kk("xn_stash"))
    c.gu_stash = ap("gu_stash", [2 * HFF, D], BF16, kind=kk("gu_stash"))
    c.wd_stash = ap("wd_stash", [D, HFF], BF16, kind=kk("wd_stash"))
    c.h_stash = ap("h_stash", [HFF, TOWN], BF16, kind=kk("h_stash"))
    c.projT_stash = ap("projT_st", [D, TOWN], F32, kind=kk("projT_st"))
    c.sum_stash = ap("sum_st", [D, TOWN], F32, kind=kk("sum_st"))

    sb = c.sb
    c.xnT_own = sb("xnT_own", [128, NDT * TOWN], BF16)
    c.alp_sb = sb("alp_sb", [128, 8], F32)
    c.one_sb = sb("one_sb", [128, 8], F32)
    c.msk_sb = sb("msk_sb", [128, 20], F32)
    c.idf_sb = sb("idf_sb", [128, 128], F32)
    c.bstack = contextlib.ExitStack()
    bsb = lambda n, s, d: c.bstack.enter_context(
        nc.sbuf_tensor(n, s, d, side="right"))
    c.v_aug = bsb("v_aug", [128, 12 * 132], BF16)
    c.krT = bsb("krT", [128, TEXT], BF16)
    c.cq_sb = bsb("cq_sb", [128, TOWN], F32)
    c.sq_sb = bsb("sq_sb", [128, TOWN], F32)
    c.g_halo = nc.sbuf_tensor("xnT_halo", [128, NDT * 512], BF16, side="right")
    c.xnT_halo = c.g_halo.__enter__()
    c.pb = [c.stack.enter_context(nc.psum_tensor(f"pb{i}", [128, 512], F32))
            for i in range(8)]

    for name in ("s_ld", "s_xt", "s_mm", "s_ve", "s_ac", "s_vsy"):
        setattr(c, name, Sem(c.stack.enter_context(nc.semaphore(name))))

    def sems(name, k):
        return [Sem(c.stack.enter_context(nc.semaphore(f"{name}{i}")))
                for i in range(k)]
    c.ldx = sems("ldx", 2)
    c.xw = sems("xw", 2)
    c.gq = sems("gq", 3)
    c.lq = sems("lq", 4)
    c.swl = sems("swl", 4)
    c.ldq = sems("ldq", 2)
    c.swd = sems("swd", 2)
    c.pan = sems("pan", 2)
    c.pans = sems("pans", 2)
    c.hw = sems("hw", 2)
    c.wd = sems("wd", 4)
    c.hh = sems("hh", 4)
    c.spr = sems("spr", 1)[0]
    c.sm = sems("sm", 2)
    c.pj = sems("pj", 2)
    c.dx = sems("dx", 2)
    c.stl = sems("stl", 2)
    c.ou = sems("ou", 2)
    c.bars = [Sem(c.stack.enter_context(nc.semaphore(f"bar{i}")))
              for i in range(10)]
    c.bar_i = 0


def barrier(c):
    sem = c.bars[c.bar_i]
    c.bar_i += 1
    for e in ENGS:
        def f(en, h=sem.h):
            en.drain()
            en.sem_inc(h, 1)
        c.P.emit(e, f)
    sem.n += 5
    for e in ENGS:
        c.P.wait(e, sem, sem.n)


def phase_a(c):
    P, nc = c.P, c.nc
    P.dma("sync", c.cq_sb[:], c.cosq, c.s_ld)
    P.dma("sync", c.sq_sb[:], c.sinq, c.s_ld)
    P.dma("sync", c.alp_sb[:], c.alpc, c.s_ld)
    P.dma("sync", c.one_sb[:], c.onec, c.s_ld)
    P.dma("sync", c.msk_sb[:], c.maskc, c.s_ld)
    P.dma("sync", c.idf_sb[:], c.idf, c.s_ld)

    ph = contextlib.ExitStack()
    t = lambda n, s, d: ph.enter_context(nc.sbuf_tensor(n, s, d))
    x_sb = [t("x_sb0", [128, D], F32), t("x_sb1", [128, D], F32)]
    xc_sb = t("xc_sb", [128, D], F32)
    sqr_sb = t("sqr_sb", [128, D], F32)
    xnb = [t("xnb0", [128, D], BF16), t("xnb1", [128, D], BF16)]
    lncol = t("lncol", [128, 4], F32)

    NSTRIP = TEXT // 128
    for st in range(NSTRIP):
        par = st % 2
        if st >= 2:
            P.wait("sync", c.xw[par], 16 * (st // 2))
        P.dma("sync", x_sb[par][:], c.x_ext[128 * st:128 * (st + 1), :],
              c.ldx[par])
        P.wait("vector", c.ldx[par], c.ldx[par].n)
        P.wait("vector", c.s_ac, 2 * st)
        if st >= 2:
            P.wait("vector", c.xw[par], 16 * (st // 2))
        P.op("vector", "reduce_sum", lncol[:, 0:1], x_sb[par][:], AXX,
             inc=c.s_vsy)
        P.vsy("vector", c.s_vsy)
        P.op("vector", "tensor_scalar", lncol[:, 1:2], lncol[:, 0:1],
             -1.0 / D, None, ALU.mult, inc=c.s_vsy)
        P.vsy("vector", c.s_vsy)
        P.op("vector", "tensor_scalar", xc_sb[:], x_sb[par][:],
             lncol[:, 1:2], None, ALU.add, inc=c.s_ve)
        P.wait("scalar", c.s_ve, c.s_ve.n)
        P.op("scalar", "activation", sqr_sb[:], xc_sb[:], AF.Square,
             inc=c.s_ac)
        P.wait("vector", c.s_ac, c.s_ac.n)
        P.op("vector", "reduce_sum", lncol[:, 2:3], sqr_sb[:], AXX,
             inc=c.s_vsy)
        P.vsy("vector", c.s_vsy)
        P.op("vector", "tensor_scalar", lncol[:, 2:3], lncol[:, 2:3],
             1.0 / D, 1e-5, ALU.mult, ALU.add, inc=c.s_ve)
        P.wait("scalar", c.s_ve, c.s_ve.n)
        P.op("scalar", "activation", lncol[:, 3:4], lncol[:, 2:3], AF.Sqrt,
             inc=c.s_ac)
        P.wait("vector", c.s_ac, c.s_ac.n)
        P.op("vector", "reciprocal", lncol[:, 0:1], lncol[:, 3:4],
             inc=c.s_vsy)
        P.vsy("vector", c.s_vsy)
        P.op("vector", "tensor_scalar", xnb[par][:], xc_sb[:],
             lncol[:, 0:1], None, ALU.mult, inc_drain=c.s_ve)
        P.wait("gpsimd", c.s_ve, c.s_ve.n)
        P.dma("gpsimd", c.xn_stash[128 * st:128 * (st + 1), :], xnb[par][:],
              c.xw[par])

    P.wait("sync", c.xw[0], 16 * (NSTRIP // 2))
    P.wait("sync", c.xw[1], 16 * (NSTRIP // 2))
    for dt in range(NDT):
        c0, c1 = 128 * dt, 128 * (dt + 1)
        P.dma("sync", c.xnT_own[:, TOWN * dt:TOWN * (dt + 1)],
              c.xn_stash[HALO:HALO + TOWN, c0:c1], c.s_xt, transpose=True)
        P.dma("sync", c.xnT_halo[:, 512 * dt:512 * dt + 256],
              c.xn_stash[0:HALO, c0:c1], c.s_xt, transpose=True)
        P.dma("sync", c.xnT_halo[:, 512 * dt + 256:512 * dt + 512],
              c.xn_stash[TEXT - HALO:TEXT, c0:c1], c.s_xt, transpose=True)
    ph.close()


def phase_b1(c):
    P, nc, pb = c.P, c.nc, c.pb
    ph = contextlib.ExitStack()
    t = lambda n, s, d: ph.enter_context(nc.sbuf_tensor(n, s, d))
    ck_sb = t("ck_sb", [128, TEXT], F32)
    sk_sb = t("sk_sb", [128, TEXT], F32)
    wk_bf = t("wk_bf", [128, NDT * DH], BF16)
    wv_bf = t("wv_bf", [128, NDT * DH], BF16)
    kf_sb = t("kf_sb", [128, 512], F32)
    ksw_sb = t("ksw_sb", [128, 512], F32)
    kt1_sb = t("kt1_sb", [128, 512], F32)

    P.dma("sync", ck_sb[:], c.cosk, c.s_ld)
    P.dma("sync", sk_sb[:], c.sink, c.s_ld)
    P.dma("gpsimd", wk_bf[:], c.wkt, c.s_ld)
    P.dma("gpsimd", wv_bf[:], c.wvt, c.s_ld)
    P.wait("tensor", c.s_ld, c.s_ld.n)
    P.wait("vector", c.s_ld, c.s_ld.n)
    P.wait("scalar", c.s_ld, c.s_ld.n)
    P.wait("tensor", c.s_xt, c.s_xt.n)

    def ext_rhs_pieces(lo, hi):
        segs = []
        if lo < 256:
            e = min(hi, 256)
            segs.append(("halo", lo, e - lo))
            lo = e
        if lo < hi and lo < 1280:
            e = min(hi, 1280)
            segs.append(("own", lo - 256, e - lo))
            lo = e
        if lo < hi:
            segs.append(("halo", 256 + (lo - 1280), hi - lo))
        return segs

    for ch in range(3):
        segs = ext_rhs_pieces(512 * ch, 512 * (ch + 1))
        off = 0
        for kind, start, width in segs:
            for dt in range(NDT):
                if kind == "own":
                    rhs = c.xnT_own[:, TOWN * dt + start:TOWN * dt + start + width]
                else:
                    rhs = c.xnT_halo[:, 512 * dt + start:512 * dt + start + width]
                P.op("tensor", "matmul", pb[ch][:, off:off + width],
                     wk_bf[:, 128 * dt:128 * (dt + 1)], rhs,
                     start=(dt == 0), stop=(dt == NDT - 1),
                     inc=(c.s_mm if dt == NDT - 1 else None))
            off += width
        P.wait("vector", c.s_mm, c.s_mm.n)
        cc = ck_sb[:, 512 * ch:512 * (ch + 1)]
        scs = sk_sb[:, 512 * ch:512 * (ch + 1)]
        P.op("vector", "tensor_copy", kf_sb[:], pb[ch][:])
        P.op("vector", "tensor_copy", ksw_sb[0:64, :], kf_sb[64:128, :])
        P.op("vector", "tensor_copy", ksw_sb[64:128, :], kf_sb[0:64, :])
        P.op("vector", "tensor_mul", kt1_sb[:], kf_sb[:], cc)
        P.op("vector", "tensor_mul", ksw_sb[:], ksw_sb[:], scs)
        P.op("vector", "tensor_add", c.krT[:, 512 * ch:512 * (ch + 1)],
             kt1_sb[:], ksw_sb[:], inc=c.s_ve)
    k_done = c.s_ve.n

    for tt in range(12):
        lo = 128 * tt
        if lo < 256:
            def lhs_t(dt, lo=lo):
                return c.xnT_halo[:, 512 * dt + lo:512 * dt + lo + 128]
        elif lo < 1280:
            def lhs_t(dt, lo=lo):
                o = lo - 256
                return c.xnT_own[:, TOWN * dt + o:TOWN * dt + o + 128]
        else:
            def lhs_t(dt, lo=lo):
                o = 256 + lo - 1280
                return c.xnT_halo[:, 512 * dt + o:512 * dt + o + 128]
        bank = pb[3 + (tt % 2)]
        if tt >= 2:
            P.wait("tensor", c.s_ve, k_done + tt - 1)
        for dt in range(NDT):
            P.op("tensor", "matmul", bank[:, 0:128],
                 lhs_t(dt), wv_bf[:, 128 * dt:128 * (dt + 1)],
                 start=(dt == 0), stop=(dt == NDT - 1),
                 inc=(c.s_mm if dt == NDT - 1 else None))
        P.wait("vector", c.s_mm, c.s_mm.n)
        P.op("vector", "tensor_copy",
             c.v_aug[:, 132 * tt:132 * tt + 128], bank[:, 0:128])
        P.op("vector", "memset",
             c.v_aug[:, 132 * tt + 128:132 * tt + 129], 1.0, inc=c.s_ve)
    ph.close()


WIN_Q = [(0, 256), (0, 512), (256, 768), (512, 1024), (768, 1024)]
WIN_SLOT0 = [2, 0, 0, 0, 0]


def phase_b2(c, attn_outT):
    P, nc, pb = c.P, c.nc, c.pb
    ph = contextlib.ExitStack()
    t = lambda n, s, d: ph.enter_context(nc.sbuf_tensor(n, s, d))
    wq_bf = [t(f"wq_bf{i}", [128, NDT * 128], BF16) for i in range(3)]
    qf_sb = t("qf_sb", [128, TOWN], F32)
    qsw_sb = t("qsw_sb", [128, TOWN], F32)
    qt1_sb = t("qt1_sb", [128, TOWN], F32)
    qrT = [t("qrT0", [128, TOWN], BF16), t("qrT1", [128, TOWN], BF16)]
    attnT = t("attnT", [128, 2048], BF16)
    wbuf = [t("wb0", [128, 4 * 132], F32), t("wb1", [128, 4 * 132], F32)]
    bcol = t("bcol", [128, 4], F32)
    blf = t("blf", [128, 128], F32)

    sig_done = {k: 0 for k in range(4)}
    av_bank_evac = {k: 0 for k in range(4)}
    tr_bank_evac = {0: 0, 1: 0}
    blend_prog = 0
    av_mm_done = 0
    gq_val = {}
    rope_done = {}
    qmm_done = {}

    def q_chunks(h):
        par = h % 3
        chunks = []
        for th in range(2):
            for cb in range(8):
                def f(th=th, cb=cb, par=par, h=h):
                    if th == 0 and cb == 0:
                        P.wait("tensor", c.gq[par], gq_val[h])
                        if h >= 1:
                            P.wait("tensor", c.s_ve, rope_done[h - 1])
                    for dt in range(4 * cb, 4 * cb + 4):
                        P.op("tensor", "matmul", pb[4 + th][:, 0:512],
                             wq_bf[par][:, 128 * dt:128 * (dt + 1)],
                             c.xnT_own[:, TOWN * dt + 512 * th:
                                       TOWN * dt + 512 * th + 512],
                             start=(dt == 0), stop=(dt == NDT - 1),
                             inc=(c.s_mm if dt == NDT - 1 and th == 1
                                  else None))
                    if th == 1 and cb == 7:
                        qmm_done[h] = c.s_mm.n
                chunks.append(f)
        return chunks

    def emit_rope(h):
        par = h % 2
        P.wait("vector", c.s_mm, qmm_done[h])
        P.op("vector", "tensor_copy", qf_sb[:, 0:512], pb[4][:])
        P.op("vector", "tensor_copy", qf_sb[:, 512:1024], pb[5][:])
        P.op("vector", "tensor_copy", qsw_sb[0:64, :], qf_sb[64:128, :])
        P.op("vector", "tensor_copy", qsw_sb[64:128, :], qf_sb[0:64, :])
        P.op("vector", "tensor_mul", qt1_sb[:], qf_sb[:], c.cq_sb[:])
        P.op("vector", "tensor_mul", qsw_sb[:], qsw_sb[:], c.sq_sb[:])
        P.op("vector", "tensor_add", qrT[par][:], qt1_sb[:], qsw_sb[:],
             inc=c.s_ve)
        rope_done[h] = c.s_ve.n

    for hh in range(3):
        P.dma("gpsimd", wq_bf[hh][:], c.wqt[128 * hh:128 * (hh + 1), :],
              c.gq[hh])
        gq_val[hh] = c.gq[hh].n
    for f in q_chunks(0):
        f()
    emit_rope(0)

    for h in range(H):
        par = h % 3
        pump = q_chunks(h + 1) if h + 1 < H else []
        pump_i = [0]

        def do_pump(n=1):
            for _ in range(n):
                if pump_i[0] < len(pump):
                    pump[pump_i[0]]()
                    pump_i[0] += 1

        if h + 3 < H:
            P.wait("gpsimd", c.s_mm, qmm_done[h])
            P.dma("gpsimd", wq_bf[par][:],
                  c.wqt[128 * (h + 3):128 * (h + 4), :], c.gq[par])
            gq_val[h + 3] = c.gq[par].n

        for w in range(5):
            qlo, qhi = WIN_Q[w]
            qn = qhi - qlo
            wslot = wbuf[w % 2]
            for kt in range(4):
                if kt == 0 and w == 0:
                    P.wait("tensor", c.s_ve, rope_done[h])
                P.wait("tensor", c.s_ac, av_bank_evac[kt])
                P.wait("tensor", c.s_ac, sig_done[kt])
                P.op("tensor", "matmul", pb[kt][:, 0:qn],
                     c.krT[:, 256 * w + 128 * kt:256 * w + 128 * (kt + 1)],
                     qrT[h % 2][:, qlo:qhi], start=True, stop=True,
                     inc=c.s_mm)
            sc_done = c.s_mm.n
            do_pump(2)
            for kt in range(4):
                P.wait("scalar", c.s_mm, sc_done - 3 + kt)
                P.wait("scalar", c.s_mm, av_mm_done)
                P.wait("scalar", c.s_ve, blend_prog)
                P.op("scalar", "activation", attnT[:, 512 * kt:512 * kt + qn],
                     pb[kt][:, 0:qn], AF.Sigmoid,
                     bias=c.msk_sb[:, 4 * w + kt:4 * w + kt + 1],
                     scale=SCALE, inc=c.s_ac)
                sig_done[kt] = c.s_ac.n
            nq = qn // 128
            for qs in range(nq):
                bank = pb[qs]
                P.wait("tensor", c.s_ac, sig_done[3])
                for kt in range(4):
                    P.op("tensor", "matmul", bank[:, 0:129],
                         attnT[:, 512 * kt + 128 * qs:512 * kt + 128 * (qs + 1)],
                         c.v_aug[:, 132 * (2 * w + kt):132 * (2 * w + kt) + 129],
                         start=(kt == 0), stop=(kt == 3),
                         inc=(c.s_mm if kt == 3 else None))
                av_mm_done = c.s_mm.n
                do_pump(1)
                P.wait("scalar", c.s_mm, av_mm_done)
                slot = WIN_SLOT0[w] + qs
                P.op("scalar", "copy", wslot[:, 132 * slot:132 * slot + 129],
                     bank[:, 0:129], inc=c.s_ac)
                av_bank_evac[qs] = c.s_ac.n

            if w == 2 and h + 1 < H:
                emit_rope(h + 1)

            if w >= 1:
                k = w - 1
                A = wbuf[(w - 1) % 2]
                B = wbuf[w % 2]
                for qs in range(2):
                    aslot = 2 + qs
                    bslot = qs
                    col = 2 * k + qs
                    P.wait("vector", c.s_ac, av_bank_evac[bslot])
                    P.op("vector", "reciprocal", bcol[:, 0:1],
                         A[:, 132 * aslot + 128:132 * aslot + 129])
                    P.op("vector", "reciprocal", bcol[:, 1:2],
                         B[:, 132 * bslot + 128:132 * bslot + 129],
                         inc=c.s_vsy)
                    P.vsy("vector", c.s_vsy)
                    P.op("vector", "tensor_mul", bcol[:, 2:3],
                         bcol[:, 0:1], c.one_sb[:, col:col + 1])
                    P.op("vector", "tensor_mul", bcol[:, 3:4],
                         bcol[:, 1:2], c.alp_sb[:, col:col + 1], inc=c.s_vsy)
                    P.vsy("vector", c.s_vsy)
                    P.op("vector", "tensor_scalar", blf[:],
                         A[:, 132 * aslot:132 * aslot + 128],
                         bcol[:, 2:3], None, ALU.mult)
                    P.op("vector", "scalar_tensor_tensor", blf[:],
                         B[:, 132 * bslot:132 * bslot + 128],
                         bcol[:, 3:4], blf[:], ALU.mult, ALU.add, inc=c.s_ve)
                    tbank = 6 + (qs % 2)
                    P.wait("tensor", c.s_ve, c.s_ve.n)
                    P.wait("tensor", c.s_ve, tr_bank_evac[qs % 2])
                    P.op("tensor", "transpose", pb[tbank][:, 0:128], blf[:],
                         c.idf_sb[:], inc=c.s_mm)
                    P.wait("vector", c.s_mm, c.s_mm.n)
                    P.op("vector", "tensor_copy",
                         attn_outT[:, TOWN * h + 128 * col:
                                   TOWN * h + 128 * (col + 1)],
                         pb[tbank][:, 0:128], inc=c.s_ve)
                    tr_bank_evac[qs % 2] = c.s_ve.n
                blend_prog = c.s_ve.n
        do_pump(16)
    ph.close()


def phase_b3(c, attn_outT, dq):
    P, nc, pb = c.P, c.nc, c.pb
    ph = contextlib.ExitStack()
    t = lambda n, s, d: ph.enter_context(nc.sbuf_tensor(n, s, d))
    wo_bf = [t(f"wo_bf{i}", [128, NDT * 128], BF16) for i in range(3)]
    pr_sb = [t("pr_sb0", [128, TOWN], F32), t("pr_sb1", [128, TOWN], F32)]
    wo_mm = {}
    gq_val = {}
    for hh in range(3):
        P.dma("gpsimd", wo_bf[hh][:], c.wot[128 * hh:128 * (hh + 1), :],
              c.gq[hh])
        gq_val[hh] = c.gq[hh].n
    for i in range(NDT):
        par = i % 2
        wpar = i % 3
        P.wait("tensor", c.gq[wpar], gq_val[i])
        if i >= 2:
            P.wait("tensor", c.pj[par], 16 * (i // 2))
        for th in range(2):
            bank = pb[2 * (i % 2) + th]
            for ot in range(NDT):
                P.op("tensor", "matmul", bank[:, 0:512],
                     wo_bf[wpar][:, 128 * ot:128 * (ot + 1)],
                     attn_outT[:, TOWN * ot + 512 * th:
                               TOWN * ot + 512 * (th + 1)],
                     start=(ot == 0), stop=(ot == NDT - 1),
                     inc=(c.s_mm if ot == NDT - 1 else None))
        wo_mm[i] = c.s_mm.n
        if i + 3 < NDT:
            P.wait("gpsimd", c.s_mm, wo_mm[i])
            P.dma("gpsimd", wo_bf[wpar][:],
                  c.wot[128 * (i + 3):128 * (i + 4), :], c.gq[wpar])
            gq_val[i + 3] = c.gq[wpar].n
        P.wait("vector", c.s_mm, c.s_mm.n)
        if i >= 2:
            P.wait("vector", c.pj[par], 16 * (i // 2))
        P.op("vector", "tensor_copy", pr_sb[par][:, 0:512], pb[2 * (i % 2)][:])
        P.op("vector", "tensor_copy", pr_sb[par][:, 512:1024],
             pb[2 * (i % 2) + 1][:], inc_drain=c.s_ve)
        P.wait("gpsimd", c.s_ve, c.s_ve.n)
        P.dma("gpsimd", c.projT_stash[128 * i:128 * (i + 1), :], pr_sb[par][:],
              c.pj[par])
        if i < 16:
            dq.load(i)
        if i % 2 == 1 and i // 2 < 12:
            dq.compute(i // 2)
    ph.close()


class DqGU:

    def __init__(self, c, scope):
        nc = c.nc
        t = lambda n, s, d: scope.enter_context(
            nc.sbuf_tensor(n, s, d, side="right"))
        self.qin = [t(f"qin{i}", [128, D], U8) for i in range(4)]
        self.qo = [t(f"qo{i}", [128, D], BF16) for i in range(2)]
        self.zin = t("zin", [128, 4 * 32], U8)
        self.sin_ = t("sin_", [128, 4 * 32], F32)
        self.zf = t("zf", [128, 4 * 32], F32)
        self.c = c
        self.ld = {}

    def load(self, q):
        c, P = self.c, self.c.P
        jj, kind = q // 2, q % 2
        lane = q % 4
        qw = c.qg if kind == 0 else c.qu
        zt = c.zg if kind == 0 else c.zu
        st_ = c.sg if kind == 0 else c.su
        r0, r1 = 128 * jj, 128 * (jj + 1)
        if q >= 4:
            P.wait("scalar", c.swl[q % 2], 16 * (q // 2 - 1))
        P.dma("scalar", self.qin[lane][:], qw[r0:r1, :], c.lq[lane])
        P.dma("scalar", self.zin[:, 32 * lane:32 * (lane + 1)], zt[r0:r1, :],
              c.lq[lane])
        P.dma("scalar", self.sin_[:, 32 * lane:32 * (lane + 1)], st_[r0:r1, :],
              c.lq[lane])
        self.ld[q] = c.lq[lane].n

    def compute(self, q):
        c, P = self.c, self.c.P
        jj, kind = q // 2, q % 2
        lane = q % 4
        olane = q % 2
        s0 = 256 * jj + 128 * kind
        P.wait("vector", c.lq[lane], self.ld[q])
        if q >= 2:
            P.wait("vector", c.swl[olane], 16 * (q // 2))
        P.op("vector", "tensor_copy", self.zf[:, 32 * lane:32 * (lane + 1)],
             self.zin[:, 32 * lane:32 * (lane + 1)], inc=c.s_vsy)
        P.vsy("vector", c.s_vsy)
        for g in range(NG_IN):
            csl = slice(128 * g, 128 * (g + 1))
            zc = slice(32 * lane + g, 32 * lane + g + 1)
            P.op("vector", "tensor_scalar", self.qo[olane][:, csl],
                 self.qin[lane][:, csl], self.zf[:, zc], self.sin_[:, zc],
                 ALU.subtract, ALU.mult,
                 inc_drain=(c.s_ve if g == NG_IN - 1 else None))
        P.wait("gpsimd", c.s_ve, c.s_ve.n)
        P.dma("gpsimd", c.gu_stash[s0:s0 + 128, :], self.qo[olane][:],
              c.swl[olane])


class DqDown:

    def __init__(self, c, scope):
        nc = c.nc
        HH = HFF // 2
        t = lambda n, s, d: scope.enter_context(nc.sbuf_tensor(n, s, d))
        self.qind = [t(f"qind{i}", [128, HH], U8) for i in range(2)]
        self.qodd = [t(f"qodd{i}", [128, HH], BF16) for i in range(2)]
        self.zind = t("zind", [128, 2 * NG_OUT], U8)
        self.sind = t("sind", [128, 2 * NG_OUT], F32)
        self.zfd = t("zfd", [128, 2 * NG_OUT], F32)
        self.c = c
        self.ld = {}

    def load(self, dj):
        c, P = self.c, self.c.P
        HH = HFF // 2
        dt, half = dj // 2, dj % 2
        lane = dj % 2
        r0, r1 = 128 * dt, 128 * (dt + 1)
        h0 = HH * half
        if dj >= 2:
            P.wait("scalar", c.swd[lane], 16 * (dj // 2))
        P.dma("scalar", self.zind[:, NG_OUT * lane:NG_OUT * (lane + 1)],
              c.zd[r0:r1, :], c.ldq[lane])
        P.dma("scalar", self.sind[:, NG_OUT * lane:NG_OUT * (lane + 1)],
              c.sd[r0:r1, :], c.ldq[lane])
        P.dma("scalar", self.qind[lane][:], c.qd[r0:r1, h0:h0 + HH],
              c.ldq[lane])
        self.ld[dj] = c.ldq[lane].n

    def compute(self, dj):
        c, P = self.c, self.c.P
        HH = HFF // 2
        NGH = NG_OUT // 2
        dt, half = dj // 2, dj % 2
        lane = dj % 2
        r0, r1 = 128 * dt, 128 * (dt + 1)
        h0 = HH * half
        P.wait("vector", c.ldq[lane], self.ld[dj])
        P.op("vector", "tensor_copy",
             self.zfd[:, NG_OUT * lane:NG_OUT * (lane + 1)],
             self.zind[:, NG_OUT * lane:NG_OUT * (lane + 1)], inc=c.s_vsy)
        P.vsy("vector", c.s_vsy)
        for gg in range(NGH):
            g = NGH * half + gg
            csl = slice(128 * gg, 128 * (gg + 1))
            zc = slice(NG_OUT * lane + g, NG_OUT * lane + g + 1)
            P.op("vector", "tensor_scalar", self.qodd[lane][:, csl],
                 self.qind[lane][:, csl], self.zfd[:, zc], self.sind[:, zc],
                 ALU.subtract, ALU.mult,
                 inc_drain=(c.s_ve if gg == NGH - 1 else None))
        P.wait("gpsimd", c.s_ve, c.s_ve.n)
        P.dma("gpsimd", c.wd_stash[r0:r1, h0:h0 + HH], self.qodd[lane][:],
              c.swd[lane])


def phase_c1(c, dq, dqd):
    P, nc, pb = c.P, c.nc, c.pb
    ph = contextlib.ExitStack()
    t = lambda n, s, d: ph.enter_context(nc.sbuf_tensor(n, s, d))
    pan = [t("pan0", [128, NDT * 512], BF16),
           t("pan1", [128, NDT * 512], BF16)]
    sg_sb = [t("sg_sb0", [128, TOWN], BF16), t("sg_sb1", [128, TOWN], BF16)]
    h_sb = [t("h_sb0", [128, TOWN], BF16), t("h_sb1", [128, TOWN], BF16)]
    NPAN = NHT // 2
    pan_ld = {}
    gu_mm = {}
    h_ve = {}
    NGU = 2 * NHT
    NDJ = 2 * NDT
    for dj in range(2):
        dqd.load(dj)
    for J in range(NPAN):
        ppar = J % 2
        for q in range(4 * J + 12, min(4 * J + 16, NGU)):
            dq.compute(q)
        for dj in range(2 * J, min(2 * J + 2, NDJ)):
            dqd.compute(dj)
        if J >= 2:
            P.wait("sync", c.s_mm, gu_mm[2 * (J - 2) + 1])
        P.wait("sync", c.swl[0], 16 * min(2 * J + 2, NHT))
        P.wait("sync", c.swl[1], 16 * min(2 * J + 2, NHT))
        for dt in range(NDT):
            P.dma("sync", pan[ppar][:, 512 * dt:512 * (dt + 1)],
                  c.gu_stash[512 * J:512 * (J + 1), 128 * dt:128 * (dt + 1)],
                  c.pan[ppar], transpose=True)
        pan_ld[J] = c.pan[ppar].n
        for r in range(2):
            jj = 2 * J + r
            jpar = jj % 2
            P.wait("tensor", c.pan[ppar], pan_ld[J])
            if jj >= 2:
                P.wait("tensor", c.s_ve, h_ve[jj - 2])
            for dt in range(NDT):
                glhs = pan[ppar][:, 512 * dt + 256 * r:
                                512 * dt + 256 * r + 128]
                ulhs = pan[ppar][:, 512 * dt + 256 * r + 128:
                                512 * dt + 256 * r + 256]
                for th in range(2):
                    P.op("tensor", "matmul", pb[4 * jpar + th][:, 0:512],
                         glhs,
                         c.xnT_own[:, TOWN * dt + 512 * th:
                                   TOWN * dt + 512 * (th + 1)],
                         start=(dt == 0), stop=(dt == NDT - 1),
                         inc=(c.s_mm if dt == NDT - 1 and th == 1 else None))
                for th in range(2):
                    P.op("tensor", "matmul",
                         pb[4 * jpar + 2 + th][:, 0:512],
                         ulhs,
                         c.xnT_own[:, TOWN * dt + 512 * th:
                                   TOWN * dt + 512 * (th + 1)],
                         start=(dt == 0), stop=(dt == NDT - 1),
                         inc=(c.s_mm if dt == NDT - 1 and th == 1 else None))
            gu_mm[jj] = c.s_mm.n
            P.wait("scalar", c.s_mm, gu_mm[jj] - 1)
            if jj >= 2:
                P.wait("scalar", c.s_ve, h_ve[jj - 2])
            P.op("scalar", "activation", sg_sb[jpar][:, 0:512],
                 pb[4 * jpar + 0][:], AF.Silu)
            P.op("scalar", "activation", sg_sb[jpar][:, 512:1024],
                 pb[4 * jpar + 1][:], AF.Silu, inc=c.s_ac)
            P.wait("vector", c.s_ac, c.s_ac.n)
            P.wait("vector", c.s_mm, gu_mm[jj])
            if jj >= 2:
                P.wait("vector", c.hw[jpar], 16 * (jj // 2))
            P.op("vector", "tensor_mul", h_sb[jpar][:, 0:512],
                 sg_sb[jpar][:, 0:512], pb[4 * jpar + 2][:])
            P.op("vector", "tensor_mul", h_sb[jpar][:, 512:1024],
                 sg_sb[jpar][:, 512:1024], pb[4 * jpar + 3][:],
                 inc_drain=c.s_ve)
            h_ve[jj] = c.s_ve.n
            P.wait("gpsimd", c.s_ve, c.s_ve.n)
            P.dma("gpsimd", c.h_stash[128 * jj:128 * (jj + 1), :],
                  h_sb[jpar][:], c.hw[jpar])
        for q in range(4 * J + 16, min(4 * J + 20, NGU)):
            dq.load(q)
        for dj in range(2 * J + 2, min(2 * J + 4, NDJ)):
            dqd.load(dj)
    ph.close()


def phase_c2(c):
    P, nc, pb = c.P, c.nc, c.pb
    ph = contextlib.ExitStack()
    t = lambda n, s, d: ph.enter_context(nc.sbuf_tensor(n, s, d))
    wdt = [t(f"wdt{i}", [128, 512], BF16) for i in range(4)]
    hst = [t(f"hst{i}", [128, TOWN], BF16) for i in range(4)]
    prt = t("prt", [128, 4 * TOWN], F32)
    smt = [t("smt0", [128, TOWN], F32), t("smt1", [128, TOWN], F32)]
    for s in (c.hw[0], c.hw[1], c.pj[0], c.pj[1], c.swd[0], c.swd[1]):
        P.wait("sync", s, s.n)
        P.wait("gpsimd", s, s.n)
    dn_mm = {}
    sum_ve = 0
    for ch in range(8):
        P.wait("sync", c.s_ve, sum_ve)
        for ic in range(4):
            i = 4 * ch + ic
            P.dma("sync", prt[:, TOWN * ic:TOWN * (ic + 1)],
                  c.projT_stash[128 * i:128 * (i + 1), :], c.spr)
        prj_ld = c.spr.n
        for jj in range(NHT):
            lane = jj % 4
            prev = (ch, jj - 4) if jj >= 4 else (ch - 1, 82 + (jj + 2) % 4)
            if prev in dn_mm:
                P.wait("sync", c.s_mm, dn_mm[prev])
                P.wait("gpsimd", c.s_mm, dn_mm[prev])
            P.dma("sync", wdt[lane][:],
                  c.wd_stash[512 * ch:512 * (ch + 1),
                             128 * jj:128 * (jj + 1)], c.wd[lane],
                  transpose=True)
            P.dma("gpsimd", hst[lane][:],
                  c.h_stash[128 * jj:128 * (jj + 1), :], c.hh[lane])
            P.wait("tensor", c.wd[lane], c.wd[lane].n)
            P.wait("tensor", c.hh[lane], c.hh[lane].n)
            if jj == 0 and ch > 0:
                P.wait("tensor", c.s_ve, sum_ve)
            for ic in range(4):
                for th in range(2):
                    P.op("tensor", "matmul", pb[2 * ic + th][:, 0:512],
                         wdt[lane][:, 128 * ic:128 * (ic + 1)],
                         hst[lane][:, 512 * th:512 * (th + 1)],
                         start=(jj == 0), stop=(jj == NHT - 1),
                         inc=(c.s_mm if (th == 1 and ic == 3) else None))
            dn_mm[(ch, jj)] = c.s_mm.n
        P.wait("vector", c.s_mm, dn_mm[(ch, NHT - 1)])
        P.wait("vector", c.spr, prj_ld)
        for ic in range(4):
            i = 4 * ch + ic
            k = 4 * ch + ic
            sl = ic % 2
            if k >= 2:
                P.wait("vector", c.sm[sl], 16 * (k // 2))
            for th in range(2):
                P.op("vector", "tensor_add",
                     smt[sl][:, 512 * th:512 * (th + 1)],
                     pb[2 * ic + th][:],
                     prt[:, TOWN * ic + 512 * th:TOWN * ic + 512 * (th + 1)],
                     inc_drain=(c.s_ve if th == 1 else None))
            P.wait("gpsimd", c.s_ve, c.s_ve.n)
            P.dma("gpsimd", c.sum_stash[128 * i:128 * (i + 1), :],
                  smt[sl][:], c.sm[sl])
        sum_ve = c.s_ve.n
    ph.close()


def phase_d(c):
    P, nc, pb = c.P, c.nc, c.pb
    ph = contextlib.ExitStack()
    t = lambda n, s, d: ph.enter_context(nc.sbuf_tensor(n, s, d))
    xs = [t("xs0", [128, D], F32), t("xs1", [128, D], F32)]
    stile = [t("stile0", [128, 128], F32), t("stile1", [128, 128], F32)]
    ostrip = [t("ostrip0", [128, D], F32), t("ostrip1", [128, D], F32)]
    for s in (c.sm[0], c.sm[1]):
        P.wait("sync", s, s.n)
    tr_mm = {0: 0, 1: 0}
    ev_ve = {0: 0, 1: 0}
    dx_val = {}
    for tt in range(8):
        par = tt % 2
        if tt >= 2:
            P.wait("sync", c.ou[par], 16 * (tt // 2))
        P.dma("sync", xs[par][:],
              c.x_ext[HALO + 128 * tt:HALO + 128 * (tt + 1), :], c.dx[par])
        dx_val[tt] = c.dx[par].n
        for i in range(NDT):
            sl = i % 2
            P.wait("sync", c.s_mm, tr_mm[sl])
            P.dma("sync", stile[sl][:],
                  c.sum_stash[128 * i:128 * (i + 1), 128 * tt:128 * (tt + 1)],
                  c.stl[sl])
            P.wait("tensor", c.stl[sl], c.stl[sl].n)
            P.wait("tensor", c.s_ve, ev_ve[sl])
            P.op("tensor", "transpose", pb[sl][:, 0:128], stile[sl][:],
                 c.idf_sb[:], inc=c.s_mm)
            tr_mm[sl] = c.s_mm.n
            P.wait("vector", c.s_mm, c.s_mm.n)
            if i == 0:
                P.wait("vector", c.dx[par], dx_val[tt])
                if tt >= 2:
                    P.wait("vector", c.ou[par], 16 * (tt // 2))
            P.op("vector", "tensor_add",
                 ostrip[par][:, 128 * i:128 * (i + 1)],
                 pb[sl][:, 0:128], xs[par][:, 128 * i:128 * (i + 1)],
                 **({"inc_drain": c.s_ve} if i == NDT - 1 else {"inc": c.s_ve}))
            ev_ve[sl] = c.s_ve.n
        P.wait("gpsimd", c.s_ve, c.s_ve.n)
        P.dma("gpsimd", c.out[128 * tt:128 * (tt + 1), :], ostrip[par][:],
              c.ou[par])
    P.wait("gpsimd", c.ou[0], c.ou[0].n)
    P.wait("gpsimd", c.ou[1], c.ou[1].n)
    ph.close()


def build_program():
    c = Ctx()
    c.nc = bass.Bass("TRN2", target_bir_lowering=False, debug=False)
    c.P = Plan()
    c.stack = contextlib.ExitStack()
    c.sb = lambda n, s, d: c.stack.enter_context(c.nc.sbuf_tensor(n, s, d))
    _mk(c)

    phase_a(c)
    barrier(c)
    phase_b1(c)
    c.g_halo.__exit__(None, None, None)
    barrier(c)
    g_attn = c.nc.sbuf_tensor("attn_outT", [128, H * TOWN], BF16)
    attn_outT = g_attn.__enter__()
    phase_b2(c, attn_outT)
    c.bstack.close()
    barrier(c)
    dq_scope = contextlib.ExitStack()
    dq = DqGU(c, dq_scope)
    phase_b3(c, attn_outT, dq)
    g_attn.__exit__(None, None, None)
    barrier(c)
    dqd_scope = contextlib.ExitStack()
    dqd = DqDown(c, dqd_scope)
    phase_c1(c, dq, dqd)
    dq_scope.close()
    dqd_scope.close()
    barrier(c)
    phase_c2(c)
    barrier(c)
    phase_d(c)

    with c.nc.Block() as block:
        @block.sync
        def _(e):
            for fn in c.P.q["sync"]:
                fn(e)

        @block.tensor
        def _(e):
            for fn in c.P.q["tensor"]:
                fn(e)

        @block.vector
        def _(e):
            for fn in c.P.q["vector"]:
                fn(e)

        @block.scalar
        def _(e):
            for fn in c.P.q["scalar"]:
                fn(e)

        @block.gpsimd
        def _(e):
            for fn in c.P.q["gpsimd"]:
                fn(e)

    c.stack.close()
    return c.nc


def _rope_tables():
    inv_freq = 1.0 / 10000.0 ** (np.arange(0, DH, 2, dtype=np.float32) / DH)
    t = np.arange(S, dtype=np.float32)
    f = np.outer(t, inv_freq)
    cos = np.zeros((S, DH), dtype=np.float32)
    sin = np.zeros((S, DH), dtype=np.float32)
    cos[:, 0::2] = np.cos(f)
    cos[:, 1::2] = np.cos(f)
    sin[:, 0::2] = np.sin(f)
    sin[:, 1::2] = np.sin(f)
    sinS = sin.copy()
    sinS[:, :DH // 2] *= -1.0
    return cos, sinS


def _u8view(a_int32):
    a = np.ascontiguousarray(a_int32)
    return np.ascontiguousarray(a.view(np.uint8)[:, ::4])


def kernel(L0_attn_x, L0_attn_Wq, L0_attn_Wk, L0_attn_Wv, L0_attn_Wo,
           L0_ffn_qw_gate, L0_ffn_z_gate, L0_ffn_s_gate,
           L0_ffn_qw_up, L0_ffn_z_up, L0_ffn_s_up,
           L0_ffn_qw_down, L0_ffn_z_down, L0_ffn_s_down):
    x = np.ascontiguousarray(
        np.asarray(L0_attn_x, dtype=np.float32).reshape(S, D))
    Wq = np.asarray(L0_attn_Wq, dtype=np.float32)
    Wk = np.asarray(L0_attn_Wk, dtype=np.float32)
    Wv = np.asarray(L0_attn_Wv, dtype=np.float32)
    Wo = np.asarray(L0_attn_Wo, dtype=np.float32)
    wqt = np.ascontiguousarray(
        Wq.reshape(H, 128, NDT, 128).transpose(0, 3, 2, 1)).reshape(H * 128, -1)
    wot = np.ascontiguousarray(
        Wo.reshape(NDT, 128, NDT, 128).transpose(0, 3, 2, 1)).reshape(D, -1)
    wkt = np.ascontiguousarray(
        Wk.reshape(DH, NDT, 128).transpose(2, 1, 0)).reshape(128, -1)
    wvt = np.ascontiguousarray(
        Wv.reshape(DH, NDT, 128).transpose(2, 1, 0)).reshape(128, -1)
    qgT = _u8view(np.ascontiguousarray(np.asarray(L0_ffn_qw_gate).T))
    quT = _u8view(np.ascontiguousarray(np.asarray(L0_ffn_qw_up).T))
    qdT = _u8view(np.ascontiguousarray(np.asarray(L0_ffn_qw_down).T))
    zgT = _u8view(np.ascontiguousarray(np.asarray(L0_ffn_z_gate).T))
    sgT = np.ascontiguousarray(np.asarray(L0_ffn_s_gate, dtype=np.float32).T)
    zuT = _u8view(np.ascontiguousarray(np.asarray(L0_ffn_z_up).T))
    suT = np.ascontiguousarray(np.asarray(L0_ffn_s_up, dtype=np.float32).T)
    zdT = _u8view(np.ascontiguousarray(np.asarray(L0_ffn_z_down).T))
    sdT = np.ascontiguousarray(np.asarray(L0_ffn_s_down, dtype=np.float32).T)

    cos, sinS = _rope_tables()
    alpha = np.linspace(0.0, 1.0, 256, dtype=np.float32)
    idf = np.eye(128, dtype=np.float32)

    in_maps = []
    for cid in range(NCORE):
        q0 = TOWN * cid
        lo, hi = q0 - HALO, q0 + TOWN + HALO
        xe = np.zeros((TEXT, D), dtype=np.float32)
        slo, shi = max(lo, 0), min(hi, S)
        xe[slo - lo:shi - lo, :] = x[slo:shi]
        cq = np.ascontiguousarray(cos[q0:q0 + TOWN].T)
        sq = np.ascontiguousarray(sinS[q0:q0 + TOWN].T)
        ck = np.zeros((128, TEXT), dtype=np.float32)
        sk = np.zeros((128, TEXT), dtype=np.float32)
        ck[:, slo - lo:shi - lo] = cos[slo:shi].T
        sk[:, slo - lo:shi - lo] = sinS[slo:shi].T
        alc = np.empty((128, 8), dtype=np.float32)
        onc = np.empty((128, 8), dtype=np.float32)
        for j in range(8):
            a = alpha[0:128] if j % 2 == 0 else alpha[128:256]
            if cid == 0 and j < 2:
                alc[:, j] = 1.0
                onc[:, j] = 0.0
            else:
                alc[:, j] = a
                onc[:, j] = 1.0 - a
        mkc = np.zeros((128, 20), dtype=np.float32)
        p = np.arange(128)
        for w in range(5):
            for kt in range(4):
                kp = q0 - HALO + 256 * w + 128 * kt + p
                mkc[:, 4 * w + kt] = np.where((kp >= 0) & (kp < S), 0.0, NEG)
        in_maps.append(dict(
            x_ext=xe, wqt=wqt, wot=wot, wkt=wkt, wvt=wvt,
            qg=qgT, qu=quT, qd=qdT, zg=zgT, sg=sgT, zu=zuT, su=suT,
            zd=zdT, sd=sdT,
            cosq=cq, sinq=sq, cosk=ck, sink=sk,
            alpc=alc, onec=onc, maskc=mkc, idf=idf,
        ))

    nc = build_program()
    res = run_bass_kernel_spmd(nc, in_maps, core_ids=list(range(NCORE)))
    outs = [np.asarray(res.results[cid]["out"], dtype=np.float32)
            for cid in range(NCORE)]
    return np.concatenate(outs, axis=0).reshape(1, S, D)


# version 20
# speedup vs baseline: 1.1145x; 1.0706x over previous
import contextlib
import numpy as np
import ml_dtypes
import concourse.bass as bass
import concourse.mybir as mybir
from concourse.bass_utils import run_bass_kernel_spmd

F32 = mybir.dt.float32
BF16 = mybir.dt.bfloat16
U8 = mybir.dt.uint8
AF = mybir.ActivationFunctionType
ALU = mybir.AluOpType
AXX = mybir.AxisListType.X

S = 8192
D = 4096
H = 32
DH = 128
HFF = 11008
NCORE = 8
TOWN = S // NCORE
HALO = 256
TEXT = TOWN + 2 * HALO
NDT = D // 128
NHT = HFF // 128
NG_IN = 32
NG_OUT = 86
SCALE = 1.0 / float(np.sqrt(DH))
NEG = -1e9

ENGS = ("sync", "tensor", "vector", "scalar", "gpsimd")


class Sem:
    def __init__(self, h):
        self.h = h
        self.n = 0


class Plan:
    def __init__(self):
        self.q = {e: [] for e in ENGS}

    def emit(self, eng, fn):
        self.q[eng].append(fn)

    def wait(self, eng, sem, val):
        if val > 0:
            self.emit(eng, lambda e, s=sem.h, v=val: e.wait_ge(s, v))

    def dma(self, eng, out, in_, sem, transpose=False):
        def f(e, o=out, i=in_, s=sem.h, t=transpose):
            e.dma_start(out=o, in_=i, transpose=t).then_inc(s, 16)
        self.emit(eng, f)
        sem.n += 16
        return sem.n

    def vsy(self, eng, sem):
        self.wait(eng, sem, sem.n)

    def op(self, eng, meth, *args, inc=None, inc_drain=None, **kw):
        def f(e, m=meth, a=args, k=kw, i=inc, d=inc_drain):
            ins = getattr(e, m)(*a, **k)
            if i is not None:
                ins.then_inc(i.h, 1)
            if d is not None:
                e.drain().then_inc(d.h, 1)
        self.emit(eng, f)
        if inc is not None:
            inc.n += 1
        if inc_drain is not None:
            inc_drain.n += 1
        if inc is not None or inc_drain is not None:
            return (inc or inc_drain).n


class Ctx:
    pass


DBG_OUT = ()


def _mk(c: Ctx):
    nc = c.nc
    ap = lambda *a, **k: nc.dram_tensor(*a, **k).ap()
    c.x_ext = ap("x_ext", [TEXT, D], F32, kind="ExternalInput")
    c.wqt = ap("wqt", [H * 128, NDT * 128], F32, kind="ExternalInput")
    c.wot = ap("wot", [NDT * 128, NDT * 128], F32, kind="ExternalInput")
    c.wkt = ap("wkt", [128, NDT * DH], F32, kind="ExternalInput")
    c.wvt = ap("wvt", [128, NDT * DH], F32, kind="ExternalInput")
    c.qg = ap("qg", [HFF, D], U8, kind="ExternalInput")
    c.qu = ap("qu", [HFF, D], U8, kind="ExternalInput")
    c.qd = ap("qd", [D, HFF], U8, kind="ExternalInput")
    c.zg = ap("zg", [HFF, NG_IN], U8, kind="ExternalInput")
    c.sg = ap("sg", [HFF, NG_IN], F32, kind="ExternalInput")
    c.zu = ap("zu", [HFF, NG_IN], U8, kind="ExternalInput")
    c.su = ap("su", [HFF, NG_IN], F32, kind="ExternalInput")
    c.zd = ap("zd", [D, NG_OUT], U8, kind="ExternalInput")
    c.sd = ap("sd", [D, NG_OUT], F32, kind="ExternalInput")
    c.cosq = ap("cosq", [128, TOWN], F32, kind="ExternalInput")
    c.sinq = ap("sinq", [128, TOWN], F32, kind="ExternalInput")
    c.cosk = ap("cosk", [128, TEXT], F32, kind="ExternalInput")
    c.sink = ap("sink", [128, TEXT], F32, kind="ExternalInput")
    c.alpc = ap("alpc", [128, 8], F32, kind="ExternalInput")
    c.onec = ap("onec", [128, 8], F32, kind="ExternalInput")
    c.maskc = ap("maskc", [128, 20], F32, kind="ExternalInput")
    c.idf = ap("idf", [128, 128], F32, kind="ExternalInput")
    c.out = ap("out", [TOWN, D], F32, kind="ExternalOutput")
    kk = lambda n: "ExternalOutput" if n in DBG_OUT else "Internal"
    c.xn_stash = ap("xn_stash", [TEXT, D], BF16, kind=kk("xn_stash"))
    c.gu_stash = ap("gu_stash", [2 * HFF, D], BF16, kind=kk("gu_stash"))
    c.wd_stash = ap("wd_stash", [D, HFF], BF16, kind=kk("wd_stash"))
    c.h_stash = ap("h_stash", [HFF, TOWN], BF16, kind=kk("h_stash"))
    c.projT_stash = ap("projT_st", [D, TOWN], F32, kind=kk("projT_st"))
    c.sum_stash = ap("sum_st", [D, TOWN], F32, kind=kk("sum_st"))

    sb = c.sb
    c.xnT_own = sb("xnT_own", [128, NDT * TOWN], BF16)
    c.alp_sb = sb("alp_sb", [128, 8], F32)
    c.one_sb = sb("one_sb", [128, 8], F32)
    c.msk_sb = sb("msk_sb", [128, 20], F32)
    c.idf_sb = sb("idf_sb", [128, 128], F32)
    c.bstack = contextlib.ExitStack()
    bsb = lambda n, s, d: c.bstack.enter_context(
        nc.sbuf_tensor(n, s, d, side="right"))
    c.v_aug = bsb("v_aug", [128, 12 * 132], BF16)
    c.krT = bsb("krT", [128, TEXT], BF16)
    c.cq_sb = bsb("cq_sb", [128, TOWN], F32)
    c.sq_sb = bsb("sq_sb", [128, TOWN], F32)
    c.g_halo = nc.sbuf_tensor("xnT_halo", [128, NDT * 512], BF16, side="right")
    c.xnT_halo = c.g_halo.__enter__()
    c.pb = [c.stack.enter_context(nc.psum_tensor(f"pb{i}", [128, 512], F32))
            for i in range(8)]

    for name in ("s_ld", "s_xt", "s_mm", "s_ve", "s_ac", "s_vsy"):
        setattr(c, name, Sem(c.stack.enter_context(nc.semaphore(name))))

    def sems(name, k):
        return [Sem(c.stack.enter_context(nc.semaphore(f"{name}{i}")))
                for i in range(k)]
    c.ldx = sems("ldx", 2)
    c.xw = sems("xw", 2)
    c.gq = sems("gq", 3)
    c.lq = sems("lq", 4)
    c.swl = sems("swl", 4)
    c.ldq = sems("ldq", 2)
    c.swd = sems("swd", 2)
    c.pan = sems("pan", 2)
    c.pans = sems("pans", 2)
    c.hw = sems("hw", 2)
    c.wd = sems("wd", 4)
    c.hh = sems("hh", 4)
    c.spr = sems("spr", 1)[0]
    c.sm = sems("sm", 2)
    c.pj = sems("pj", 2)
    c.dx = sems("dx", 2)
    c.stl = sems("stl", 2)
    c.ou = sems("ou", 2)
    c.bars = [Sem(c.stack.enter_context(nc.semaphore(f"bar{i}")))
              for i in range(10)]
    c.bar_i = 0


def barrier(c):
    sem = c.bars[c.bar_i]
    c.bar_i += 1
    for e in ENGS:
        def f(en, h=sem.h):
            en.drain()
            en.sem_inc(h, 1)
        c.P.emit(e, f)
    sem.n += 5
    for e in ENGS:
        c.P.wait(e, sem, sem.n)


def phase_a(c):
    P, nc = c.P, c.nc
    P.dma("sync", c.cq_sb[:], c.cosq, c.s_ld)
    P.dma("sync", c.sq_sb[:], c.sinq, c.s_ld)
    P.dma("sync", c.alp_sb[:], c.alpc, c.s_ld)
    P.dma("sync", c.one_sb[:], c.onec, c.s_ld)
    P.dma("sync", c.msk_sb[:], c.maskc, c.s_ld)
    P.dma("sync", c.idf_sb[:], c.idf, c.s_ld)

    ph = contextlib.ExitStack()
    t = lambda n, s, d: ph.enter_context(nc.sbuf_tensor(n, s, d))
    x_sb = [t("x_sb0", [128, D], F32), t("x_sb1", [128, D], F32)]
    xc_sb = t("xc_sb", [128, D], F32)
    sqr_sb = t("sqr_sb", [128, D], F32)
    xnb = [t("xnb0", [128, D], BF16), t("xnb1", [128, D], BF16)]
    lncol = t("lncol", [128, 4], F32)

    NSTRIP = TEXT // 128
    for st in range(NSTRIP):
        par = st % 2
        if st >= 2:
            P.wait("sync", c.xw[par], 16 * (st // 2))
        P.dma("sync", x_sb[par][:], c.x_ext[128 * st:128 * (st + 1), :],
              c.ldx[par])
        P.wait("vector", c.ldx[par], c.ldx[par].n)
        P.wait("vector", c.s_ac, 2 * st)
        if st >= 2:
            P.wait("vector", c.xw[par], 16 * (st // 2))
        P.op("vector", "reduce_sum", lncol[:, 0:1], x_sb[par][:], AXX,
             inc=c.s_vsy)
        P.vsy("vector", c.s_vsy)
        P.op("vector", "tensor_scalar", lncol[:, 1:2], lncol[:, 0:1],
             -1.0 / D, None, ALU.mult, inc=c.s_vsy)
        P.vsy("vector", c.s_vsy)
        P.op("vector", "tensor_scalar", xc_sb[:], x_sb[par][:],
             lncol[:, 1:2], None, ALU.add, inc=c.s_ve)
        P.wait("scalar", c.s_ve, c.s_ve.n)
        P.op("scalar", "activation", sqr_sb[:], xc_sb[:], AF.Square,
             inc=c.s_ac)
        P.wait("vector", c.s_ac, c.s_ac.n)
        P.op("vector", "reduce_sum", lncol[:, 2:3], sqr_sb[:], AXX,
             inc=c.s_vsy)
        P.vsy("vector", c.s_vsy)
        P.op("vector", "tensor_scalar", lncol[:, 2:3], lncol[:, 2:3],
             1.0 / D, 1e-5, ALU.mult, ALU.add, inc=c.s_ve)
        P.wait("scalar", c.s_ve, c.s_ve.n)
        P.op("scalar", "activation", lncol[:, 3:4], lncol[:, 2:3], AF.Sqrt,
             inc=c.s_ac)
        P.wait("vector", c.s_ac, c.s_ac.n)
        P.op("vector", "reciprocal", lncol[:, 0:1], lncol[:, 3:4],
             inc=c.s_vsy)
        P.vsy("vector", c.s_vsy)
        P.op("vector", "tensor_scalar", xnb[par][:], xc_sb[:],
             lncol[:, 0:1], None, ALU.mult, inc_drain=c.s_ve)
        P.wait("gpsimd", c.s_ve, c.s_ve.n)
        P.dma("gpsimd", c.xn_stash[128 * st:128 * (st + 1), :], xnb[par][:],
              c.xw[par])

    P.wait("sync", c.xw[0], 16 * (NSTRIP // 2))
    P.wait("sync", c.xw[1], 16 * (NSTRIP // 2))
    for dt in range(NDT):
        c0, c1 = 128 * dt, 128 * (dt + 1)
        P.dma("sync", c.xnT_own[:, TOWN * dt:TOWN * (dt + 1)],
              c.xn_stash[HALO:HALO + TOWN, c0:c1], c.s_xt, transpose=True)
        P.dma("sync", c.xnT_halo[:, 512 * dt:512 * dt + 256],
              c.xn_stash[0:HALO, c0:c1], c.s_xt, transpose=True)
        P.dma("sync", c.xnT_halo[:, 512 * dt + 256:512 * dt + 512],
              c.xn_stash[TEXT - HALO:TEXT, c0:c1], c.s_xt, transpose=True)
    ph.close()


def phase_b1(c):
    P, nc, pb = c.P, c.nc, c.pb
    ph = contextlib.ExitStack()
    t = lambda n, s, d: ph.enter_context(nc.sbuf_tensor(n, s, d))
    ck_sb = t("ck_sb", [128, TEXT], F32)
    sk_sb = t("sk_sb", [128, TEXT], F32)
    wk_bf = t("wk_bf", [128, NDT * DH], BF16)
    wv_bf = t("wv_bf", [128, NDT * DH], BF16)
    kf_sb = t("kf_sb", [128, 512], F32)
    ksw_sb = t("ksw_sb", [128, 512], F32)
    kt1_sb = t("kt1_sb", [128, 512], F32)

    P.dma("sync", ck_sb[:], c.cosk, c.s_ld)
    P.dma("sync", sk_sb[:], c.sink, c.s_ld)
    P.dma("gpsimd", wk_bf[:], c.wkt, c.s_ld)
    P.dma("gpsimd", wv_bf[:], c.wvt, c.s_ld)
    P.wait("tensor", c.s_ld, c.s_ld.n)
    P.wait("vector", c.s_ld, c.s_ld.n)
    P.wait("scalar", c.s_ld, c.s_ld.n)
    P.wait("tensor", c.s_xt, c.s_xt.n)

    def ext_rhs_pieces(lo, hi):
        segs = []
        if lo < 256:
            e = min(hi, 256)
            segs.append(("halo", lo, e - lo))
            lo = e
        if lo < hi and lo < 1280:
            e = min(hi, 1280)
            segs.append(("own", lo - 256, e - lo))
            lo = e
        if lo < hi:
            segs.append(("halo", 256 + (lo - 1280), hi - lo))
        return segs

    for ch in range(3):
        segs = ext_rhs_pieces(512 * ch, 512 * (ch + 1))
        off = 0
        for kind, start, width in segs:
            for dt in range(NDT):
                if kind == "own":
                    rhs = c.xnT_own[:, TOWN * dt + start:TOWN * dt + start + width]
                else:
                    rhs = c.xnT_halo[:, 512 * dt + start:512 * dt + start + width]
                P.op("tensor", "matmul", pb[ch][:, off:off + width],
                     wk_bf[:, 128 * dt:128 * (dt + 1)], rhs,
                     start=(dt == 0), stop=(dt == NDT - 1),
                     inc=(c.s_mm if dt == NDT - 1 else None))
            off += width
        P.wait("vector", c.s_mm, c.s_mm.n)
        cc = ck_sb[:, 512 * ch:512 * (ch + 1)]
        scs = sk_sb[:, 512 * ch:512 * (ch + 1)]
        P.op("vector", "tensor_copy", kf_sb[:], pb[ch][:])
        P.op("vector", "tensor_copy", ksw_sb[0:64, :], kf_sb[64:128, :])
        P.op("vector", "tensor_copy", ksw_sb[64:128, :], kf_sb[0:64, :])
        P.op("vector", "tensor_mul", kt1_sb[:], kf_sb[:], cc)
        P.op("vector", "tensor_mul", ksw_sb[:], ksw_sb[:], scs)
        P.op("vector", "tensor_add", c.krT[:, 512 * ch:512 * (ch + 1)],
             kt1_sb[:], ksw_sb[:], inc=c.s_ve)
    k_done = c.s_ve.n

    for tt in range(12):
        lo = 128 * tt
        if lo < 256:
            def lhs_t(dt, lo=lo):
                return c.xnT_halo[:, 512 * dt + lo:512 * dt + lo + 128]
        elif lo < 1280:
            def lhs_t(dt, lo=lo):
                o = lo - 256
                return c.xnT_own[:, TOWN * dt + o:TOWN * dt + o + 128]
        else:
            def lhs_t(dt, lo=lo):
                o = 256 + lo - 1280
                return c.xnT_halo[:, 512 * dt + o:512 * dt + o + 128]
        bank = pb[3 + (tt % 2)]
        if tt >= 2:
            P.wait("tensor", c.s_ve, k_done + tt - 1)
        for dt in range(NDT):
            P.op("tensor", "matmul", bank[:, 0:128],
                 lhs_t(dt), wv_bf[:, 128 * dt:128 * (dt + 1)],
                 start=(dt == 0), stop=(dt == NDT - 1),
                 inc=(c.s_mm if dt == NDT - 1 else None))
        P.wait("vector", c.s_mm, c.s_mm.n)
        P.op("vector", "tensor_copy",
             c.v_aug[:, 132 * tt:132 * tt + 128], bank[:, 0:128])
        P.op("vector", "memset",
             c.v_aug[:, 132 * tt + 128:132 * tt + 129], 1.0, inc=c.s_ve)
    ph.close()


WIN_Q = [(0, 256), (0, 512), (256, 768), (512, 1024), (768, 1024)]
WIN_SLOT0 = [2, 0, 0, 0, 0]


def phase_b2(c, attn_outT):
    P, nc, pb = c.P, c.nc, c.pb
    ph = contextlib.ExitStack()
    t = lambda n, s, d: ph.enter_context(nc.sbuf_tensor(n, s, d))
    wq_bf = [t(f"wq_bf{i}", [128, NDT * 128], BF16) for i in range(3)]
    qf_sb = t("qf_sb", [128, TOWN], F32)
    qsw_sb = t("qsw_sb", [128, TOWN], F32)
    qt1_sb = t("qt1_sb", [128, TOWN], F32)
    qrT = [t("qrT0", [128, TOWN], BF16), t("qrT1", [128, TOWN], BF16)]
    attnT = t("attnT", [128, 2048], BF16)
    wbuf = [t("wb0", [128, 4 * 132], F32), t("wb1", [128, 4 * 132], F32)]
    bcol = t("bcol", [128, 4], F32)
    blf = [t("blf0", [128, 2 * 128], F32), t("blf1", [128, 2 * 128], F32)]

    sig_done = {k: 0 for k in range(4)}
    av_bank_evac = {k: 0 for k in range(4)}
    tr_bank_evac = {0: 0, 1: 0}
    blend_prog = 0
    av_mm_done = 0
    gq_val = {}
    rope_done = {}
    qmm_done = {}

    def q_chunks(h):
        par = h % 3
        chunks = []
        for cb in range(8):
            def f(cb=cb, par=par, h=h):
                if cb == 0:
                    P.wait("tensor", c.gq[par], gq_val[h])
                    if h >= 1:
                        P.wait("tensor", c.s_ve, rope_done[h - 1])
                for dt in range(4 * cb, 4 * cb + 4):
                    lhs = wq_bf[par][:, 128 * dt:128 * (dt + 1)]
                    for th in range(2):
                        P.op("tensor", "matmul", pb[4 + th][:, 0:512],
                             lhs,
                             c.xnT_own[:, TOWN * dt + 512 * th:
                                       TOWN * dt + 512 * th + 512],
                             start=(dt == 0), stop=(dt == NDT - 1),
                             inc=(c.s_mm if dt == NDT - 1 and th == 1
                                  else None))
                if cb == 7:
                    qmm_done[h] = c.s_mm.n
            chunks.append(f)
        return chunks

    def emit_rope(h):
        par = h % 2
        P.wait("vector", c.s_mm, qmm_done[h])
        P.op("vector", "tensor_copy", qf_sb[:, 0:512], pb[4][:])
        P.op("vector", "tensor_copy", qf_sb[:, 512:1024], pb[5][:])
        P.op("vector", "tensor_copy", qsw_sb[0:64, :], qf_sb[64:128, :])
        P.op("vector", "tensor_copy", qsw_sb[64:128, :], qf_sb[0:64, :])
        P.op("vector", "tensor_mul", qt1_sb[:], qf_sb[:], c.cq_sb[:])
        P.op("vector", "tensor_mul", qsw_sb[:], qsw_sb[:], c.sq_sb[:])
        P.op("vector", "tensor_add", qrT[par][:], qt1_sb[:], qsw_sb[:],
             inc=c.s_ve)
        rope_done[h] = c.s_ve.n

    for hh in range(3):
        P.dma("gpsimd", wq_bf[hh][:], c.wqt[128 * hh:128 * (hh + 1), :],
              c.gq[hh])
        gq_val[hh] = c.gq[hh].n
    for f in q_chunks(0):
        f()
    emit_rope(0)

    for h in range(H):
        par = h % 3
        pump = q_chunks(h + 1) if h + 1 < H else []
        pump_i = [0]
        bl_ve = {}

        def do_pump(n=1):
            for _ in range(n):
                if pump_i[0] < len(pump):
                    pump[pump_i[0]]()
                    pump_i[0] += 1

        if h + 3 < H:
            P.wait("gpsimd", c.s_mm, qmm_done[h])
            P.dma("gpsimd", wq_bf[par][:],
                  c.wqt[128 * (h + 3):128 * (h + 4), :], c.gq[par])
            gq_val[h + 3] = c.gq[par].n

        for w in range(5):
            qlo, qhi = WIN_Q[w]
            qn = qhi - qlo
            wslot = wbuf[w % 2]
            for kt in range(4):
                if kt == 0 and w == 0:
                    P.wait("tensor", c.s_ve, rope_done[h])
                P.wait("tensor", c.s_ac, av_bank_evac[kt])
                P.wait("tensor", c.s_ac, sig_done[kt])
                P.op("tensor", "matmul", pb[kt][:, 0:qn],
                     c.krT[:, 256 * w + 128 * kt:256 * w + 128 * (kt + 1)],
                     qrT[h % 2][:, qlo:qhi], start=True, stop=True,
                     inc=c.s_mm)
            sc_done = c.s_mm.n
            do_pump(1)
            for kt in range(4):
                P.wait("scalar", c.s_mm, sc_done - 3 + kt)
                P.wait("scalar", c.s_mm, av_mm_done)
                P.wait("scalar", c.s_ve, blend_prog)
                P.op("scalar", "activation", attnT[:, 512 * kt:512 * kt + qn],
                     pb[kt][:, 0:qn], AF.Sigmoid,
                     bias=c.msk_sb[:, 4 * w + kt:4 * w + kt + 1],
                     scale=SCALE, inc=c.s_ac)
                sig_done[kt] = c.s_ac.n
            nq = qn // 128
            for qs in range(nq):
                bank = pb[qs]
                P.wait("tensor", c.s_ac, sig_done[3])
                for kt in range(4):
                    P.op("tensor", "matmul", bank[:, 0:129],
                         attnT[:, 512 * kt + 128 * qs:512 * kt + 128 * (qs + 1)],
                         c.v_aug[:, 132 * (2 * w + kt):132 * (2 * w + kt) + 129],
                         start=(kt == 0), stop=(kt == 3),
                         inc=(c.s_mm if kt == 3 else None))
                av_mm_done = c.s_mm.n
                do_pump(1)
                P.wait("scalar", c.s_mm, av_mm_done)
                slot = WIN_SLOT0[w] + qs
                P.op("scalar", "copy", wslot[:, 132 * slot:132 * slot + 129],
                     bank[:, 0:129], inc=c.s_ac)
                av_bank_evac[qs] = c.s_ac.n

            if w == 2 and h + 1 < H:
                emit_rope(h + 1)

            def emit_blend(k):
                A = wbuf[k % 2]
                B = wbuf[(k + 1) % 2]
                for qs in range(2):
                    aslot = 2 + qs
                    bslot = qs
                    col = 2 * k + qs
                    P.wait("vector", c.s_ac, av_bank_evac[bslot])
                    P.op("vector", "reciprocal", bcol[:, 0:1],
                         A[:, 132 * aslot + 128:132 * aslot + 129])
                    P.op("vector", "reciprocal", bcol[:, 1:2],
                         B[:, 132 * bslot + 128:132 * bslot + 129],
                         inc=c.s_vsy)
                    P.vsy("vector", c.s_vsy)
                    P.op("vector", "tensor_mul", bcol[:, 2:3],
                         bcol[:, 0:1], c.one_sb[:, col:col + 1])
                    P.op("vector", "tensor_mul", bcol[:, 3:4],
                         bcol[:, 1:2], c.alp_sb[:, col:col + 1], inc=c.s_vsy)
                    P.vsy("vector", c.s_vsy)
                    bl = blf[k % 2][:, 128 * qs:128 * (qs + 1)]
                    P.op("vector", "tensor_scalar", bl,
                         A[:, 132 * aslot:132 * aslot + 128],
                         bcol[:, 2:3], None, ALU.mult)
                    P.op("vector", "scalar_tensor_tensor", bl,
                         B[:, 132 * bslot:132 * bslot + 128],
                         bcol[:, 3:4], bl, ALU.mult, ALU.add, inc=c.s_ve)
                return c.s_ve.n

            def emit_transpose(k, bl_ve):
                for qs in range(2):
                    col = 2 * k + qs
                    tbank = 6 + (qs % 2)
                    P.wait("tensor", c.s_ve, bl_ve)
                    P.wait("tensor", c.s_ve, tr_bank_evac[qs % 2])
                    P.op("tensor", "transpose", pb[tbank][:, 0:128],
                         blf[k % 2][:, 128 * qs:128 * (qs + 1)],
                         c.idf_sb[:], inc=c.s_mm)
                    P.wait("vector", c.s_mm, c.s_mm.n)
                    P.op("vector", "tensor_copy",
                         attn_outT[:, TOWN * h + 128 * col:
                                   TOWN * h + 128 * (col + 1)],
                         pb[tbank][:, 0:128], inc=c.s_ve)
                    tr_bank_evac[qs % 2] = c.s_ve.n

            if w >= 1:
                bl_ve[w - 1] = emit_blend(w - 1)
                blend_prog = c.s_ve.n
            if w >= 2:
                emit_transpose(w - 2, bl_ve[w - 2])
        emit_transpose(3, bl_ve[3])
        do_pump(16)
    ph.close()


def phase_b3(c, attn_outT, dq):
    P, nc, pb = c.P, c.nc, c.pb
    ph = contextlib.ExitStack()
    t = lambda n, s, d: ph.enter_context(nc.sbuf_tensor(n, s, d))
    wo_bf = [t(f"wo_bf{i}", [128, NDT * 128], BF16) for i in range(3)]
    pr_sb = [t("pr_sb0", [128, TOWN], F32), t("pr_sb1", [128, TOWN], F32)]
    wo_mm = {}
    gq_val = {}
    for hh in range(3):
        P.dma("gpsimd", wo_bf[hh][:], c.wot[128 * hh:128 * (hh + 1), :],
              c.gq[hh])
        gq_val[hh] = c.gq[hh].n
    for i in range(NDT):
        par = i % 2
        wpar = i % 3
        P.wait("tensor", c.gq[wpar], gq_val[i])
        if i >= 2:
            P.wait("tensor", c.pj[par], 16 * (i // 2))
        for ot in range(NDT):
            lhs = wo_bf[wpar][:, 128 * ot:128 * (ot + 1)]
            for th in range(2):
                P.op("tensor", "matmul", pb[2 * (i % 2) + th][:, 0:512],
                     lhs,
                     attn_outT[:, TOWN * ot + 512 * th:
                               TOWN * ot + 512 * (th + 1)],
                     start=(ot == 0), stop=(ot == NDT - 1),
                     inc=(c.s_mm if ot == NDT - 1 and th == 1 else None))
        wo_mm[i] = c.s_mm.n
        if i + 3 < NDT:
            P.wait("gpsimd", c.s_mm, wo_mm[i])
            P.dma("gpsimd", wo_bf[wpar][:],
                  c.wot[128 * (i + 3):128 * (i + 4), :], c.gq[wpar])
            gq_val[i + 3] = c.gq[wpar].n
        P.wait("vector", c.s_mm, c.s_mm.n)
        if i >= 2:
            P.wait("vector", c.pj[par], 16 * (i // 2))
        P.op("vector", "tensor_copy", pr_sb[par][:, 0:512], pb[2 * (i % 2)][:])
        P.op("vector", "tensor_copy", pr_sb[par][:, 512:1024],
             pb[2 * (i % 2) + 1][:], inc_drain=c.s_ve)
        P.wait("gpsimd", c.s_ve, c.s_ve.n)
        P.dma("gpsimd", c.projT_stash[128 * i:128 * (i + 1), :], pr_sb[par][:],
              c.pj[par])
        if i < 16:
            dq.load(i)
        if i % 2 == 1 and i // 2 < 12:
            dq.compute(i // 2)
    ph.close()


class DqGU:

    def __init__(self, c, scope):
        nc = c.nc
        t = lambda n, s, d: scope.enter_context(
            nc.sbuf_tensor(n, s, d, side="right"))
        self.qin = [t(f"qin{i}", [128, D], U8) for i in range(4)]
        self.qo = [t(f"qo{i}", [128, D], BF16) for i in range(2)]
        self.zin = t("zin", [128, 4 * 32], U8)
        self.sin_ = t("sin_", [128, 4 * 32], F32)
        self.zf = t("zf", [128, 4 * 32], F32)
        self.c = c
        self.ld = {}

    def load(self, q):
        c, P = self.c, self.c.P
        jj, kind = q // 2, q % 2
        lane = q % 4
        qw = c.qg if kind == 0 else c.qu
        zt = c.zg if kind == 0 else c.zu
        st_ = c.sg if kind == 0 else c.su
        r0, r1 = 128 * jj, 128 * (jj + 1)
        if q >= 4:
            P.wait("scalar", c.swl[q % 2], 16 * (q // 2 - 1))
        P.dma("scalar", self.qin[lane][:], qw[r0:r1, :], c.lq[lane])
        P.dma("scalar", self.zin[:, 32 * lane:32 * (lane + 1)], zt[r0:r1, :],
              c.lq[lane])
        P.dma("scalar", self.sin_[:, 32 * lane:32 * (lane + 1)], st_[r0:r1, :],
              c.lq[lane])
        self.ld[q] = c.lq[lane].n

    def compute(self, q):
        c, P = self.c, self.c.P
        jj, kind = q // 2, q % 2
        lane = q % 4
        olane = q % 2
        s0 = 256 * jj + 128 * kind
        P.wait("vector", c.lq[lane], self.ld[q])
        if q >= 2:
            P.wait("vector", c.swl[olane], 16 * (q // 2))
        P.op("vector", "tensor_copy", self.zf[:, 32 * lane:32 * (lane + 1)],
             self.zin[:, 32 * lane:32 * (lane + 1)], inc=c.s_vsy)
        P.vsy("vector", c.s_vsy)
        for g in range(NG_IN):
            csl = slice(128 * g, 128 * (g + 1))
            zc = slice(32 * lane + g, 32 * lane + g + 1)
            P.op("vector", "tensor_scalar", self.qo[olane][:, csl],
                 self.qin[lane][:, csl], self.zf[:, zc], self.sin_[:, zc],
                 ALU.subtract, ALU.mult,
                 inc_drain=(c.s_ve if g == NG_IN - 1 else None))
        P.wait("gpsimd", c.s_ve, c.s_ve.n)
        P.dma("gpsimd", c.gu_stash[s0:s0 + 128, :], self.qo[olane][:],
              c.swl[olane])


class DqDown:

    def __init__(self, c, scope):
        nc = c.nc
        HH = HFF // 2
        t = lambda n, s, d: scope.enter_context(nc.sbuf_tensor(n, s, d))
        self.qind = [t(f"qind{i}", [128, HH], U8) for i in range(2)]
        self.qodd = [t(f"qodd{i}", [128, HH], BF16) for i in range(2)]
        self.zind = t("zind", [128, 2 * NG_OUT], U8)
        self.sind = t("sind", [128, 2 * NG_OUT], F32)
        self.zfd = t("zfd", [128, 2 * NG_OUT], F32)
        self.c = c
        self.ld = {}

    def load(self, dj):
        c, P = self.c, self.c.P
        HH = HFF // 2
        dt, half = dj // 2, dj % 2
        lane = dj % 2
        r0, r1 = 128 * dt, 128 * (dt + 1)
        h0 = HH * half
        if dj >= 2:
            P.wait("scalar", c.swd[lane], 16 * (dj // 2))
        P.dma("scalar", self.zind[:, NG_OUT * lane:NG_OUT * (lane + 1)],
              c.zd[r0:r1, :], c.ldq[lane])
        P.dma("scalar", self.sind[:, NG_OUT * lane:NG_OUT * (lane + 1)],
              c.sd[r0:r1, :], c.ldq[lane])
        P.dma("scalar", self.qind[lane][:], c.qd[r0:r1, h0:h0 + HH],
              c.ldq[lane])
        self.ld[dj] = c.ldq[lane].n

    def compute(self, dj):
        c, P = self.c, self.c.P
        HH = HFF // 2
        NGH = NG_OUT // 2
        dt, half = dj // 2, dj % 2
        lane = dj % 2
        r0, r1 = 128 * dt, 128 * (dt + 1)
        h0 = HH * half
        P.wait("vector", c.ldq[lane], self.ld[dj])
        P.op("vector", "tensor_copy",
             self.zfd[:, NG_OUT * lane:NG_OUT * (lane + 1)],
             self.zind[:, NG_OUT * lane:NG_OUT * (lane + 1)], inc=c.s_vsy)
        P.vsy("vector", c.s_vsy)
        for gg in range(NGH):
            g = NGH * half + gg
            csl = slice(128 * gg, 128 * (gg + 1))
            zc = slice(NG_OUT * lane + g, NG_OUT * lane + g + 1)
            P.op("vector", "tensor_scalar", self.qodd[lane][:, csl],
                 self.qind[lane][:, csl], self.zfd[:, zc], self.sind[:, zc],
                 ALU.subtract, ALU.mult,
                 inc_drain=(c.s_ve if gg == NGH - 1 else None))
        P.wait("gpsimd", c.s_ve, c.s_ve.n)
        P.dma("gpsimd", c.wd_stash[r0:r1, h0:h0 + HH], self.qodd[lane][:],
              c.swd[lane])


def phase_c1(c, dq, dqd):
    P, nc, pb = c.P, c.nc, c.pb
    ph = contextlib.ExitStack()
    t = lambda n, s, d: ph.enter_context(nc.sbuf_tensor(n, s, d))
    pan = [t("pan0", [128, NDT * 512], BF16),
           t("pan1", [128, NDT * 512], BF16)]
    sg_sb = [t("sg_sb0", [128, TOWN], BF16), t("sg_sb1", [128, TOWN], BF16)]
    h_sb = [t("h_sb0", [128, TOWN], BF16), t("h_sb1", [128, TOWN], BF16)]
    NPAN = NHT // 2
    pan_ld = {}
    gu_mm = {}
    h_ve = {}
    NGU = 2 * NHT
    NDJ = 2 * NDT
    for dj in range(2):
        dqd.load(dj)
    for J in range(NPAN):
        ppar = J % 2
        for q in range(4 * J + 12, min(4 * J + 16, NGU)):
            dq.compute(q)
        for dj in range(2 * J, min(2 * J + 2, NDJ)):
            dqd.compute(dj)
        if J >= 2:
            P.wait("sync", c.s_mm, gu_mm[2 * (J - 2) + 1])
        P.wait("sync", c.swl[0], 16 * min(2 * J + 2, NHT))
        P.wait("sync", c.swl[1], 16 * min(2 * J + 2, NHT))
        for dt in range(NDT):
            P.dma("sync", pan[ppar][:, 512 * dt:512 * (dt + 1)],
                  c.gu_stash[512 * J:512 * (J + 1), 128 * dt:128 * (dt + 1)],
                  c.pan[ppar], transpose=True)
        pan_ld[J] = c.pan[ppar].n
        for r in range(2):
            jj = 2 * J + r
            jpar = jj % 2
            P.wait("tensor", c.pan[ppar], pan_ld[J])
            if jj >= 2:
                P.wait("tensor", c.s_ve, h_ve[jj - 2])
            for dt in range(NDT):
                glhs = pan[ppar][:, 512 * dt + 256 * r:
                                512 * dt + 256 * r + 128]
                ulhs = pan[ppar][:, 512 * dt + 256 * r + 128:
                                512 * dt + 256 * r + 256]
                for th in range(2):
                    P.op("tensor", "matmul", pb[4 * jpar + th][:, 0:512],
                         glhs,
                         c.xnT_own[:, TOWN * dt + 512 * th:
                                   TOWN * dt + 512 * (th + 1)],
                         start=(dt == 0), stop=(dt == NDT - 1),
                         inc=(c.s_mm if dt == NDT - 1 and th == 1 else None))
                for th in range(2):
                    P.op("tensor", "matmul",
                         pb[4 * jpar + 2 + th][:, 0:512],
                         ulhs,
                         c.xnT_own[:, TOWN * dt + 512 * th:
                                   TOWN * dt + 512 * (th + 1)],
                         start=(dt == 0), stop=(dt == NDT - 1),
                         inc=(c.s_mm if dt == NDT - 1 and th == 1 else None))
            gu_mm[jj] = c.s_mm.n
            P.wait("scalar", c.s_mm, gu_mm[jj] - 1)
            if jj >= 2:
                P.wait("scalar", c.s_ve, h_ve[jj - 2])
            P.op("scalar", "activation", sg_sb[jpar][:, 0:512],
                 pb[4 * jpar + 0][:], AF.Silu)
            P.op("scalar", "activation", sg_sb[jpar][:, 512:1024],
                 pb[4 * jpar + 1][:], AF.Silu, inc=c.s_ac)
            P.wait("vector", c.s_ac, c.s_ac.n)
            P.wait("vector", c.s_mm, gu_mm[jj])
            if jj >= 2:
                P.wait("vector", c.hw[jpar], 16 * (jj // 2))
            P.op("vector", "tensor_mul", h_sb[jpar][:, 0:512],
                 sg_sb[jpar][:, 0:512], pb[4 * jpar + 2][:])
            P.op("vector", "tensor_mul", h_sb[jpar][:, 512:1024],
                 sg_sb[jpar][:, 512:1024], pb[4 * jpar + 3][:],
                 inc_drain=c.s_ve)
            h_ve[jj] = c.s_ve.n
            P.wait("gpsimd", c.s_ve, c.s_ve.n)
            P.dma("gpsimd", c.h_stash[128 * jj:128 * (jj + 1), :],
                  h_sb[jpar][:], c.hw[jpar])
        for q in range(4 * J + 16, min(4 * J + 20, NGU)):
            dq.load(q)
        for dj in range(2 * J + 2, min(2 * J + 4, NDJ)):
            dqd.load(dj)
    ph.close()


def phase_c2(c):
    P, nc, pb = c.P, c.nc, c.pb
    ph = contextlib.ExitStack()
    t = lambda n, s, d: ph.enter_context(nc.sbuf_tensor(n, s, d))
    wdt = [t(f"wdt{i}", [128, 512], BF16) for i in range(4)]
    hst = [t(f"hst{i}", [128, TOWN], BF16) for i in range(4)]
    prt = t("prt", [128, 4 * TOWN], F32)
    smt = [t("smt0", [128, TOWN], F32), t("smt1", [128, TOWN], F32)]
    for s in (c.hw[0], c.hw[1], c.pj[0], c.pj[1], c.swd[0], c.swd[1]):
        P.wait("sync", s, s.n)
        P.wait("gpsimd", s, s.n)
    dn_mm = {}
    sum_ve = 0
    for ch in range(8):
        P.wait("sync", c.s_ve, sum_ve)
        for ic in range(4):
            i = 4 * ch + ic
            P.dma("sync", prt[:, TOWN * ic:TOWN * (ic + 1)],
                  c.projT_stash[128 * i:128 * (i + 1), :], c.spr)
        prj_ld = c.spr.n
        for jj in range(NHT):
            lane = jj % 4
            prev = (ch, jj - 4) if jj >= 4 else (ch - 1, 82 + (jj + 2) % 4)
            if prev in dn_mm:
                P.wait("sync", c.s_mm, dn_mm[prev])
                P.wait("gpsimd", c.s_mm, dn_mm[prev])
            P.dma("sync", wdt[lane][:],
                  c.wd_stash[512 * ch:512 * (ch + 1),
                             128 * jj:128 * (jj + 1)], c.wd[lane],
                  transpose=True)
            P.dma("gpsimd", hst[lane][:],
                  c.h_stash[128 * jj:128 * (jj + 1), :], c.hh[lane])
            P.wait("tensor", c.wd[lane], c.wd[lane].n)
            P.wait("tensor", c.hh[lane], c.hh[lane].n)
            if jj == 0 and ch > 0:
                P.wait("tensor", c.s_ve, sum_ve)
            for ic in range(4):
                for th in range(2):
                    P.op("tensor", "matmul", pb[2 * ic + th][:, 0:512],
                         wdt[lane][:, 128 * ic:128 * (ic + 1)],
                         hst[lane][:, 512 * th:512 * (th + 1)],
                         start=(jj == 0), stop=(jj == NHT - 1),
                         inc=(c.s_mm if (th == 1 and ic == 3) else None))
            dn_mm[(ch, jj)] = c.s_mm.n
        P.wait("vector", c.s_mm, dn_mm[(ch, NHT - 1)])
        P.wait("vector", c.spr, prj_ld)
        for ic in range(4):
            i = 4 * ch + ic
            k = 4 * ch + ic
            sl = ic % 2
            if k >= 2:
                P.wait("vector", c.sm[sl], 16 * (k // 2))
            for th in range(2):
                P.op("vector", "tensor_add",
                     smt[sl][:, 512 * th:512 * (th + 1)],
                     pb[2 * ic + th][:],
                     prt[:, TOWN * ic + 512 * th:TOWN * ic + 512 * (th + 1)],
                     inc_drain=(c.s_ve if th == 1 else None))
            P.wait("gpsimd", c.s_ve, c.s_ve.n)
            P.dma("gpsimd", c.sum_stash[128 * i:128 * (i + 1), :],
                  smt[sl][:], c.sm[sl])
        sum_ve = c.s_ve.n
    ph.close()


def phase_d(c):
    P, nc, pb = c.P, c.nc, c.pb
    ph = contextlib.ExitStack()
    t = lambda n, s, d: ph.enter_context(nc.sbuf_tensor(n, s, d))
    xs = [t("xs0", [128, D], F32), t("xs1", [128, D], F32)]
    stile = [t("stile0", [128, 128], F32), t("stile1", [128, 128], F32)]
    ostrip = [t("ostrip0", [128, D], F32), t("ostrip1", [128, D], F32)]
    for s in (c.sm[0], c.sm[1]):
        P.wait("sync", s, s.n)
    tr_mm = {0: 0, 1: 0}
    ev_ve = {0: 0, 1: 0}
    dx_val = {}
    for tt in range(8):
        par = tt % 2
        if tt >= 2:
            P.wait("sync", c.ou[par], 16 * (tt // 2))
        P.dma("sync", xs[par][:],
              c.x_ext[HALO + 128 * tt:HALO + 128 * (tt + 1), :], c.dx[par])
        dx_val[tt] = c.dx[par].n
        for i in range(NDT):
            sl = i % 2
            P.wait("sync", c.s_mm, tr_mm[sl])
            P.dma("sync", stile[sl][:],
                  c.sum_stash[128 * i:128 * (i + 1), 128 * tt:128 * (tt + 1)],
                  c.stl[sl])
            P.wait("tensor", c.stl[sl], c.stl[sl].n)
            P.wait("tensor", c.s_ve, ev_ve[sl])
            P.op("tensor", "transpose", pb[sl][:, 0:128], stile[sl][:],
                 c.idf_sb[:], inc=c.s_mm)
            tr_mm[sl] = c.s_mm.n
            P.wait("vector", c.s_mm, c.s_mm.n)
            if i == 0:
                P.wait("vector", c.dx[par], dx_val[tt])
                if tt >= 2:
                    P.wait("vector", c.ou[par], 16 * (tt // 2))
            P.op("vector", "tensor_add",
                 ostrip[par][:, 128 * i:128 * (i + 1)],
                 pb[sl][:, 0:128], xs[par][:, 128 * i:128 * (i + 1)],
                 **({"inc_drain": c.s_ve} if i == NDT - 1 else {"inc": c.s_ve}))
            ev_ve[sl] = c.s_ve.n
        P.wait("gpsimd", c.s_ve, c.s_ve.n)
        P.dma("gpsimd", c.out[128 * tt:128 * (tt + 1), :], ostrip[par][:],
              c.ou[par])
    P.wait("gpsimd", c.ou[0], c.ou[0].n)
    P.wait("gpsimd", c.ou[1], c.ou[1].n)
    ph.close()


def build_program():
    c = Ctx()
    c.nc = bass.Bass("TRN2", target_bir_lowering=False, debug=False)
    c.P = Plan()
    c.stack = contextlib.ExitStack()
    c.sb = lambda n, s, d: c.stack.enter_context(c.nc.sbuf_tensor(n, s, d))
    _mk(c)

    phase_a(c)
    barrier(c)
    phase_b1(c)
    c.g_halo.__exit__(None, None, None)
    barrier(c)
    g_attn = c.nc.sbuf_tensor("attn_outT", [128, H * TOWN], BF16)
    attn_outT = g_attn.__enter__()
    phase_b2(c, attn_outT)
    c.bstack.close()
    barrier(c)
    dq_scope = contextlib.ExitStack()
    dq = DqGU(c, dq_scope)
    phase_b3(c, attn_outT, dq)
    g_attn.__exit__(None, None, None)
    barrier(c)
    dqd_scope = contextlib.ExitStack()
    dqd = DqDown(c, dqd_scope)
    phase_c1(c, dq, dqd)
    dq_scope.close()
    dqd_scope.close()
    barrier(c)
    phase_c2(c)
    barrier(c)
    phase_d(c)

    with c.nc.Block() as block:
        @block.sync
        def _(e):
            for fn in c.P.q["sync"]:
                fn(e)

        @block.tensor
        def _(e):
            for fn in c.P.q["tensor"]:
                fn(e)

        @block.vector
        def _(e):
            for fn in c.P.q["vector"]:
                fn(e)

        @block.scalar
        def _(e):
            for fn in c.P.q["scalar"]:
                fn(e)

        @block.gpsimd
        def _(e):
            for fn in c.P.q["gpsimd"]:
                fn(e)

    c.stack.close()
    return c.nc


def _rope_tables():
    inv_freq = 1.0 / 10000.0 ** (np.arange(0, DH, 2, dtype=np.float32) / DH)
    t = np.arange(S, dtype=np.float32)
    f = np.outer(t, inv_freq)
    cos = np.zeros((S, DH), dtype=np.float32)
    sin = np.zeros((S, DH), dtype=np.float32)
    cos[:, 0::2] = np.cos(f)
    cos[:, 1::2] = np.cos(f)
    sin[:, 0::2] = np.sin(f)
    sin[:, 1::2] = np.sin(f)
    sinS = sin.copy()
    sinS[:, :DH // 2] *= -1.0
    return cos, sinS


def _u8view(a_int32):
    a = np.ascontiguousarray(a_int32)
    return np.ascontiguousarray(a.view(np.uint8)[:, ::4])


def kernel(L0_attn_x, L0_attn_Wq, L0_attn_Wk, L0_attn_Wv, L0_attn_Wo,
           L0_ffn_qw_gate, L0_ffn_z_gate, L0_ffn_s_gate,
           L0_ffn_qw_up, L0_ffn_z_up, L0_ffn_s_up,
           L0_ffn_qw_down, L0_ffn_z_down, L0_ffn_s_down):
    x = np.ascontiguousarray(
        np.asarray(L0_attn_x, dtype=np.float32).reshape(S, D))
    Wq = np.asarray(L0_attn_Wq, dtype=np.float32)
    Wk = np.asarray(L0_attn_Wk, dtype=np.float32)
    Wv = np.asarray(L0_attn_Wv, dtype=np.float32)
    Wo = np.asarray(L0_attn_Wo, dtype=np.float32)
    wqt = np.ascontiguousarray(
        Wq.reshape(H, 128, NDT, 128).transpose(0, 3, 2, 1)).reshape(H * 128, -1)
    wot = np.ascontiguousarray(
        Wo.reshape(NDT, 128, NDT, 128).transpose(0, 3, 2, 1)).reshape(D, -1)
    wkt = np.ascontiguousarray(
        Wk.reshape(DH, NDT, 128).transpose(2, 1, 0)).reshape(128, -1)
    wvt = np.ascontiguousarray(
        Wv.reshape(DH, NDT, 128).transpose(2, 1, 0)).reshape(128, -1)
    qgT = _u8view(np.ascontiguousarray(np.asarray(L0_ffn_qw_gate).T))
    quT = _u8view(np.ascontiguousarray(np.asarray(L0_ffn_qw_up).T))
    qdT = _u8view(np.ascontiguousarray(np.asarray(L0_ffn_qw_down).T))
    zgT = _u8view(np.ascontiguousarray(np.asarray(L0_ffn_z_gate).T))
    sgT = np.ascontiguousarray(np.asarray(L0_ffn_s_gate, dtype=np.float32).T)
    zuT = _u8view(np.ascontiguousarray(np.asarray(L0_ffn_z_up).T))
    suT = np.ascontiguousarray(np.asarray(L0_ffn_s_up, dtype=np.float32).T)
    zdT = _u8view(np.ascontiguousarray(np.asarray(L0_ffn_z_down).T))
    sdT = np.ascontiguousarray(np.asarray(L0_ffn_s_down, dtype=np.float32).T)

    cos, sinS = _rope_tables()
    alpha = np.linspace(0.0, 1.0, 256, dtype=np.float32)
    idf = np.eye(128, dtype=np.float32)

    in_maps = []
    for cid in range(NCORE):
        q0 = TOWN * cid
        lo, hi = q0 - HALO, q0 + TOWN + HALO
        xe = np.zeros((TEXT, D), dtype=np.float32)
        slo, shi = max(lo, 0), min(hi, S)
        xe[slo - lo:shi - lo, :] = x[slo:shi]
        cq = np.ascontiguousarray(cos[q0:q0 + TOWN].T)
        sq = np.ascontiguousarray(sinS[q0:q0 + TOWN].T)
        ck = np.zeros((128, TEXT), dtype=np.float32)
        sk = np.zeros((128, TEXT), dtype=np.float32)
        ck[:, slo - lo:shi - lo] = cos[slo:shi].T
        sk[:, slo - lo:shi - lo] = sinS[slo:shi].T
        alc = np.empty((128, 8), dtype=np.float32)
        onc = np.empty((128, 8), dtype=np.float32)
        for j in range(8):
            a = alpha[0:128] if j % 2 == 0 else alpha[128:256]
            if cid == 0 and j < 2:
                alc[:, j] = 1.0
                onc[:, j] = 0.0
            else:
                alc[:, j] = a
                onc[:, j] = 1.0 - a
        mkc = np.zeros((128, 20), dtype=np.float32)
        p = np.arange(128)
        for w in range(5):
            for kt in range(4):
                kp = q0 - HALO + 256 * w + 128 * kt + p
                mkc[:, 4 * w + kt] = np.where((kp >= 0) & (kp < S), 0.0, NEG)
        in_maps.append(dict(
            x_ext=xe, wqt=wqt, wot=wot, wkt=wkt, wvt=wvt,
            qg=qgT, qu=quT, qd=qdT, zg=zgT, sg=sgT, zu=zuT, su=suT,
            zd=zdT, sd=sdT,
            cosq=cq, sinq=sq, cosk=ck, sink=sk,
            alpc=alc, onec=onc, maskc=mkc, idf=idf,
        ))

    nc = build_program()
    res = run_bass_kernel_spmd(nc, in_maps, core_ids=list(range(NCORE)))
    outs = [np.asarray(res.results[cid]["out"], dtype=np.float32)
            for cid in range(NCORE)]
    return np.concatenate(outs, axis=0).reshape(1, S, D)


# version 21
# speedup vs baseline: 1.1303x; 1.0141x over previous
import contextlib
import numpy as np
import ml_dtypes
import concourse.bass as bass
import concourse.mybir as mybir
from concourse.bass_utils import run_bass_kernel_spmd

F32 = mybir.dt.float32
BF16 = mybir.dt.bfloat16
U8 = mybir.dt.uint8
AF = mybir.ActivationFunctionType
ALU = mybir.AluOpType
AXX = mybir.AxisListType.X

S = 8192
D = 4096
H = 32
DH = 128
HFF = 11008
NCORE = 8
TOWN = S // NCORE
HALO = 256
TEXT = TOWN + 2 * HALO
NDT = D // 128
NHT = HFF // 128
NG_IN = 32
NG_OUT = 86
SCALE = 1.0 / float(np.sqrt(DH))
NEG = -1e9

ENGS = ("sync", "tensor", "vector", "scalar", "gpsimd")


class Sem:
    def __init__(self, h):
        self.h = h
        self.n = 0


class Plan:
    def __init__(self):
        self.q = {e: [] for e in ENGS}

    def emit(self, eng, fn):
        self.q[eng].append(fn)

    def wait(self, eng, sem, val):
        if val > 0:
            self.emit(eng, lambda e, s=sem.h, v=val: e.wait_ge(s, v))

    def dma(self, eng, out, in_, sem, transpose=False):
        def f(e, o=out, i=in_, s=sem.h, t=transpose):
            e.dma_start(out=o, in_=i, transpose=t).then_inc(s, 16)
        self.emit(eng, f)
        sem.n += 16
        return sem.n

    def vsy(self, eng, sem):
        self.wait(eng, sem, sem.n)

    def op(self, eng, meth, *args, inc=None, inc_drain=None, **kw):
        def f(e, m=meth, a=args, k=kw, i=inc, d=inc_drain):
            ins = getattr(e, m)(*a, **k)
            if i is not None:
                ins.then_inc(i.h, 1)
            if d is not None:
                e.drain().then_inc(d.h, 1)
        self.emit(eng, f)
        if inc is not None:
            inc.n += 1
        if inc_drain is not None:
            inc_drain.n += 1
        if inc is not None or inc_drain is not None:
            return (inc or inc_drain).n


class Ctx:
    pass


DBG_OUT = ()


def _mk(c: Ctx):
    nc = c.nc
    ap = lambda *a, **k: nc.dram_tensor(*a, **k).ap()
    c.x_ext = ap("x_ext", [TEXT, D], F32, kind="ExternalInput")
    c.wqt = ap("wqt", [H * 128, NDT * 128], F32, kind="ExternalInput")
    c.wot = ap("wot", [NDT * 128, NDT * 128], F32, kind="ExternalInput")
    c.wkt = ap("wkt", [128, NDT * DH], F32, kind="ExternalInput")
    c.wvt = ap("wvt", [128, NDT * DH], F32, kind="ExternalInput")
    c.qg = ap("qg", [HFF, D], U8, kind="ExternalInput")
    c.qu = ap("qu", [HFF, D], U8, kind="ExternalInput")
    c.qd = ap("qd", [D, HFF], U8, kind="ExternalInput")
    c.zg = ap("zg", [HFF, NG_IN], U8, kind="ExternalInput")
    c.sg = ap("sg", [HFF, NG_IN], F32, kind="ExternalInput")
    c.zu = ap("zu", [HFF, NG_IN], U8, kind="ExternalInput")
    c.su = ap("su", [HFF, NG_IN], F32, kind="ExternalInput")
    c.zd = ap("zd", [D, NG_OUT], U8, kind="ExternalInput")
    c.sd = ap("sd", [D, NG_OUT], F32, kind="ExternalInput")
    c.cosq = ap("cosq", [128, TOWN], F32, kind="ExternalInput")
    c.sinq = ap("sinq", [128, TOWN], F32, kind="ExternalInput")
    c.cosk = ap("cosk", [128, TEXT], F32, kind="ExternalInput")
    c.sink = ap("sink", [128, TEXT], F32, kind="ExternalInput")
    c.alpc = ap("alpc", [128, 8], F32, kind="ExternalInput")
    c.onec = ap("onec", [128, 8], F32, kind="ExternalInput")
    c.maskc = ap("maskc", [128, 20], F32, kind="ExternalInput")
    c.idf = ap("idf", [128, 128], F32, kind="ExternalInput")
    c.out = ap("out", [TOWN, D], F32, kind="ExternalOutput")
    kk = lambda n: "ExternalOutput" if n in DBG_OUT else "Internal"
    c.xn_stash = ap("xn_stash", [TEXT, D], BF16, kind=kk("xn_stash"))
    c.gu_stash = ap("gu_stash", [2 * HFF, D], BF16, kind=kk("gu_stash"))
    c.wd_stash = ap("wd_stash", [D, HFF], BF16, kind=kk("wd_stash"))
    c.h_stash = ap("h_stash", [HFF, TOWN], BF16, kind=kk("h_stash"))
    c.projT_stash = ap("projT_st", [D, TOWN], F32, kind=kk("projT_st"))
    c.sum_stash = ap("sum_st", [D, TOWN], F32, kind=kk("sum_st"))

    sb = c.sb
    c.xnT_own = sb("xnT_own", [128, NDT * TOWN], BF16)
    c.alp_sb = sb("alp_sb", [128, 8], F32)
    c.one_sb = sb("one_sb", [128, 8], F32)
    c.msk_sb = sb("msk_sb", [128, 20], F32)
    c.idf_sb = sb("idf_sb", [128, 128], F32)
    c.bstack = contextlib.ExitStack()
    bsb = lambda n, s, d: c.bstack.enter_context(
        nc.sbuf_tensor(n, s, d, side="right"))
    c.v_aug = bsb("v_aug", [128, 12 * 132], BF16)
    c.krT = bsb("krT", [128, TEXT], BF16)
    c.cq_sb = bsb("cq_sb", [128, TOWN], F32)
    c.sq_sb = bsb("sq_sb", [128, TOWN], F32)
    c.g_halo = nc.sbuf_tensor("xnT_halo", [128, NDT * 512], BF16, side="right")
    c.xnT_halo = c.g_halo.__enter__()
    c.pb = [c.stack.enter_context(nc.psum_tensor(f"pb{i}", [128, 512], F32))
            for i in range(8)]

    for name in ("s_ld", "s_xt", "s_mm", "s_ve", "s_ac", "s_vsy"):
        setattr(c, name, Sem(c.stack.enter_context(nc.semaphore(name))))

    def sems(name, k):
        return [Sem(c.stack.enter_context(nc.semaphore(f"{name}{i}")))
                for i in range(k)]
    c.ldx = sems("ldx", 2)
    c.xw = sems("xw", 2)
    c.gq = sems("gq", 3)
    c.lq = sems("lq", 4)
    c.swl = sems("swl", 4)
    c.ldq = sems("ldq", 2)
    c.swd = sems("swd", 2)
    c.pan = sems("pan", 2)
    c.pans = sems("pans", 2)
    c.hw = sems("hw", 2)
    c.wd = sems("wd", 4)
    c.hh = sems("hh", 4)
    c.spr = sems("spr", 1)[0]
    c.sm = sems("sm", 2)
    c.pj = sems("pj", 2)
    c.dx = sems("dx", 2)
    c.stl = sems("stl", 2)
    c.ou = sems("ou", 2)
    c.bars = [Sem(c.stack.enter_context(nc.semaphore(f"bar{i}")))
              for i in range(10)]
    c.bar_i = 0


def barrier(c):
    sem = c.bars[c.bar_i]
    c.bar_i += 1
    for e in ENGS:
        def f(en, h=sem.h):
            en.drain()
            en.sem_inc(h, 1)
        c.P.emit(e, f)
    sem.n += 5
    for e in ENGS:
        c.P.wait(e, sem, sem.n)


def phase_a(c):
    P, nc = c.P, c.nc
    P.dma("sync", c.cq_sb[:], c.cosq, c.s_ld)
    P.dma("sync", c.sq_sb[:], c.sinq, c.s_ld)
    P.dma("sync", c.alp_sb[:], c.alpc, c.s_ld)
    P.dma("sync", c.one_sb[:], c.onec, c.s_ld)
    P.dma("sync", c.msk_sb[:], c.maskc, c.s_ld)
    P.dma("sync", c.idf_sb[:], c.idf, c.s_ld)

    ph = contextlib.ExitStack()
    t = lambda n, s, d: ph.enter_context(nc.sbuf_tensor(n, s, d))
    x_sb = [t("x_sb0", [128, D], F32), t("x_sb1", [128, D], F32)]
    xc_sb = [t("xc_sb0", [128, D], F32), t("xc_sb1", [128, D], F32)]
    sqr_sb = x_sb
    xnb = [t("xnb0", [128, D], BF16), t("xnb1", [128, D], BF16)]
    lncol = t("lncol", [128, 16], F32)

    NSTRIP = TEXT // 128
    def ln_steps(st):
        par = st % 2
        cl = lncol[:, 8 * par:8 * par + 8]
        steps = []
        def s0():
            if st >= 2:
                P.wait("sync", c.xw[par], 16 * (st // 2))
            P.dma("sync", x_sb[par][:], c.x_ext[128 * st:128 * (st + 1), :],
                  c.ldx[par])
        def s1():
            P.wait("vector", c.ldx[par], 16 * (st // 2 + 1))
            if st >= 2:
                P.wait("vector", c.xw[par], 16 * (st // 2))
            P.op("vector", "reduce_sum", cl[:, 0:1], x_sb[par][:], AXX,
                 inc=c.s_vsy)
        def s2():
            P.vsy("vector", c.s_vsy)
            P.op("vector", "tensor_scalar", cl[:, 1:2], cl[:, 0:1],
                 -1.0 / D, None, ALU.mult, inc=c.s_vsy)
        def s3():
            P.vsy("vector", c.s_vsy)
            P.op("vector", "tensor_scalar", xc_sb[par][:], x_sb[par][:],
                 cl[:, 1:2], None, ALU.add, inc=c.s_ve)
            c.lnve[st] = c.s_ve.n
        def s4():
            P.wait("scalar", c.s_ve, c.lnve[st])
            P.op("scalar", "activation", sqr_sb[par][:], xc_sb[par][:],
                 AF.Square, inc=c.s_ac)
            c.lnac[st] = c.s_ac.n
        def s5():
            P.wait("vector", c.s_ac, c.lnac[st])
            P.op("vector", "reduce_sum", cl[:, 2:3], sqr_sb[par][:], AXX,
                 inc=c.s_vsy)
        def s6():
            P.vsy("vector", c.s_vsy)
            P.op("vector", "tensor_scalar", cl[:, 2:3], cl[:, 2:3],
                 1.0 / D, 1e-5, ALU.mult, ALU.add, inc=c.s_ve)
            c.lnve[st] = c.s_ve.n
        def s7():
            P.wait("scalar", c.s_ve, c.lnve[st])
            P.op("scalar", "activation", cl[:, 3:4], cl[:, 2:3], AF.Sqrt,
                 inc=c.s_ac)
            c.lnac[st] = c.s_ac.n
        def s8():
            P.wait("vector", c.s_ac, c.lnac[st])
            P.op("vector", "reciprocal", cl[:, 0:1], cl[:, 3:4], inc=c.s_vsy)
        def s9():
            P.vsy("vector", c.s_vsy)
            P.op("vector", "tensor_scalar", xnb[par][:], xc_sb[par][:],
                 cl[:, 0:1], None, ALU.mult, inc_drain=c.s_ve)
            c.lnve[st] = c.s_ve.n
        def s10():
            P.wait("gpsimd", c.s_ve, c.lnve[st])
            P.dma("gpsimd", c.xn_stash[128 * st:128 * (st + 1), :],
                  xnb[par][:], c.xw[par])
        return [s0, s1, s2, s3, s4, s5, s6, s7, s8, s9, s10]

    c.lnve = {}
    c.lnac = {}
    chains = [ln_steps(st) for st in range(NSTRIP)]
    for base in range(0, NSTRIP, 2):
        a, b2 = chains[base], chains[base + 1]
        for i in range(11):
            a[i]()
            b2[i]()

    P.wait("sync", c.xw[0], 16 * (NSTRIP // 2))
    P.wait("sync", c.xw[1], 16 * (NSTRIP // 2))
    for dt in range(NDT):
        c0, c1 = 128 * dt, 128 * (dt + 1)
        P.dma("sync", c.xnT_own[:, TOWN * dt:TOWN * (dt + 1)],
              c.xn_stash[HALO:HALO + TOWN, c0:c1], c.s_xt, transpose=True)
        P.dma("sync", c.xnT_halo[:, 512 * dt:512 * dt + 256],
              c.xn_stash[0:HALO, c0:c1], c.s_xt, transpose=True)
        P.dma("sync", c.xnT_halo[:, 512 * dt + 256:512 * dt + 512],
              c.xn_stash[TEXT - HALO:TEXT, c0:c1], c.s_xt, transpose=True)
    ph.close()


def phase_b1(c):
    P, nc, pb = c.P, c.nc, c.pb
    ph = contextlib.ExitStack()
    t = lambda n, s, d: ph.enter_context(nc.sbuf_tensor(n, s, d))
    ck_sb = t("ck_sb", [128, TEXT], F32)
    sk_sb = t("sk_sb", [128, TEXT], F32)
    wk_bf = t("wk_bf", [128, NDT * DH], BF16)
    wv_bf = t("wv_bf", [128, NDT * DH], BF16)
    kf_sb = t("kf_sb", [128, 512], F32)
    ksw_sb = t("ksw_sb", [128, 512], F32)
    kt1_sb = t("kt1_sb", [128, 512], F32)

    P.dma("sync", ck_sb[:], c.cosk, c.s_ld)
    P.dma("sync", sk_sb[:], c.sink, c.s_ld)
    P.dma("gpsimd", wk_bf[:], c.wkt, c.s_ld)
    P.dma("gpsimd", wv_bf[:], c.wvt, c.s_ld)
    P.wait("tensor", c.s_ld, c.s_ld.n)
    P.wait("vector", c.s_ld, c.s_ld.n)
    P.wait("scalar", c.s_ld, c.s_ld.n)
    P.wait("tensor", c.s_xt, c.s_xt.n)

    def ext_rhs_pieces(lo, hi):
        segs = []
        if lo < 256:
            e = min(hi, 256)
            segs.append(("halo", lo, e - lo))
            lo = e
        if lo < hi and lo < 1280:
            e = min(hi, 1280)
            segs.append(("own", lo - 256, e - lo))
            lo = e
        if lo < hi:
            segs.append(("halo", 256 + (lo - 1280), hi - lo))
        return segs

    for ch in range(3):
        segs = ext_rhs_pieces(512 * ch, 512 * (ch + 1))
        off = 0
        for kind, start, width in segs:
            for dt in range(NDT):
                if kind == "own":
                    rhs = c.xnT_own[:, TOWN * dt + start:TOWN * dt + start + width]
                else:
                    rhs = c.xnT_halo[:, 512 * dt + start:512 * dt + start + width]
                P.op("tensor", "matmul", pb[ch][:, off:off + width],
                     wk_bf[:, 128 * dt:128 * (dt + 1)], rhs,
                     start=(dt == 0), stop=(dt == NDT - 1),
                     inc=(c.s_mm if dt == NDT - 1 else None))
            off += width
        P.wait("vector", c.s_mm, c.s_mm.n)
        cc = ck_sb[:, 512 * ch:512 * (ch + 1)]
        scs = sk_sb[:, 512 * ch:512 * (ch + 1)]
        P.op("vector", "tensor_copy", kf_sb[:], pb[ch][:])
        P.op("vector", "tensor_copy", ksw_sb[0:64, :], kf_sb[64:128, :])
        P.op("vector", "tensor_copy", ksw_sb[64:128, :], kf_sb[0:64, :])
        P.op("vector", "tensor_mul", kt1_sb[:], kf_sb[:], cc)
        P.op("vector", "tensor_mul", ksw_sb[:], ksw_sb[:], scs)
        P.op("vector", "tensor_add", c.krT[:, 512 * ch:512 * (ch + 1)],
             kt1_sb[:], ksw_sb[:], inc=c.s_ve)
    k_done = c.s_ve.n

    for tt in range(12):
        lo = 128 * tt
        if lo < 256:
            def lhs_t(dt, lo=lo):
                return c.xnT_halo[:, 512 * dt + lo:512 * dt + lo + 128]
        elif lo < 1280:
            def lhs_t(dt, lo=lo):
                o = lo - 256
                return c.xnT_own[:, TOWN * dt + o:TOWN * dt + o + 128]
        else:
            def lhs_t(dt, lo=lo):
                o = 256 + lo - 1280
                return c.xnT_halo[:, 512 * dt + o:512 * dt + o + 128]
        bank = pb[3 + (tt % 2)]
        if tt >= 2:
            P.wait("tensor", c.s_ve, k_done + tt - 1)
        for dt in range(NDT):
            P.op("tensor", "matmul", bank[:, 0:128],
                 lhs_t(dt), wv_bf[:, 128 * dt:128 * (dt + 1)],
                 start=(dt == 0), stop=(dt == NDT - 1),
                 inc=(c.s_mm if dt == NDT - 1 else None))
        P.wait("vector", c.s_mm, c.s_mm.n)
        P.op("vector", "tensor_copy",
             c.v_aug[:, 132 * tt:132 * tt + 128], bank[:, 0:128])
        P.op("vector", "memset",
             c.v_aug[:, 132 * tt + 128:132 * tt + 129], 1.0, inc=c.s_ve)
    ph.close()


WIN_Q = [(0, 256), (0, 512), (256, 768), (512, 1024), (768, 1024)]
WIN_SLOT0 = [2, 0, 0, 0, 0]


def phase_b2(c, attn_outT):
    P, nc, pb = c.P, c.nc, c.pb
    ph = contextlib.ExitStack()
    t = lambda n, s, d: ph.enter_context(nc.sbuf_tensor(n, s, d))
    wq_bf = [t(f"wq_bf{i}", [128, NDT * 128], BF16) for i in range(3)]
    qf_sb = t("qf_sb", [128, TOWN], F32)
    qsw_sb = t("qsw_sb", [128, TOWN], F32)
    qt1_sb = t("qt1_sb", [128, TOWN], F32)
    qrT = [t("qrT0", [128, TOWN], BF16), t("qrT1", [128, TOWN], BF16)]
    attnT = t("attnT", [128, 2048], BF16)
    wbuf = [t("wb0", [128, 4 * 132], F32), t("wb1", [128, 4 * 132], F32)]
    bcol = t("bcol", [128, 4], F32)
    blf = [t("blf0", [128, 2 * 128], F32), t("blf1", [128, 2 * 128], F32)]

    sig_done = {k: 0 for k in range(4)}
    av_bank_evac = {k: 0 for k in range(4)}
    tr_bank_evac = {0: 0, 1: 0}
    blend_prog = 0
    av_mm_done = 0
    gq_val = {}
    rope_done = {}
    qmm_done = {}

    def q_chunks(h):
        par = h % 3
        chunks = []
        for cb in range(8):
            def f(cb=cb, par=par, h=h):
                if cb == 0:
                    P.wait("tensor", c.gq[par], gq_val[h])
                    if h >= 1:
                        P.wait("tensor", c.s_ve, rope_done[h - 1])
                for dt in range(4 * cb, 4 * cb + 4):
                    lhs = wq_bf[par][:, 128 * dt:128 * (dt + 1)]
                    for th in range(2):
                        P.op("tensor", "matmul", pb[4 + th][:, 0:512],
                             lhs,
                             c.xnT_own[:, TOWN * dt + 512 * th:
                                       TOWN * dt + 512 * th + 512],
                             start=(dt == 0), stop=(dt == NDT - 1),
                             inc=(c.s_mm if dt == NDT - 1 and th == 1
                                  else None))
                if cb == 7:
                    qmm_done[h] = c.s_mm.n
            chunks.append(f)
        return chunks

    def emit_rope(h):
        par = h % 2
        P.wait("vector", c.s_mm, qmm_done[h])
        P.op("vector", "tensor_copy", qf_sb[:, 0:512], pb[4][:])
        P.op("vector", "tensor_copy", qf_sb[:, 512:1024], pb[5][:])
        P.op("vector", "tensor_copy", qsw_sb[0:64, :], qf_sb[64:128, :])
        P.op("vector", "tensor_copy", qsw_sb[64:128, :], qf_sb[0:64, :])
        P.op("vector", "tensor_mul", qt1_sb[:], qf_sb[:], c.cq_sb[:])
        P.op("vector", "tensor_mul", qsw_sb[:], qsw_sb[:], c.sq_sb[:])
        P.op("vector", "tensor_add", qrT[par][:], qt1_sb[:], qsw_sb[:],
             inc=c.s_ve)
        rope_done[h] = c.s_ve.n

    for hh in range(3):
        P.dma("gpsimd", wq_bf[hh][:], c.wqt[128 * hh:128 * (hh + 1), :],
              c.gq[hh])
        gq_val[hh] = c.gq[hh].n
    for f in q_chunks(0):
        f()
    emit_rope(0)

    for h in range(H):
        par = h % 3
        pump = q_chunks(h + 1) if h + 1 < H else []
        pump_i = [0]
        bl_ve = {}

        def do_pump(n=1):
            for _ in range(n):
                if pump_i[0] < len(pump):
                    pump[pump_i[0]]()
                    pump_i[0] += 1

        if h + 3 < H:
            P.wait("gpsimd", c.s_mm, qmm_done[h])
            P.dma("gpsimd", wq_bf[par][:],
                  c.wqt[128 * (h + 3):128 * (h + 4), :], c.gq[par])
            gq_val[h + 3] = c.gq[par].n

        for w in range(5):
            qlo, qhi = WIN_Q[w]
            qn = qhi - qlo
            wslot = wbuf[w % 2]
            for kt in range(4):
                if kt == 0 and w == 0:
                    P.wait("tensor", c.s_ve, rope_done[h])
                P.wait("tensor", c.s_ac, av_bank_evac[kt])
                P.wait("tensor", c.s_ac, sig_done[kt])
                P.op("tensor", "matmul", pb[kt][:, 0:qn],
                     c.krT[:, 256 * w + 128 * kt:256 * w + 128 * (kt + 1)],
                     qrT[h % 2][:, qlo:qhi], start=True, stop=True,
                     inc=c.s_mm)
            sc_done = c.s_mm.n
            do_pump(1)
            for kt in range(4):
                P.wait("scalar", c.s_mm, sc_done - 3 + kt)
                P.wait("scalar", c.s_mm, av_mm_done)
                P.wait("scalar", c.s_ve, blend_prog)
                P.op("scalar", "activation", attnT[:, 512 * kt:512 * kt + qn],
                     pb[kt][:, 0:qn], AF.Sigmoid,
                     bias=c.msk_sb[:, 4 * w + kt:4 * w + kt + 1],
                     scale=SCALE, inc=c.s_ac)
                sig_done[kt] = c.s_ac.n
            nq = qn // 128
            for qs in range(nq):
                bank = pb[qs]
                P.wait("tensor", c.s_ac, sig_done[3])
                for kt in range(4):
                    P.op("tensor", "matmul", bank[:, 0:129],
                         attnT[:, 512 * kt + 128 * qs:512 * kt + 128 * (qs + 1)],
                         c.v_aug[:, 132 * (2 * w + kt):132 * (2 * w + kt) + 129],
                         start=(kt == 0), stop=(kt == 3),
                         inc=(c.s_mm if kt == 3 else None))
                av_mm_done = c.s_mm.n
                do_pump(1)
                P.wait("scalar", c.s_mm, av_mm_done)
                slot = WIN_SLOT0[w] + qs
                P.op("scalar", "copy", wslot[:, 132 * slot:132 * slot + 129],
                     bank[:, 0:129], inc=c.s_ac)
                av_bank_evac[qs] = c.s_ac.n

            if w == 2 and h + 1 < H:
                emit_rope(h + 1)

            def emit_blend(k):
                A = wbuf[k % 2]
                B = wbuf[(k + 1) % 2]
                for qs in range(2):
                    aslot = 2 + qs
                    bslot = qs
                    col = 2 * k + qs
                    P.wait("vector", c.s_ac, av_bank_evac[bslot])
                    P.op("vector", "reciprocal", bcol[:, 0:1],
                         A[:, 132 * aslot + 128:132 * aslot + 129])
                    P.op("vector", "reciprocal", bcol[:, 1:2],
                         B[:, 132 * bslot + 128:132 * bslot + 129],
                         inc=c.s_vsy)
                    P.vsy("vector", c.s_vsy)
                    P.op("vector", "tensor_mul", bcol[:, 2:3],
                         bcol[:, 0:1], c.one_sb[:, col:col + 1])
                    P.op("vector", "tensor_mul", bcol[:, 3:4],
                         bcol[:, 1:2], c.alp_sb[:, col:col + 1], inc=c.s_vsy)
                    P.vsy("vector", c.s_vsy)
                    bl = blf[k % 2][:, 128 * qs:128 * (qs + 1)]
                    P.op("vector", "tensor_scalar", bl,
                         A[:, 132 * aslot:132 * aslot + 128],
                         bcol[:, 2:3], None, ALU.mult)
                    P.op("vector", "scalar_tensor_tensor", bl,
                         B[:, 132 * bslot:132 * bslot + 128],
                         bcol[:, 3:4], bl, ALU.mult, ALU.add, inc=c.s_ve)
                return c.s_ve.n

            def emit_transpose(k, bl_ve):
                for qs in range(2):
                    col = 2 * k + qs
                    tbank = 6 + (qs % 2)
                    P.wait("tensor", c.s_ve, bl_ve)
                    P.wait("tensor", c.s_ve, tr_bank_evac[qs % 2])
                    P.op("tensor", "transpose", pb[tbank][:, 0:128],
                         blf[k % 2][:, 128 * qs:128 * (qs + 1)],
                         c.idf_sb[:], inc=c.s_mm)
                    P.wait("vector", c.s_mm, c.s_mm.n)
                    P.op("vector", "tensor_copy",
                         attn_outT[:, TOWN * h + 128 * col:
                                   TOWN * h + 128 * (col + 1)],
                         pb[tbank][:, 0:128], inc=c.s_ve)
                    tr_bank_evac[qs % 2] = c.s_ve.n

            if w >= 1:
                bl_ve[w - 1] = emit_blend(w - 1)
                blend_prog = c.s_ve.n
            if w >= 2:
                emit_transpose(w - 2, bl_ve[w - 2])
        emit_transpose(3, bl_ve[3])
        do_pump(16)
    ph.close()


def phase_b3(c, attn_outT, dq):
    P, nc, pb = c.P, c.nc, c.pb
    ph = contextlib.ExitStack()
    t = lambda n, s, d: ph.enter_context(nc.sbuf_tensor(n, s, d))
    wo_bf = [t(f"wo_bf{i}", [128, NDT * 128], BF16) for i in range(3)]
    pr_sb = [t("pr_sb0", [128, TOWN], F32), t("pr_sb1", [128, TOWN], F32)]
    wo_mm = {}
    gq_val = {}
    for hh in range(3):
        P.dma("gpsimd", wo_bf[hh][:], c.wot[128 * hh:128 * (hh + 1), :],
              c.gq[hh])
        gq_val[hh] = c.gq[hh].n
    for i in range(NDT):
        par = i % 2
        wpar = i % 3
        P.wait("tensor", c.gq[wpar], gq_val[i])
        if i >= 2:
            P.wait("tensor", c.pj[par], 16 * (i // 2))
        for ot in range(NDT):
            lhs = wo_bf[wpar][:, 128 * ot:128 * (ot + 1)]
            for th in range(2):
                P.op("tensor", "matmul", pb[2 * (i % 2) + th][:, 0:512],
                     lhs,
                     attn_outT[:, TOWN * ot + 512 * th:
                               TOWN * ot + 512 * (th + 1)],
                     start=(ot == 0), stop=(ot == NDT - 1),
                     inc=(c.s_mm if ot == NDT - 1 and th == 1 else None))
        wo_mm[i] = c.s_mm.n
        if i + 3 < NDT:
            P.wait("gpsimd", c.s_mm, wo_mm[i])
            P.dma("gpsimd", wo_bf[wpar][:],
                  c.wot[128 * (i + 3):128 * (i + 4), :], c.gq[wpar])
            gq_val[i + 3] = c.gq[wpar].n
        P.wait("vector", c.s_mm, c.s_mm.n)
        if i >= 2:
            P.wait("vector", c.pj[par], 16 * (i // 2))
        P.op("vector", "tensor_copy", pr_sb[par][:, 0:512], pb[2 * (i % 2)][:])
        P.op("vector", "tensor_copy", pr_sb[par][:, 512:1024],
             pb[2 * (i % 2) + 1][:], inc_drain=c.s_ve)
        P.wait("gpsimd", c.s_ve, c.s_ve.n)
        P.dma("gpsimd", c.projT_stash[128 * i:128 * (i + 1), :], pr_sb[par][:],
              c.pj[par])
        if i < 16:
            dq.load(i)
        if i % 2 == 1 and i // 2 < 12:
            dq.compute(i // 2)
    ph.close()


class DqGU:

    def __init__(self, c, scope):
        nc = c.nc
        t = lambda n, s, d: scope.enter_context(
            nc.sbuf_tensor(n, s, d, side="right"))
        self.qin = [t(f"qin{i}", [128, D], U8) for i in range(4)]
        self.qo = [t(f"qo{i}", [128, D], BF16) for i in range(2)]
        self.zin = t("zin", [128, 4 * 32], U8)
        self.sin_ = t("sin_", [128, 4 * 32], F32)
        self.zf = t("zf", [128, 4 * 32], F32)
        self.c = c
        self.ld = {}

    def load(self, q):
        c, P = self.c, self.c.P
        jj, kind = q // 2, q % 2
        lane = q % 4
        qw = c.qg if kind == 0 else c.qu
        zt = c.zg if kind == 0 else c.zu
        st_ = c.sg if kind == 0 else c.su
        r0, r1 = 128 * jj, 128 * (jj + 1)
        if q >= 4:
            P.wait("scalar", c.swl[q % 2], 16 * (q // 2 - 1))
        P.dma("scalar", self.qin[lane][:], qw[r0:r1, :], c.lq[lane])
        P.dma("scalar", self.zin[:, 32 * lane:32 * (lane + 1)], zt[r0:r1, :],
              c.lq[lane])
        P.dma("scalar", self.sin_[:, 32 * lane:32 * (lane + 1)], st_[r0:r1, :],
              c.lq[lane])
        self.ld[q] = c.lq[lane].n

    def compute(self, q):
        c, P = self.c, self.c.P
        jj, kind = q // 2, q % 2
        lane = q % 4
        olane = q % 2
        s0 = 256 * jj + 128 * kind
        P.wait("vector", c.lq[lane], self.ld[q])
        if q >= 2:
            P.wait("vector", c.swl[olane], 16 * (q // 2))
        P.op("vector", "tensor_copy", self.zf[:, 32 * lane:32 * (lane + 1)],
             self.zin[:, 32 * lane:32 * (lane + 1)], inc=c.s_vsy)
        P.vsy("vector", c.s_vsy)
        for g in range(NG_IN):
            csl = slice(128 * g, 128 * (g + 1))
            zc = slice(32 * lane + g, 32 * lane + g + 1)
            P.op("vector", "tensor_scalar", self.qo[olane][:, csl],
                 self.qin[lane][:, csl], self.zf[:, zc], self.sin_[:, zc],
                 ALU.subtract, ALU.mult,
                 inc_drain=(c.s_ve if g == NG_IN - 1 else None))
        P.wait("gpsimd", c.s_ve, c.s_ve.n)
        P.dma("gpsimd", c.gu_stash[s0:s0 + 128, :], self.qo[olane][:],
              c.swl[olane])


class DqDown:

    def __init__(self, c, scope):
        nc = c.nc
        HH = HFF // 2
        t = lambda n, s, d: scope.enter_context(nc.sbuf_tensor(n, s, d))
        self.qind = [t(f"qind{i}", [128, HH], U8) for i in range(2)]
        self.qodd = [t(f"qodd{i}", [128, HH], BF16) for i in range(2)]
        self.zind = t("zind", [128, 2 * NG_OUT], U8)
        self.sind = t("sind", [128, 2 * NG_OUT], F32)
        self.zfd = t("zfd", [128, 2 * NG_OUT], F32)
        self.c = c
        self.ld = {}

    def load(self, dj):
        c, P = self.c, self.c.P
        HH = HFF // 2
        dt, half = dj // 2, dj % 2
        lane = dj % 2
        r0, r1 = 128 * dt, 128 * (dt + 1)
        h0 = HH * half
        if dj >= 2:
            P.wait("scalar", c.swd[lane], 16 * (dj // 2))
        P.dma("scalar", self.zind[:, NG_OUT * lane:NG_OUT * (lane + 1)],
              c.zd[r0:r1, :], c.ldq[lane])
        P.dma("scalar", self.sind[:, NG_OUT * lane:NG_OUT * (lane + 1)],
              c.sd[r0:r1, :], c.ldq[lane])
        P.dma("scalar", self.qind[lane][:], c.qd[r0:r1, h0:h0 + HH],
              c.ldq[lane])
        self.ld[dj] = c.ldq[lane].n

    def compute(self, dj):
        c, P = self.c, self.c.P
        HH = HFF // 2
        NGH = NG_OUT // 2
        dt, half = dj // 2, dj % 2
        lane = dj % 2
        r0, r1 = 128 * dt, 128 * (dt + 1)
        h0 = HH * half
        P.wait("vector", c.ldq[lane], self.ld[dj])
        P.op("vector", "tensor_copy",
             self.zfd[:, NG_OUT * lane:NG_OUT * (lane + 1)],
             self.zind[:, NG_OUT * lane:NG_OUT * (lane + 1)], inc=c.s_vsy)
        P.vsy("vector", c.s_vsy)
        for gg in range(NGH):
            g = NGH * half + gg
            csl = slice(128 * gg, 128 * (gg + 1))
            zc = slice(NG_OUT * lane + g, NG_OUT * lane + g + 1)
            P.op("vector", "tensor_scalar", self.qodd[lane][:, csl],
                 self.qind[lane][:, csl], self.zfd[:, zc], self.sind[:, zc],
                 ALU.subtract, ALU.mult,
                 inc_drain=(c.s_ve if gg == NGH - 1 else None))
        P.wait("gpsimd", c.s_ve, c.s_ve.n)
        P.dma("gpsimd", c.wd_stash[r0:r1, h0:h0 + HH], self.qodd[lane][:],
              c.swd[lane])


def phase_c1(c, dq, dqd):
    P, nc, pb = c.P, c.nc, c.pb
    ph = contextlib.ExitStack()
    t = lambda n, s, d: ph.enter_context(nc.sbuf_tensor(n, s, d))
    pan = [t("pan0", [128, NDT * 512], BF16),
           t("pan1", [128, NDT * 512], BF16)]
    sg_sb = [t("sg_sb0", [128, TOWN], BF16), t("sg_sb1", [128, TOWN], BF16)]
    h_sb = [t("h_sb0", [128, TOWN], BF16), t("h_sb1", [128, TOWN], BF16)]
    NPAN = NHT // 2
    pan_ld = {}
    gu_mm = {}
    h_ve = {}
    NGU = 2 * NHT
    NDJ = 2 * NDT
    for dj in range(2):
        dqd.load(dj)
    for J in range(NPAN):
        ppar = J % 2
        for q in range(4 * J + 12, min(4 * J + 16, NGU)):
            dq.compute(q)
        for dj in range(2 * J, min(2 * J + 2, NDJ)):
            dqd.compute(dj)
        if J >= 2:
            P.wait("sync", c.s_mm, gu_mm[2 * (J - 2) + 1])
        P.wait("sync", c.swl[0], 16 * min(2 * J + 2, NHT))
        P.wait("sync", c.swl[1], 16 * min(2 * J + 2, NHT))
        for dt in range(NDT):
            P.dma("sync", pan[ppar][:, 512 * dt:512 * (dt + 1)],
                  c.gu_stash[512 * J:512 * (J + 1), 128 * dt:128 * (dt + 1)],
                  c.pan[ppar], transpose=True)
        pan_ld[J] = c.pan[ppar].n
        for r in range(2):
            jj = 2 * J + r
            jpar = jj % 2
            P.wait("tensor", c.pan[ppar], pan_ld[J])
            if jj >= 2:
                P.wait("tensor", c.s_ve, h_ve[jj - 2])
            for dt in range(NDT):
                glhs = pan[ppar][:, 512 * dt + 256 * r:
                                512 * dt + 256 * r + 128]
                ulhs = pan[ppar][:, 512 * dt + 256 * r + 128:
                                512 * dt + 256 * r + 256]
                for th in range(2):
                    P.op("tensor", "matmul", pb[4 * jpar + th][:, 0:512],
                         glhs,
                         c.xnT_own[:, TOWN * dt + 512 * th:
                                   TOWN * dt + 512 * (th + 1)],
                         start=(dt == 0), stop=(dt == NDT - 1),
                         inc=(c.s_mm if dt == NDT - 1 and th == 1 else None))
                for th in range(2):
                    P.op("tensor", "matmul",
                         pb[4 * jpar + 2 + th][:, 0:512],
                         ulhs,
                         c.xnT_own[:, TOWN * dt + 512 * th:
                                   TOWN * dt + 512 * (th + 1)],
                         start=(dt == 0), stop=(dt == NDT - 1),
                         inc=(c.s_mm if dt == NDT - 1 and th == 1 else None))
            gu_mm[jj] = c.s_mm.n
            P.wait("scalar", c.s_mm, gu_mm[jj] - 1)
            if jj >= 2:
                P.wait("scalar", c.s_ve, h_ve[jj - 2])
            P.op("scalar", "activation", sg_sb[jpar][:, 0:512],
                 pb[4 * jpar + 0][:], AF.Silu)
            P.op("scalar", "activation", sg_sb[jpar][:, 512:1024],
                 pb[4 * jpar + 1][:], AF.Silu, inc=c.s_ac)
            P.wait("vector", c.s_ac, c.s_ac.n)
            P.wait("vector", c.s_mm, gu_mm[jj])
            if jj >= 2:
                P.wait("vector", c.hw[jpar], 16 * (jj // 2))
            P.op("vector", "tensor_mul", h_sb[jpar][:, 0:512],
                 sg_sb[jpar][:, 0:512], pb[4 * jpar + 2][:])
            P.op("vector", "tensor_mul", h_sb[jpar][:, 512:1024],
                 sg_sb[jpar][:, 512:1024], pb[4 * jpar + 3][:],
                 inc_drain=c.s_ve)
            h_ve[jj] = c.s_ve.n
            P.wait("gpsimd", c.s_ve, c.s_ve.n)
            P.dma("gpsimd", c.h_stash[128 * jj:128 * (jj + 1), :],
                  h_sb[jpar][:], c.hw[jpar])
        for q in range(4 * J + 16, min(4 * J + 20, NGU)):
            dq.load(q)
        for dj in range(2 * J + 2, min(2 * J + 4, NDJ)):
            dqd.load(dj)
    ph.close()


def phase_c2(c):
    P, nc, pb = c.P, c.nc, c.pb
    ph = contextlib.ExitStack()
    t = lambda n, s, d: ph.enter_context(nc.sbuf_tensor(n, s, d))
    wdt = [t(f"wdt{i}", [128, 512], BF16) for i in range(4)]
    hst = [t(f"hst{i}", [128, TOWN], BF16) for i in range(4)]
    prt = t("prt", [128, 4 * TOWN], F32)
    smt = [t("smt0", [128, TOWN], F32), t("smt1", [128, TOWN], F32)]
    for s in (c.hw[0], c.hw[1], c.pj[0], c.pj[1], c.swd[0], c.swd[1]):
        P.wait("sync", s, s.n)
        P.wait("gpsimd", s, s.n)
    dn_mm = {}
    sum_ve = 0
    for ch in range(8):
        P.wait("sync", c.s_ve, sum_ve)
        for ic in range(4):
            i = 4 * ch + ic
            P.dma("sync", prt[:, TOWN * ic:TOWN * (ic + 1)],
                  c.projT_stash[128 * i:128 * (i + 1), :], c.spr)
        prj_ld = c.spr.n
        for jj in range(NHT):
            lane = jj % 4
            prev = (ch, jj - 4) if jj >= 4 else (ch - 1, 82 + (jj + 2) % 4)
            if prev in dn_mm:
                P.wait("sync", c.s_mm, dn_mm[prev])
                P.wait("gpsimd", c.s_mm, dn_mm[prev])
            P.dma("sync", wdt[lane][:],
                  c.wd_stash[512 * ch:512 * (ch + 1),
                             128 * jj:128 * (jj + 1)], c.wd[lane],
                  transpose=True)
            P.dma("gpsimd", hst[lane][:],
                  c.h_stash[128 * jj:128 * (jj + 1), :], c.hh[lane])
            P.wait("tensor", c.wd[lane], c.wd[lane].n)
            P.wait("tensor", c.hh[lane], c.hh[lane].n)
            if jj == 0 and ch > 0:
                P.wait("tensor", c.s_ve, sum_ve)
            for ic in range(4):
                for th in range(2):
                    P.op("tensor", "matmul", pb[2 * ic + th][:, 0:512],
                         wdt[lane][:, 128 * ic:128 * (ic + 1)],
                         hst[lane][:, 512 * th:512 * (th + 1)],
                         start=(jj == 0), stop=(jj == NHT - 1),
                         inc=(c.s_mm if (th == 1 and ic == 3) else None))
            dn_mm[(ch, jj)] = c.s_mm.n
        P.wait("vector", c.s_mm, dn_mm[(ch, NHT - 1)])
        P.wait("vector", c.spr, prj_ld)
        for ic in range(4):
            i = 4 * ch + ic
            k = 4 * ch + ic
            sl = ic % 2
            if k >= 2:
                P.wait("vector", c.sm[sl], 16 * (k // 2))
            for th in range(2):
                P.op("vector", "tensor_add",
                     smt[sl][:, 512 * th:512 * (th + 1)],
                     pb[2 * ic + th][:],
                     prt[:, TOWN * ic + 512 * th:TOWN * ic + 512 * (th + 1)],
                     inc_drain=(c.s_ve if th == 1 else None))
            P.wait("gpsimd", c.s_ve, c.s_ve.n)
            P.dma("gpsimd", c.sum_stash[128 * i:128 * (i + 1), :],
                  smt[sl][:], c.sm[sl])
        sum_ve = c.s_ve.n
    ph.close()


def phase_d(c):
    P, nc, pb = c.P, c.nc, c.pb
    ph = contextlib.ExitStack()
    t = lambda n, s, d: ph.enter_context(nc.sbuf_tensor(n, s, d))
    xs = [t("xs0", [128, D], F32), t("xs1", [128, D], F32)]
    stile = [t("stile0", [128, 128], F32), t("stile1", [128, 128], F32)]
    ostrip = [t("ostrip0", [128, D], F32), t("ostrip1", [128, D], F32)]
    for s in (c.sm[0], c.sm[1]):
        P.wait("sync", s, s.n)
    tr_mm = {0: 0, 1: 0}
    ev_ve = {0: 0, 1: 0}
    dx_val = {}
    for tt in range(8):
        par = tt % 2
        if tt >= 2:
            P.wait("sync", c.ou[par], 16 * (tt // 2))
        P.dma("sync", xs[par][:],
              c.x_ext[HALO + 128 * tt:HALO + 128 * (tt + 1), :], c.dx[par])
        dx_val[tt] = c.dx[par].n
        for i in range(NDT):
            sl = i % 2
            P.wait("sync", c.s_mm, tr_mm[sl])
            P.dma("sync", stile[sl][:],
                  c.sum_stash[128 * i:128 * (i + 1), 128 * tt:128 * (tt + 1)],
                  c.stl[sl])
            P.wait("tensor", c.stl[sl], c.stl[sl].n)
            P.wait("tensor", c.s_ve, ev_ve[sl])
            P.op("tensor", "transpose", pb[sl][:, 0:128], stile[sl][:],
                 c.idf_sb[:], inc=c.s_mm)
            tr_mm[sl] = c.s_mm.n
            P.wait("vector", c.s_mm, c.s_mm.n)
            if i == 0:
                P.wait("vector", c.dx[par], dx_val[tt])
                if tt >= 2:
                    P.wait("vector", c.ou[par], 16 * (tt // 2))
            P.op("vector", "tensor_add",
                 ostrip[par][:, 128 * i:128 * (i + 1)],
                 pb[sl][:, 0:128], xs[par][:, 128 * i:128 * (i + 1)],
                 **({"inc_drain": c.s_ve} if i == NDT - 1 else {"inc": c.s_ve}))
            ev_ve[sl] = c.s_ve.n
        P.wait("gpsimd", c.s_ve, c.s_ve.n)
        P.dma("gpsimd", c.out[128 * tt:128 * (tt + 1), :], ostrip[par][:],
              c.ou[par])
    P.wait("gpsimd", c.ou[0], c.ou[0].n)
    P.wait("gpsimd", c.ou[1], c.ou[1].n)
    ph.close()


def build_program():
    c = Ctx()
    c.nc = bass.Bass("TRN2", target_bir_lowering=False, debug=False)
    c.P = Plan()
    c.stack = contextlib.ExitStack()
    c.sb = lambda n, s, d: c.stack.enter_context(c.nc.sbuf_tensor(n, s, d))
    _mk(c)

    phase_a(c)
    barrier(c)
    phase_b1(c)
    c.g_halo.__exit__(None, None, None)
    barrier(c)
    g_attn = c.nc.sbuf_tensor("attn_outT", [128, H * TOWN], BF16)
    attn_outT = g_attn.__enter__()
    phase_b2(c, attn_outT)
    c.bstack.close()
    barrier(c)
    dq_scope = contextlib.ExitStack()
    dq = DqGU(c, dq_scope)
    phase_b3(c, attn_outT, dq)
    g_attn.__exit__(None, None, None)
    barrier(c)
    dqd_scope = contextlib.ExitStack()
    dqd = DqDown(c, dqd_scope)
    phase_c1(c, dq, dqd)
    dq_scope.close()
    dqd_scope.close()
    barrier(c)
    phase_c2(c)
    barrier(c)
    phase_d(c)

    with c.nc.Block() as block:
        @block.sync
        def _(e):
            for fn in c.P.q["sync"]:
                fn(e)

        @block.tensor
        def _(e):
            for fn in c.P.q["tensor"]:
                fn(e)

        @block.vector
        def _(e):
            for fn in c.P.q["vector"]:
                fn(e)

        @block.scalar
        def _(e):
            for fn in c.P.q["scalar"]:
                fn(e)

        @block.gpsimd
        def _(e):
            for fn in c.P.q["gpsimd"]:
                fn(e)

    c.stack.close()
    return c.nc


def _rope_tables():
    inv_freq = 1.0 / 10000.0 ** (np.arange(0, DH, 2, dtype=np.float32) / DH)
    t = np.arange(S, dtype=np.float32)
    f = np.outer(t, inv_freq)
    cos = np.zeros((S, DH), dtype=np.float32)
    sin = np.zeros((S, DH), dtype=np.float32)
    cos[:, 0::2] = np.cos(f)
    cos[:, 1::2] = np.cos(f)
    sin[:, 0::2] = np.sin(f)
    sin[:, 1::2] = np.sin(f)
    sinS = sin.copy()
    sinS[:, :DH // 2] *= -1.0
    return cos, sinS


def _u8view(a_int32):
    a = np.ascontiguousarray(a_int32)
    return np.ascontiguousarray(a.view(np.uint8)[:, ::4])


def kernel(L0_attn_x, L0_attn_Wq, L0_attn_Wk, L0_attn_Wv, L0_attn_Wo,
           L0_ffn_qw_gate, L0_ffn_z_gate, L0_ffn_s_gate,
           L0_ffn_qw_up, L0_ffn_z_up, L0_ffn_s_up,
           L0_ffn_qw_down, L0_ffn_z_down, L0_ffn_s_down):
    x = np.ascontiguousarray(
        np.asarray(L0_attn_x, dtype=np.float32).reshape(S, D))
    Wq = np.asarray(L0_attn_Wq, dtype=np.float32)
    Wk = np.asarray(L0_attn_Wk, dtype=np.float32)
    Wv = np.asarray(L0_attn_Wv, dtype=np.float32)
    Wo = np.asarray(L0_attn_Wo, dtype=np.float32)
    wqt = np.ascontiguousarray(
        Wq.reshape(H, 128, NDT, 128).transpose(0, 3, 2, 1)).reshape(H * 128, -1)
    wot = np.ascontiguousarray(
        Wo.reshape(NDT, 128, NDT, 128).transpose(0, 3, 2, 1)).reshape(D, -1)
    wkt = np.ascontiguousarray(
        Wk.reshape(DH, NDT, 128).transpose(2, 1, 0)).reshape(128, -1)
    wvt = np.ascontiguousarray(
        Wv.reshape(DH, NDT, 128).transpose(2, 1, 0)).reshape(128, -1)
    qgT = _u8view(np.ascontiguousarray(np.asarray(L0_ffn_qw_gate).T))
    quT = _u8view(np.ascontiguousarray(np.asarray(L0_ffn_qw_up).T))
    qdT = _u8view(np.ascontiguousarray(np.asarray(L0_ffn_qw_down).T))
    zgT = _u8view(np.ascontiguousarray(np.asarray(L0_ffn_z_gate).T))
    sgT = np.ascontiguousarray(np.asarray(L0_ffn_s_gate, dtype=np.float32).T)
    zuT = _u8view(np.ascontiguousarray(np.asarray(L0_ffn_z_up).T))
    suT = np.ascontiguousarray(np.asarray(L0_ffn_s_up, dtype=np.float32).T)
    zdT = _u8view(np.ascontiguousarray(np.asarray(L0_ffn_z_down).T))
    sdT = np.ascontiguousarray(np.asarray(L0_ffn_s_down, dtype=np.float32).T)

    cos, sinS = _rope_tables()
    alpha = np.linspace(0.0, 1.0, 256, dtype=np.float32)
    idf = np.eye(128, dtype=np.float32)

    in_maps = []
    for cid in range(NCORE):
        q0 = TOWN * cid
        lo, hi = q0 - HALO, q0 + TOWN + HALO
        xe = np.zeros((TEXT, D), dtype=np.float32)
        slo, shi = max(lo, 0), min(hi, S)
        xe[slo - lo:shi - lo, :] = x[slo:shi]
        cq = np.ascontiguousarray(cos[q0:q0 + TOWN].T)
        sq = np.ascontiguousarray(sinS[q0:q0 + TOWN].T)
        ck = np.zeros((128, TEXT), dtype=np.float32)
        sk = np.zeros((128, TEXT), dtype=np.float32)
        ck[:, slo - lo:shi - lo] = cos[slo:shi].T
        sk[:, slo - lo:shi - lo] = sinS[slo:shi].T
        alc = np.empty((128, 8), dtype=np.float32)
        onc = np.empty((128, 8), dtype=np.float32)
        for j in range(8):
            a = alpha[0:128] if j % 2 == 0 else alpha[128:256]
            if cid == 0 and j < 2:
                alc[:, j] = 1.0
                onc[:, j] = 0.0
            else:
                alc[:, j] = a
                onc[:, j] = 1.0 - a
        mkc = np.zeros((128, 20), dtype=np.float32)
        p = np.arange(128)
        for w in range(5):
            for kt in range(4):
                kp = q0 - HALO + 256 * w + 128 * kt + p
                mkc[:, 4 * w + kt] = np.where((kp >= 0) & (kp < S), 0.0, NEG)
        in_maps.append(dict(
            x_ext=xe, wqt=wqt, wot=wot, wkt=wkt, wvt=wvt,
            qg=qgT, qu=quT, qd=qdT, zg=zgT, sg=sgT, zu=zuT, su=suT,
            zd=zdT, sd=sdT,
            cosq=cq, sinq=sq, cosk=ck, sink=sk,
            alpc=alc, onec=onc, maskc=mkc, idf=idf,
        ))

    nc = build_program()
    res = run_bass_kernel_spmd(nc, in_maps, core_ids=list(range(NCORE)))
    outs = [np.asarray(res.results[cid]["out"], dtype=np.float32)
            for cid in range(NCORE)]
    return np.concatenate(outs, axis=0).reshape(1, S, D)


# version 22
# speedup vs baseline: 1.1441x; 1.0122x over previous
import contextlib
import numpy as np
import ml_dtypes
import concourse.bass as bass
import concourse.mybir as mybir
from concourse.bass_utils import run_bass_kernel_spmd

F32 = mybir.dt.float32
BF16 = mybir.dt.bfloat16
U8 = mybir.dt.uint8
AF = mybir.ActivationFunctionType
ALU = mybir.AluOpType
AXX = mybir.AxisListType.X

S = 8192
D = 4096
H = 32
DH = 128
HFF = 11008
NCORE = 8
TOWN = S // NCORE
HALO = 256
TEXT = TOWN + 2 * HALO
NDT = D // 128
NHT = HFF // 128
NG_IN = 32
NG_OUT = 86
SCALE = 1.0 / float(np.sqrt(DH))
NEG = -1e9

ENGS = ("sync", "tensor", "vector", "scalar", "gpsimd")


class Sem:
    def __init__(self, h):
        self.h = h
        self.n = 0


class Plan:
    def __init__(self):
        self.q = {e: [] for e in ENGS}

    def emit(self, eng, fn):
        self.q[eng].append(fn)

    def wait(self, eng, sem, val):
        if val > 0:
            self.emit(eng, lambda e, s=sem.h, v=val: e.wait_ge(s, v))

    def dma(self, eng, out, in_, sem, transpose=False):
        def f(e, o=out, i=in_, s=sem.h, t=transpose):
            e.dma_start(out=o, in_=i, transpose=t).then_inc(s, 16)
        self.emit(eng, f)
        sem.n += 16
        return sem.n

    def vsy(self, eng, sem):
        self.wait(eng, sem, sem.n)

    def op(self, eng, meth, *args, inc=None, inc_drain=None, **kw):
        def f(e, m=meth, a=args, k=kw, i=inc, d=inc_drain):
            ins = getattr(e, m)(*a, **k)
            if i is not None:
                ins.then_inc(i.h, 1)
            if d is not None:
                e.drain().then_inc(d.h, 1)
        self.emit(eng, f)
        if inc is not None:
            inc.n += 1
        if inc_drain is not None:
            inc_drain.n += 1
        if inc is not None or inc_drain is not None:
            return (inc or inc_drain).n


class Ctx:
    pass


DBG_OUT = ()


def _mk(c: Ctx):
    nc = c.nc
    ap = lambda *a, **k: nc.dram_tensor(*a, **k).ap()
    c.x_ext = ap("x_ext", [TEXT, D], F32, kind="ExternalInput")
    c.wqt = ap("wqt", [H * 128, NDT * 128], F32, kind="ExternalInput")
    c.wot = ap("wot", [NDT * 128, NDT * 128], F32, kind="ExternalInput")
    c.wkt = ap("wkt", [128, NDT * DH], F32, kind="ExternalInput")
    c.wvt = ap("wvt", [128, NDT * DH], F32, kind="ExternalInput")
    c.qg = ap("qg", [HFF, D], U8, kind="ExternalInput")
    c.qu = ap("qu", [HFF, D], U8, kind="ExternalInput")
    c.qd = ap("qd", [D, HFF], U8, kind="ExternalInput")
    c.zg = ap("zg", [HFF, NG_IN], U8, kind="ExternalInput")
    c.sg = ap("sg", [HFF, NG_IN], F32, kind="ExternalInput")
    c.zu = ap("zu", [HFF, NG_IN], U8, kind="ExternalInput")
    c.su = ap("su", [HFF, NG_IN], F32, kind="ExternalInput")
    c.zd = ap("zd", [D, NG_OUT], U8, kind="ExternalInput")
    c.sd = ap("sd", [D, NG_OUT], F32, kind="ExternalInput")
    c.cosq = ap("cosq", [128, TOWN], F32, kind="ExternalInput")
    c.sinq = ap("sinq", [128, TOWN], F32, kind="ExternalInput")
    c.cosk = ap("cosk", [128, TEXT], F32, kind="ExternalInput")
    c.sink = ap("sink", [128, TEXT], F32, kind="ExternalInput")
    c.alpc = ap("alpc", [128, 8], F32, kind="ExternalInput")
    c.onec = ap("onec", [128, 8], F32, kind="ExternalInput")
    c.maskc = ap("maskc", [128, 20], F32, kind="ExternalInput")
    c.idf = ap("idf", [128, 128], F32, kind="ExternalInput")
    c.out = ap("out", [TOWN, D], F32, kind="ExternalOutput")
    kk = lambda n: "ExternalOutput" if n in DBG_OUT else "Internal"
    c.xn_stash = ap("xn_stash", [TEXT, D], BF16, kind=kk("xn_stash"))
    c.gu_stash = ap("gu_stash", [2 * HFF, D], BF16, kind=kk("gu_stash"))
    c.wd_stash = ap("wd_stash", [D, HFF], BF16, kind=kk("wd_stash"))
    c.h_stash = ap("h_stash", [HFF, TOWN], BF16, kind=kk("h_stash"))
    c.projT_stash = ap("projT_st", [D, TOWN], F32, kind=kk("projT_st"))
    c.sum_stash = ap("sum_st", [D, TOWN], F32, kind=kk("sum_st"))

    sb = c.sb
    c.g_xnT = nc.sbuf_tensor("xnT_own", [128, NDT * TOWN], BF16, side="right")
    c.xnT_own = c.g_xnT.__enter__()
    c.alp_sb = sb("alp_sb", [128, 8], F32)
    c.one_sb = sb("one_sb", [128, 8], F32)
    c.msk_sb = sb("msk_sb", [128, 20], F32)
    c.idf_sb = sb("idf_sb", [128, 128], F32)
    c.bstack = contextlib.ExitStack()
    bsb = lambda n, s, d: c.bstack.enter_context(
        nc.sbuf_tensor(n, s, d, side="right"))
    c.v_aug = bsb("v_aug", [128, 12 * 132], BF16)
    c.krT = bsb("krT", [128, TEXT], BF16)
    c.cq_sb = bsb("cq_sb", [128, TOWN], F32)
    c.sq_sb = bsb("sq_sb", [128, TOWN], F32)
    c.g_halo = nc.sbuf_tensor("xnT_halo", [128, NDT * 512], BF16, side="right")
    c.xnT_halo = c.g_halo.__enter__()
    c.pb = [c.stack.enter_context(nc.psum_tensor(f"pb{i}", [128, 512], F32))
            for i in range(8)]

    for name in ("s_ld", "s_xt", "s_mm", "s_ve", "s_ac", "s_vsy"):
        setattr(c, name, Sem(c.stack.enter_context(nc.semaphore(name))))

    def sems(name, k):
        return [Sem(c.stack.enter_context(nc.semaphore(f"{name}{i}")))
                for i in range(k)]
    c.ldx = sems("ldx", 2)
    c.xw = sems("xw", 2)
    c.gq = sems("gq", 3)
    c.lq = sems("lq", 4)
    c.swl = sems("swl", 4)
    c.ldq = sems("ldq", 2)
    c.swd = sems("swd", 2)
    c.pan = sems("pan", 2)
    c.pans = sems("pans", 2)
    c.hw = sems("hw", 2)
    c.wd = sems("wd", 4)
    c.hh = sems("hh", 8)
    c.spr = sems("spr", 1)[0]
    c.sm = sems("sm", 2)
    c.pj = sems("pj", 2)
    c.dx = sems("dx", 2)
    c.stl = sems("stl", 2)
    c.ou = sems("ou", 2)
    c.bars = [Sem(c.stack.enter_context(nc.semaphore(f"bar{i}")))
              for i in range(10)]
    c.bar_i = 0


def barrier(c):
    sem = c.bars[c.bar_i]
    c.bar_i += 1
    for e in ENGS:
        def f(en, h=sem.h):
            en.drain()
            en.sem_inc(h, 1)
        c.P.emit(e, f)
    sem.n += 5
    for e in ENGS:
        c.P.wait(e, sem, sem.n)


def phase_a(c):
    P, nc = c.P, c.nc
    P.dma("sync", c.cq_sb[:], c.cosq, c.s_ld)
    P.dma("sync", c.sq_sb[:], c.sinq, c.s_ld)
    P.dma("sync", c.alp_sb[:], c.alpc, c.s_ld)
    P.dma("sync", c.one_sb[:], c.onec, c.s_ld)
    P.dma("sync", c.msk_sb[:], c.maskc, c.s_ld)
    P.dma("sync", c.idf_sb[:], c.idf, c.s_ld)

    ph = contextlib.ExitStack()
    t = lambda n, s, d: ph.enter_context(nc.sbuf_tensor(n, s, d))
    x_sb = [t("x_sb0", [128, D], F32), t("x_sb1", [128, D], F32)]
    xc_sb = [t("xc_sb0", [128, D], F32), t("xc_sb1", [128, D], F32)]
    sqr_sb = x_sb
    xnb = [t("xnb0", [128, D], BF16), t("xnb1", [128, D], BF16)]
    lncol = t("lncol", [128, 16], F32)

    NSTRIP = TEXT // 128
    def ln_steps(st):
        par = st % 2
        cl = lncol[:, 8 * par:8 * par + 8]
        steps = []
        def s0():
            if st >= 2:
                P.wait("sync", c.xw[par], 16 * (st // 2))
            P.dma("sync", x_sb[par][:], c.x_ext[128 * st:128 * (st + 1), :],
                  c.ldx[par])
        def s1():
            P.wait("vector", c.ldx[par], 16 * (st // 2 + 1))
            if st >= 2:
                P.wait("vector", c.xw[par], 16 * (st // 2))
            P.op("vector", "reduce_sum", cl[:, 0:1], x_sb[par][:], AXX,
                 inc=c.s_vsy)
        def s2():
            P.vsy("vector", c.s_vsy)
            P.op("vector", "tensor_scalar", cl[:, 1:2], cl[:, 0:1],
                 -1.0 / D, None, ALU.mult, inc=c.s_vsy)
        def s3():
            P.vsy("vector", c.s_vsy)
            P.op("vector", "tensor_scalar", xc_sb[par][:], x_sb[par][:],
                 cl[:, 1:2], None, ALU.add, inc=c.s_ve)
            c.lnve[st] = c.s_ve.n
        def s4():
            P.wait("scalar", c.s_ve, c.lnve[st])
            P.op("scalar", "activation", sqr_sb[par][:], xc_sb[par][:],
                 AF.Square, inc=c.s_ac)
            c.lnac[st] = c.s_ac.n
        def s5():
            P.wait("vector", c.s_ac, c.lnac[st])
            P.op("vector", "reduce_sum", cl[:, 2:3], sqr_sb[par][:], AXX,
                 inc=c.s_vsy)
        def s6():
            P.vsy("vector", c.s_vsy)
            P.op("vector", "tensor_scalar", cl[:, 2:3], cl[:, 2:3],
                 1.0 / D, 1e-5, ALU.mult, ALU.add, inc=c.s_ve)
            c.lnve[st] = c.s_ve.n
        def s7():
            P.wait("scalar", c.s_ve, c.lnve[st])
            P.op("scalar", "activation", cl[:, 3:4], cl[:, 2:3], AF.Sqrt,
                 inc=c.s_ac)
            c.lnac[st] = c.s_ac.n
        def s8():
            P.wait("vector", c.s_ac, c.lnac[st])
            P.op("vector", "reciprocal", cl[:, 0:1], cl[:, 3:4], inc=c.s_vsy)
        def s9():
            P.vsy("vector", c.s_vsy)
            P.op("vector", "tensor_scalar", xnb[par][:], xc_sb[par][:],
                 cl[:, 0:1], None, ALU.mult, inc_drain=c.s_ve)
            c.lnve[st] = c.s_ve.n
        def s10():
            P.wait("gpsimd", c.s_ve, c.lnve[st])
            P.dma("gpsimd", c.xn_stash[128 * st:128 * (st + 1), :],
                  xnb[par][:], c.xw[par])
        return [s0, s1, s2, s3, s4, s5, s6, s7, s8, s9, s10]

    c.lnve = {}
    c.lnac = {}
    chains = [ln_steps(st) for st in range(NSTRIP)]
    for base in range(0, NSTRIP, 2):
        a, b2 = chains[base], chains[base + 1]
        for i in range(11):
            a[i]()
            b2[i]()

    P.wait("sync", c.xw[0], 16 * (NSTRIP // 2))
    P.wait("sync", c.xw[1], 16 * (NSTRIP // 2))
    for dt in range(NDT):
        c0, c1 = 128 * dt, 128 * (dt + 1)
        P.dma("sync", c.xnT_own[:, TOWN * dt:TOWN * (dt + 1)],
              c.xn_stash[HALO:HALO + TOWN, c0:c1], c.s_xt, transpose=True)
        P.dma("sync", c.xnT_halo[:, 512 * dt:512 * dt + 256],
              c.xn_stash[0:HALO, c0:c1], c.s_xt, transpose=True)
        P.dma("sync", c.xnT_halo[:, 512 * dt + 256:512 * dt + 512],
              c.xn_stash[TEXT - HALO:TEXT, c0:c1], c.s_xt, transpose=True)
    ph.close()


def phase_b1(c):
    P, nc, pb = c.P, c.nc, c.pb
    ph = contextlib.ExitStack()
    t = lambda n, s, d: ph.enter_context(nc.sbuf_tensor(n, s, d))
    ck_sb = t("ck_sb", [128, TEXT], F32)
    sk_sb = t("sk_sb", [128, TEXT], F32)
    wk_bf = t("wk_bf", [128, NDT * DH], BF16)
    wv_bf = t("wv_bf", [128, NDT * DH], BF16)
    kf_sb = t("kf_sb", [128, 512], F32)
    ksw_sb = t("ksw_sb", [128, 512], F32)
    kt1_sb = t("kt1_sb", [128, 512], F32)

    P.dma("sync", ck_sb[:], c.cosk, c.s_ld)
    P.dma("sync", sk_sb[:], c.sink, c.s_ld)
    P.dma("gpsimd", wk_bf[:], c.wkt, c.s_ld)
    P.dma("gpsimd", wv_bf[:], c.wvt, c.s_ld)
    P.wait("tensor", c.s_ld, c.s_ld.n)
    P.wait("vector", c.s_ld, c.s_ld.n)
    P.wait("scalar", c.s_ld, c.s_ld.n)
    P.wait("tensor", c.s_xt, c.s_xt.n)

    def ext_rhs_pieces(lo, hi):
        segs = []
        if lo < 256:
            e = min(hi, 256)
            segs.append(("halo", lo, e - lo))
            lo = e
        if lo < hi and lo < 1280:
            e = min(hi, 1280)
            segs.append(("own", lo - 256, e - lo))
            lo = e
        if lo < hi:
            segs.append(("halo", 256 + (lo - 1280), hi - lo))
        return segs

    for ch in range(3):
        segs = ext_rhs_pieces(512 * ch, 512 * (ch + 1))
        off = 0
        for kind, start, width in segs:
            for dt in range(NDT):
                if kind == "own":
                    rhs = c.xnT_own[:, TOWN * dt + start:TOWN * dt + start + width]
                else:
                    rhs = c.xnT_halo[:, 512 * dt + start:512 * dt + start + width]
                P.op("tensor", "matmul", pb[ch][:, off:off + width],
                     wk_bf[:, 128 * dt:128 * (dt + 1)], rhs,
                     start=(dt == 0), stop=(dt == NDT - 1),
                     inc=(c.s_mm if dt == NDT - 1 else None))
            off += width
        P.wait("vector", c.s_mm, c.s_mm.n)
        cc = ck_sb[:, 512 * ch:512 * (ch + 1)]
        scs = sk_sb[:, 512 * ch:512 * (ch + 1)]
        P.op("vector", "tensor_copy", kf_sb[:], pb[ch][:])
        P.op("vector", "tensor_copy", ksw_sb[0:64, :], kf_sb[64:128, :])
        P.op("vector", "tensor_copy", ksw_sb[64:128, :], kf_sb[0:64, :])
        P.op("vector", "tensor_mul", kt1_sb[:], kf_sb[:], cc)
        P.op("vector", "tensor_mul", ksw_sb[:], ksw_sb[:], scs)
        P.op("vector", "tensor_add", c.krT[:, 512 * ch:512 * (ch + 1)],
             kt1_sb[:], ksw_sb[:], inc=c.s_ve)
    k_done = c.s_ve.n

    for tt in range(12):
        lo = 128 * tt
        if lo < 256:
            def lhs_t(dt, lo=lo):
                return c.xnT_halo[:, 512 * dt + lo:512 * dt + lo + 128]
        elif lo < 1280:
            def lhs_t(dt, lo=lo):
                o = lo - 256
                return c.xnT_own[:, TOWN * dt + o:TOWN * dt + o + 128]
        else:
            def lhs_t(dt, lo=lo):
                o = 256 + lo - 1280
                return c.xnT_halo[:, 512 * dt + o:512 * dt + o + 128]
        bank = pb[3 + (tt % 2)]
        if tt >= 2:
            P.wait("tensor", c.s_ve, k_done + tt - 1)
        for dt in range(NDT):
            P.op("tensor", "matmul", bank[:, 0:128],
                 lhs_t(dt), wv_bf[:, 128 * dt:128 * (dt + 1)],
                 start=(dt == 0), stop=(dt == NDT - 1),
                 inc=(c.s_mm if dt == NDT - 1 else None))
        P.wait("vector", c.s_mm, c.s_mm.n)
        P.op("vector", "tensor_copy",
             c.v_aug[:, 132 * tt:132 * tt + 128], bank[:, 0:128])
        P.op("vector", "memset",
             c.v_aug[:, 132 * tt + 128:132 * tt + 129], 1.0, inc=c.s_ve)
    ph.close()


WIN_Q = [(0, 256), (0, 512), (256, 768), (512, 1024), (768, 1024)]
WIN_SLOT0 = [2, 0, 0, 0, 0]


def phase_b2(c, attn_outT):
    P, nc, pb = c.P, c.nc, c.pb
    ph = contextlib.ExitStack()
    t = lambda n, s, d: ph.enter_context(nc.sbuf_tensor(n, s, d))
    wq_bf = [t(f"wq_bf{i}", [128, NDT * 128], BF16) for i in range(3)]
    qf_sb = t("qf_sb", [128, TOWN], F32)
    qsw_sb = t("qsw_sb", [128, TOWN], F32)
    qt1_sb = t("qt1_sb", [128, TOWN], F32)
    qrT = [t("qrT0", [128, TOWN], BF16), t("qrT1", [128, TOWN], BF16)]
    attnT = t("attnT", [128, 2048], BF16)
    wbuf = [t("wb0", [128, 4 * 132], F32), t("wb1", [128, 4 * 132], F32)]
    bcol = t("bcol", [128, 4], F32)
    blf = [t("blf0", [128, 2 * 128], F32), t("blf1", [128, 2 * 128], F32)]

    sig_done = {k: 0 for k in range(4)}
    av_bank_evac = {k: 0 for k in range(4)}
    tr_bank_evac = {0: 0, 1: 0}
    blend_prog = 0
    av_mm_done = 0
    gq_val = {}
    rope_done = {}
    qmm_done = {}

    def q_chunks(h):
        par = h % 3
        chunks = []
        for cb in range(8):
            def f(cb=cb, par=par, h=h):
                if cb == 0:
                    P.wait("tensor", c.gq[par], gq_val[h])
                    if h >= 1:
                        P.wait("tensor", c.s_ve, rope_done[h - 1])
                for dt in range(4 * cb, 4 * cb + 4):
                    lhs = wq_bf[par][:, 128 * dt:128 * (dt + 1)]
                    for th in range(2):
                        P.op("tensor", "matmul", pb[4 + th][:, 0:512],
                             lhs,
                             c.xnT_own[:, TOWN * dt + 512 * th:
                                       TOWN * dt + 512 * th + 512],
                             start=(dt == 0), stop=(dt == NDT - 1),
                             inc=(c.s_mm if dt == NDT - 1 and th == 1
                                  else None))
                if cb == 7:
                    qmm_done[h] = c.s_mm.n
            chunks.append(f)
        return chunks

    def emit_rope(h):
        par = h % 2
        P.wait("vector", c.s_mm, qmm_done[h])
        P.op("vector", "tensor_copy", qf_sb[:, 0:512], pb[4][:])
        P.op("vector", "tensor_copy", qf_sb[:, 512:1024], pb[5][:])
        P.op("vector", "tensor_copy", qsw_sb[0:64, :], qf_sb[64:128, :])
        P.op("vector", "tensor_copy", qsw_sb[64:128, :], qf_sb[0:64, :])
        P.op("vector", "tensor_mul", qt1_sb[:], qf_sb[:], c.cq_sb[:])
        P.op("vector", "tensor_mul", qsw_sb[:], qsw_sb[:], c.sq_sb[:])
        P.op("vector", "tensor_add", qrT[par][:], qt1_sb[:], qsw_sb[:],
             inc=c.s_ve)
        rope_done[h] = c.s_ve.n

    for hh in range(3):
        P.dma("gpsimd", wq_bf[hh][:], c.wqt[128 * hh:128 * (hh + 1), :],
              c.gq[hh])
        gq_val[hh] = c.gq[hh].n
    for f in q_chunks(0):
        f()
    emit_rope(0)

    for h in range(H):
        par = h % 3
        pump = q_chunks(h + 1) if h + 1 < H else []
        pump_i = [0]
        bl_ve = {}

        def do_pump(n=1):
            for _ in range(n):
                if pump_i[0] < len(pump):
                    pump[pump_i[0]]()
                    pump_i[0] += 1

        if h + 3 < H:
            P.wait("gpsimd", c.s_mm, qmm_done[h])
            P.dma("gpsimd", wq_bf[par][:],
                  c.wqt[128 * (h + 3):128 * (h + 4), :], c.gq[par])
            gq_val[h + 3] = c.gq[par].n

        for w in range(5):
            qlo, qhi = WIN_Q[w]
            qn = qhi - qlo
            wslot = wbuf[w % 2]
            for kt in range(4):
                if kt == 0 and w == 0:
                    P.wait("tensor", c.s_ve, rope_done[h])
                P.wait("tensor", c.s_ac, av_bank_evac[kt])
                P.wait("tensor", c.s_ac, sig_done[kt])
                P.op("tensor", "matmul", pb[kt][:, 0:qn],
                     c.krT[:, 256 * w + 128 * kt:256 * w + 128 * (kt + 1)],
                     qrT[h % 2][:, qlo:qhi], start=True, stop=True,
                     inc=c.s_mm)
            sc_done = c.s_mm.n
            do_pump(1)
            for kt in range(4):
                P.wait("scalar", c.s_mm, sc_done - 3 + kt)
                P.wait("scalar", c.s_mm, av_mm_done)
                P.wait("scalar", c.s_ve, blend_prog)
                P.op("scalar", "activation", attnT[:, 512 * kt:512 * kt + qn],
                     pb[kt][:, 0:qn], AF.Sigmoid,
                     bias=c.msk_sb[:, 4 * w + kt:4 * w + kt + 1],
                     scale=SCALE, inc=c.s_ac)
                sig_done[kt] = c.s_ac.n
            nq = qn // 128
            for qs in range(nq):
                bank = pb[qs]
                P.wait("tensor", c.s_ac, sig_done[3])
                for kt in range(4):
                    P.op("tensor", "matmul", bank[:, 0:129],
                         attnT[:, 512 * kt + 128 * qs:512 * kt + 128 * (qs + 1)],
                         c.v_aug[:, 132 * (2 * w + kt):132 * (2 * w + kt) + 129],
                         start=(kt == 0), stop=(kt == 3),
                         inc=(c.s_mm if kt == 3 else None))
                av_mm_done = c.s_mm.n
                do_pump(1)
                P.wait("scalar", c.s_mm, av_mm_done)
                slot = WIN_SLOT0[w] + qs
                P.op("scalar", "copy", wslot[:, 132 * slot:132 * slot + 129],
                     bank[:, 0:129], inc=c.s_ac)
                av_bank_evac[qs] = c.s_ac.n

            if w == 2 and h + 1 < H:
                emit_rope(h + 1)

            def emit_blend(k):
                A = wbuf[k % 2]
                B = wbuf[(k + 1) % 2]
                for qs in range(2):
                    aslot = 2 + qs
                    bslot = qs
                    col = 2 * k + qs
                    P.wait("vector", c.s_ac, av_bank_evac[bslot])
                    P.op("vector", "reciprocal", bcol[:, 0:1],
                         A[:, 132 * aslot + 128:132 * aslot + 129])
                    P.op("vector", "reciprocal", bcol[:, 1:2],
                         B[:, 132 * bslot + 128:132 * bslot + 129],
                         inc=c.s_vsy)
                    P.vsy("vector", c.s_vsy)
                    P.op("vector", "tensor_mul", bcol[:, 2:3],
                         bcol[:, 0:1], c.one_sb[:, col:col + 1])
                    P.op("vector", "tensor_mul", bcol[:, 3:4],
                         bcol[:, 1:2], c.alp_sb[:, col:col + 1], inc=c.s_vsy)
                    P.vsy("vector", c.s_vsy)
                    bl = blf[k % 2][:, 128 * qs:128 * (qs + 1)]
                    P.op("vector", "tensor_scalar", bl,
                         A[:, 132 * aslot:132 * aslot + 128],
                         bcol[:, 2:3], None, ALU.mult)
                    P.op("vector", "scalar_tensor_tensor", bl,
                         B[:, 132 * bslot:132 * bslot + 128],
                         bcol[:, 3:4], bl, ALU.mult, ALU.add, inc=c.s_ve)
                return c.s_ve.n

            def emit_transpose(k, bl_ve):
                for qs in range(2):
                    col = 2 * k + qs
                    tbank = 6 + (qs % 2)
                    P.wait("tensor", c.s_ve, bl_ve)
                    P.wait("tensor", c.s_ve, tr_bank_evac[qs % 2])
                    P.op("tensor", "transpose", pb[tbank][:, 0:128],
                         blf[k % 2][:, 128 * qs:128 * (qs + 1)],
                         c.idf_sb[:], inc=c.s_mm)
                    P.wait("vector", c.s_mm, c.s_mm.n)
                    P.op("vector", "tensor_copy",
                         attn_outT[:, TOWN * h + 128 * col:
                                   TOWN * h + 128 * (col + 1)],
                         pb[tbank][:, 0:128], inc=c.s_ve)
                    tr_bank_evac[qs % 2] = c.s_ve.n

            if w >= 1:
                bl_ve[w - 1] = emit_blend(w - 1)
                blend_prog = c.s_ve.n
            if w >= 2:
                emit_transpose(w - 2, bl_ve[w - 2])
        emit_transpose(3, bl_ve[3])
        do_pump(16)
    ph.close()


def phase_b3(c, attn_outT, dq):
    P, nc, pb = c.P, c.nc, c.pb
    ph = contextlib.ExitStack()
    t = lambda n, s, d: ph.enter_context(nc.sbuf_tensor(n, s, d))
    wo_bf = [t(f"wo_bf{i}", [128, NDT * 128], BF16) for i in range(3)]
    pr_sb = [t("pr_sb0", [128, TOWN], F32), t("pr_sb1", [128, TOWN], F32)]
    wo_mm = {}
    gq_val = {}
    for hh in range(3):
        P.dma("gpsimd", wo_bf[hh][:], c.wot[128 * hh:128 * (hh + 1), :],
              c.gq[hh])
        gq_val[hh] = c.gq[hh].n
    for i in range(NDT):
        par = i % 2
        wpar = i % 3
        P.wait("tensor", c.gq[wpar], gq_val[i])
        if i >= 2:
            P.wait("tensor", c.pj[par], 16 * (i // 2))
        for ot in range(NDT):
            lhs = wo_bf[wpar][:, 128 * ot:128 * (ot + 1)]
            for th in range(2):
                P.op("tensor", "matmul", pb[2 * (i % 2) + th][:, 0:512],
                     lhs,
                     attn_outT[:, TOWN * ot + 512 * th:
                               TOWN * ot + 512 * (th + 1)],
                     start=(ot == 0), stop=(ot == NDT - 1),
                     inc=(c.s_mm if ot == NDT - 1 and th == 1 else None))
        wo_mm[i] = c.s_mm.n
        if i + 3 < NDT:
            P.wait("gpsimd", c.s_mm, wo_mm[i])
            P.dma("gpsimd", wo_bf[wpar][:],
                  c.wot[128 * (i + 3):128 * (i + 4), :], c.gq[wpar])
            gq_val[i + 3] = c.gq[wpar].n
        P.wait("vector", c.s_mm, c.s_mm.n)
        if i >= 2:
            P.wait("vector", c.pj[par], 16 * (i // 2))
        P.op("vector", "tensor_copy", pr_sb[par][:, 0:512], pb[2 * (i % 2)][:])
        P.op("vector", "tensor_copy", pr_sb[par][:, 512:1024],
             pb[2 * (i % 2) + 1][:], inc_drain=c.s_ve)
        P.wait("gpsimd", c.s_ve, c.s_ve.n)
        P.dma("gpsimd", c.projT_stash[128 * i:128 * (i + 1), :], pr_sb[par][:],
              c.pj[par])
        if i < 16:
            dq.load(i)
        if i % 2 == 1 and i // 2 < 12:
            dq.compute(i // 2)
    ph.close()


class DqGU:

    def __init__(self, c, scope):
        nc = c.nc
        t = lambda n, s, d: scope.enter_context(
            nc.sbuf_tensor(n, s, d, side="right"))
        self.qin = [t(f"qin{i}", [128, D], U8) for i in range(4)]
        self.qo = [t(f"qo{i}", [128, D], BF16) for i in range(2)]
        self.zin = t("zin", [128, 4 * 32], U8)
        self.sin_ = t("sin_", [128, 4 * 32], F32)
        self.zf = t("zf", [128, 4 * 32], F32)
        self.c = c
        self.ld = {}

    def load(self, q):
        c, P = self.c, self.c.P
        jj, kind = q // 2, q % 2
        lane = q % 4
        qw = c.qg if kind == 0 else c.qu
        zt = c.zg if kind == 0 else c.zu
        st_ = c.sg if kind == 0 else c.su
        r0, r1 = 128 * jj, 128 * (jj + 1)
        if q >= 4:
            P.wait("scalar", c.swl[q % 2], 16 * (q // 2 - 1))
        P.dma("scalar", self.qin[lane][:], qw[r0:r1, :], c.lq[lane])
        P.dma("scalar", self.zin[:, 32 * lane:32 * (lane + 1)], zt[r0:r1, :],
              c.lq[lane])
        P.dma("scalar", self.sin_[:, 32 * lane:32 * (lane + 1)], st_[r0:r1, :],
              c.lq[lane])
        self.ld[q] = c.lq[lane].n

    def compute(self, q):
        c, P = self.c, self.c.P
        jj, kind = q // 2, q % 2
        lane = q % 4
        olane = q % 2
        s0 = 256 * jj + 128 * kind
        P.wait("vector", c.lq[lane], self.ld[q])
        if q >= 2:
            P.wait("vector", c.swl[olane], 16 * (q // 2))
        P.op("vector", "tensor_copy", self.zf[:, 32 * lane:32 * (lane + 1)],
             self.zin[:, 32 * lane:32 * (lane + 1)], inc=c.s_vsy)
        P.vsy("vector", c.s_vsy)
        for g in range(NG_IN):
            csl = slice(128 * g, 128 * (g + 1))
            zc = slice(32 * lane + g, 32 * lane + g + 1)
            P.op("vector", "tensor_scalar", self.qo[olane][:, csl],
                 self.qin[lane][:, csl], self.zf[:, zc], self.sin_[:, zc],
                 ALU.subtract, ALU.mult,
                 inc_drain=(c.s_ve if g == NG_IN - 1 else None))
        P.wait("gpsimd", c.s_ve, c.s_ve.n)
        P.dma("gpsimd", c.gu_stash[s0:s0 + 128, :], self.qo[olane][:],
              c.swl[olane])


class DqDown:

    def __init__(self, c, scope):
        nc = c.nc
        HH = HFF // 2
        t = lambda n, s, d: scope.enter_context(nc.sbuf_tensor(n, s, d))
        self.qind = [t(f"qind{i}", [128, HH], U8) for i in range(2)]
        self.qodd = [t(f"qodd{i}", [128, HH], BF16) for i in range(2)]
        self.zind = t("zind", [128, 2 * NG_OUT], U8)
        self.sind = t("sind", [128, 2 * NG_OUT], F32)
        self.zfd = t("zfd", [128, 2 * NG_OUT], F32)
        self.c = c
        self.ld = {}

    def load(self, dj):
        c, P = self.c, self.c.P
        HH = HFF // 2
        dt, half = dj // 2, dj % 2
        lane = dj % 2
        r0, r1 = 128 * dt, 128 * (dt + 1)
        h0 = HH * half
        if dj >= 2:
            P.wait("scalar", c.swd[lane], 16 * (dj // 2))
        P.dma("scalar", self.zind[:, NG_OUT * lane:NG_OUT * (lane + 1)],
              c.zd[r0:r1, :], c.ldq[lane])
        P.dma("scalar", self.sind[:, NG_OUT * lane:NG_OUT * (lane + 1)],
              c.sd[r0:r1, :], c.ldq[lane])
        P.dma("scalar", self.qind[lane][:], c.qd[r0:r1, h0:h0 + HH],
              c.ldq[lane])
        self.ld[dj] = c.ldq[lane].n

    def compute(self, dj):
        c, P = self.c, self.c.P
        HH = HFF // 2
        NGH = NG_OUT // 2
        dt, half = dj // 2, dj % 2
        lane = dj % 2
        r0, r1 = 128 * dt, 128 * (dt + 1)
        h0 = HH * half
        P.wait("vector", c.ldq[lane], self.ld[dj])
        P.op("vector", "tensor_copy",
             self.zfd[:, NG_OUT * lane:NG_OUT * (lane + 1)],
             self.zind[:, NG_OUT * lane:NG_OUT * (lane + 1)], inc=c.s_vsy)
        P.vsy("vector", c.s_vsy)
        for gg in range(NGH):
            g = NGH * half + gg
            csl = slice(128 * gg, 128 * (gg + 1))
            zc = slice(NG_OUT * lane + g, NG_OUT * lane + g + 1)
            P.op("vector", "tensor_scalar", self.qodd[lane][:, csl],
                 self.qind[lane][:, csl], self.zfd[:, zc], self.sind[:, zc],
                 ALU.subtract, ALU.mult,
                 inc_drain=(c.s_ve if gg == NGH - 1 else None))
        P.wait("gpsimd", c.s_ve, c.s_ve.n)
        P.dma("gpsimd", c.wd_stash[r0:r1, h0:h0 + HH], self.qodd[lane][:],
              c.swd[lane])


def phase_c1(c, dq, dqd):
    P, nc, pb = c.P, c.nc, c.pb
    ph = contextlib.ExitStack()
    t = lambda n, s, d: ph.enter_context(nc.sbuf_tensor(n, s, d))
    pan = [t("pan0", [128, NDT * 512], BF16),
           t("pan1", [128, NDT * 512], BF16)]
    sg_sb = [t("sg_sb0", [128, TOWN], BF16), t("sg_sb1", [128, TOWN], BF16)]
    h_sb = [t("h_sb0", [128, TOWN], BF16), t("h_sb1", [128, TOWN], BF16)]
    NPAN = NHT // 2
    pan_ld = {}
    gu_mm = {}
    h_ve = {}
    NGU = 2 * NHT
    NDJ = 2 * NDT
    for dj in range(2):
        dqd.load(dj)
    for J in range(NPAN):
        ppar = J % 2
        for q in range(4 * J + 12, min(4 * J + 16, NGU)):
            dq.compute(q)
        for dj in range(2 * J, min(2 * J + 2, NDJ)):
            dqd.compute(dj)
        if J >= 2:
            P.wait("sync", c.s_mm, gu_mm[2 * (J - 2) + 1])
        P.wait("sync", c.swl[0], 16 * min(2 * J + 2, NHT))
        P.wait("sync", c.swl[1], 16 * min(2 * J + 2, NHT))
        for dt in range(NDT):
            P.dma("sync", pan[ppar][:, 512 * dt:512 * (dt + 1)],
                  c.gu_stash[512 * J:512 * (J + 1), 128 * dt:128 * (dt + 1)],
                  c.pan[ppar], transpose=True)
        pan_ld[J] = c.pan[ppar].n
        for r in range(2):
            jj = 2 * J + r
            jpar = jj % 2
            P.wait("tensor", c.pan[ppar], pan_ld[J])
            if jj >= 2:
                P.wait("tensor", c.s_ve, h_ve[jj - 2])
            for dt in range(NDT):
                glhs = pan[ppar][:, 512 * dt + 256 * r:
                                512 * dt + 256 * r + 128]
                ulhs = pan[ppar][:, 512 * dt + 256 * r + 128:
                                512 * dt + 256 * r + 256]
                for th in range(2):
                    P.op("tensor", "matmul", pb[4 * jpar + th][:, 0:512],
                         glhs,
                         c.xnT_own[:, TOWN * dt + 512 * th:
                                   TOWN * dt + 512 * (th + 1)],
                         start=(dt == 0), stop=(dt == NDT - 1),
                         inc=(c.s_mm if dt == NDT - 1 and th == 1 else None))
                for th in range(2):
                    P.op("tensor", "matmul",
                         pb[4 * jpar + 2 + th][:, 0:512],
                         ulhs,
                         c.xnT_own[:, TOWN * dt + 512 * th:
                                   TOWN * dt + 512 * (th + 1)],
                         start=(dt == 0), stop=(dt == NDT - 1),
                         inc=(c.s_mm if dt == NDT - 1 and th == 1 else None))
            gu_mm[jj] = c.s_mm.n
            P.wait("scalar", c.s_mm, gu_mm[jj] - 1)
            if jj >= 2:
                P.wait("scalar", c.s_ve, h_ve[jj - 2])
            P.op("scalar", "activation", sg_sb[jpar][:, 0:512],
                 pb[4 * jpar + 0][:], AF.Silu)
            P.op("scalar", "activation", sg_sb[jpar][:, 512:1024],
                 pb[4 * jpar + 1][:], AF.Silu, inc=c.s_ac)
            P.wait("vector", c.s_ac, c.s_ac.n)
            P.wait("vector", c.s_mm, gu_mm[jj])
            if jj >= 2:
                P.wait("vector", c.hw[jpar], 16 * (jj // 2))
            P.op("vector", "tensor_mul", h_sb[jpar][:, 0:512],
                 sg_sb[jpar][:, 0:512], pb[4 * jpar + 2][:])
            P.op("vector", "tensor_mul", h_sb[jpar][:, 512:1024],
                 sg_sb[jpar][:, 512:1024], pb[4 * jpar + 3][:],
                 inc_drain=c.s_ve)
            h_ve[jj] = c.s_ve.n
            P.wait("gpsimd", c.s_ve, c.s_ve.n)
            P.dma("gpsimd", c.h_stash[128 * jj:128 * (jj + 1), :],
                  h_sb[jpar][:], c.hw[jpar])
        for q in range(4 * J + 16, min(4 * J + 20, NGU)):
            dq.load(q)
        for dj in range(2 * J + 2, min(2 * J + 4, NDJ)):
            dqd.load(dj)
    ph.close()


def phase_c2(c):
    P, nc, pb = c.P, c.nc, c.pb
    ph = contextlib.ExitStack()
    t = lambda n, s, d: ph.enter_context(nc.sbuf_tensor(n, s, d))
    wdt = [t(f"wdt{i}", [128, 512], BF16) for i in range(4)]
    hT = t("hT", [128, NHT * TOWN], BF16)
    prt = t("prt", [128, 4 * TOWN], F32)
    smt = [t("smt0", [128, TOWN], F32), t("smt1", [128, TOWN], F32)]
    for s in (c.hw[0], c.hw[1], c.pj[0], c.pj[1], c.swd[0], c.swd[1]):
        P.wait("sync", s, s.n)
        P.wait("gpsimd", s, s.n)
    dn_mm = {}
    sum_ve = 0
    hh_val = {}

    def load_h(jj):
        if jj >= 8:
            P.wait("gpsimd", c.hh[jj % 8], hh_val[jj - 8])
        P.dma("gpsimd", hT[:, TOWN * jj:TOWN * (jj + 1)],
              c.h_stash[128 * jj:128 * (jj + 1), :], c.hh[jj % 8])
        hh_val[jj] = c.hh[jj % 8].n

    for jj in range(8):
        load_h(jj)
    for ch in range(8):
        P.wait("sync", c.s_ve, sum_ve)
        for ic in range(4):
            i = 4 * ch + ic
            P.dma("sync", prt[:, TOWN * ic:TOWN * (ic + 1)],
                  c.projT_stash[128 * i:128 * (i + 1), :], c.spr)
        prj_ld = c.spr.n
        for jj in range(NHT):
            lane = jj % 4
            prev = (ch, jj - 4) if jj >= 4 else (ch - 1, 82 + (jj + 2) % 4)
            if prev in dn_mm:
                P.wait("sync", c.s_mm, dn_mm[prev])
            P.dma("sync", wdt[lane][:],
                  c.wd_stash[512 * ch:512 * (ch + 1),
                             128 * jj:128 * (jj + 1)], c.wd[lane],
                  transpose=True)
            P.wait("tensor", c.wd[lane], c.wd[lane].n)
            if ch == 0:
                P.wait("tensor", c.hh[jj % 8], hh_val[jj])
                if jj + 8 < NHT:
                    load_h(jj + 8)
            if jj == 0 and ch > 0:
                P.wait("tensor", c.s_ve, sum_ve)
            for ic in range(4):
                for th in range(2):
                    P.op("tensor", "matmul", pb[2 * ic + th][:, 0:512],
                         wdt[lane][:, 128 * ic:128 * (ic + 1)],
                         hT[:, TOWN * jj + 512 * th:TOWN * jj + 512 * (th + 1)],
                         start=(jj == 0), stop=(jj == NHT - 1),
                         inc=(c.s_mm if (th == 1 and ic == 3) else None))
            dn_mm[(ch, jj)] = c.s_mm.n
        P.wait("vector", c.s_mm, dn_mm[(ch, NHT - 1)])
        P.wait("vector", c.spr, prj_ld)
        for ic in range(4):
            i = 4 * ch + ic
            k = 4 * ch + ic
            sl = ic % 2
            if k >= 2:
                P.wait("vector", c.sm[sl], 16 * (k // 2))
            for th in range(2):
                P.op("vector", "tensor_add",
                     smt[sl][:, 512 * th:512 * (th + 1)],
                     pb[2 * ic + th][:],
                     prt[:, TOWN * ic + 512 * th:TOWN * ic + 512 * (th + 1)],
                     inc_drain=(c.s_ve if th == 1 else None))
            P.wait("gpsimd", c.s_ve, c.s_ve.n)
            P.dma("gpsimd", c.sum_stash[128 * i:128 * (i + 1), :],
                  smt[sl][:], c.sm[sl])
        sum_ve = c.s_ve.n
    ph.close()


def phase_d(c):
    P, nc, pb = c.P, c.nc, c.pb
    ph = contextlib.ExitStack()
    t = lambda n, s, d: ph.enter_context(nc.sbuf_tensor(n, s, d))
    xs = [t("xs0", [128, D], F32), t("xs1", [128, D], F32)]
    stile = [t("stile0", [128, 128], F32), t("stile1", [128, 128], F32)]
    ostrip = [t("ostrip0", [128, D], F32), t("ostrip1", [128, D], F32)]
    for s in (c.sm[0], c.sm[1]):
        P.wait("sync", s, s.n)
    tr_mm = {0: 0, 1: 0}
    ev_ve = {0: 0, 1: 0}
    dx_val = {}
    for tt in range(8):
        par = tt % 2
        if tt >= 2:
            P.wait("sync", c.ou[par], 16 * (tt // 2))
        P.dma("sync", xs[par][:],
              c.x_ext[HALO + 128 * tt:HALO + 128 * (tt + 1), :], c.dx[par])
        dx_val[tt] = c.dx[par].n
        for i in range(NDT):
            sl = i % 2
            P.wait("sync", c.s_mm, tr_mm[sl])
            P.dma("sync", stile[sl][:],
                  c.sum_stash[128 * i:128 * (i + 1), 128 * tt:128 * (tt + 1)],
                  c.stl[sl])
            P.wait("tensor", c.stl[sl], c.stl[sl].n)
            P.wait("tensor", c.s_ve, ev_ve[sl])
            P.op("tensor", "transpose", pb[sl][:, 0:128], stile[sl][:],
                 c.idf_sb[:], inc=c.s_mm)
            tr_mm[sl] = c.s_mm.n
            P.wait("vector", c.s_mm, c.s_mm.n)
            if i == 0:
                P.wait("vector", c.dx[par], dx_val[tt])
                if tt >= 2:
                    P.wait("vector", c.ou[par], 16 * (tt // 2))
            P.op("vector", "tensor_add",
                 ostrip[par][:, 128 * i:128 * (i + 1)],
                 pb[sl][:, 0:128], xs[par][:, 128 * i:128 * (i + 1)],
                 **({"inc_drain": c.s_ve} if i == NDT - 1 else {"inc": c.s_ve}))
            ev_ve[sl] = c.s_ve.n
        P.wait("gpsimd", c.s_ve, c.s_ve.n)
        P.dma("gpsimd", c.out[128 * tt:128 * (tt + 1), :], ostrip[par][:],
              c.ou[par])
    P.wait("gpsimd", c.ou[0], c.ou[0].n)
    P.wait("gpsimd", c.ou[1], c.ou[1].n)
    ph.close()


def build_program():
    c = Ctx()
    c.nc = bass.Bass("TRN2", target_bir_lowering=False, debug=False)
    c.P = Plan()
    c.stack = contextlib.ExitStack()
    c.sb = lambda n, s, d: c.stack.enter_context(c.nc.sbuf_tensor(n, s, d))
    _mk(c)

    phase_a(c)
    barrier(c)
    phase_b1(c)
    c.g_halo.__exit__(None, None, None)
    barrier(c)
    g_attn = c.nc.sbuf_tensor("attn_outT", [128, H * TOWN], BF16)
    attn_outT = g_attn.__enter__()
    phase_b2(c, attn_outT)
    c.bstack.close()
    barrier(c)
    dq_scope = contextlib.ExitStack()
    dq = DqGU(c, dq_scope)
    phase_b3(c, attn_outT, dq)
    g_attn.__exit__(None, None, None)
    barrier(c)
    dqd_scope = contextlib.ExitStack()
    dqd = DqDown(c, dqd_scope)
    phase_c1(c, dq, dqd)
    dq_scope.close()
    dqd_scope.close()
    c.g_xnT.__exit__(None, None, None)
    barrier(c)
    phase_c2(c)
    barrier(c)
    phase_d(c)

    with c.nc.Block() as block:
        @block.sync
        def _(e):
            for fn in c.P.q["sync"]:
                fn(e)

        @block.tensor
        def _(e):
            for fn in c.P.q["tensor"]:
                fn(e)

        @block.vector
        def _(e):
            for fn in c.P.q["vector"]:
                fn(e)

        @block.scalar
        def _(e):
            for fn in c.P.q["scalar"]:
                fn(e)

        @block.gpsimd
        def _(e):
            for fn in c.P.q["gpsimd"]:
                fn(e)

    c.stack.close()
    return c.nc


def _rope_tables():
    inv_freq = 1.0 / 10000.0 ** (np.arange(0, DH, 2, dtype=np.float32) / DH)
    t = np.arange(S, dtype=np.float32)
    f = np.outer(t, inv_freq)
    cos = np.zeros((S, DH), dtype=np.float32)
    sin = np.zeros((S, DH), dtype=np.float32)
    cos[:, 0::2] = np.cos(f)
    cos[:, 1::2] = np.cos(f)
    sin[:, 0::2] = np.sin(f)
    sin[:, 1::2] = np.sin(f)
    sinS = sin.copy()
    sinS[:, :DH // 2] *= -1.0
    return cos, sinS


def _u8view(a_int32):
    a = np.ascontiguousarray(a_int32)
    return np.ascontiguousarray(a.view(np.uint8)[:, ::4])


def kernel(L0_attn_x, L0_attn_Wq, L0_attn_Wk, L0_attn_Wv, L0_attn_Wo,
           L0_ffn_qw_gate, L0_ffn_z_gate, L0_ffn_s_gate,
           L0_ffn_qw_up, L0_ffn_z_up, L0_ffn_s_up,
           L0_ffn_qw_down, L0_ffn_z_down, L0_ffn_s_down):
    x = np.ascontiguousarray(
        np.asarray(L0_attn_x, dtype=np.float32).reshape(S, D))
    Wq = np.asarray(L0_attn_Wq, dtype=np.float32)
    Wk = np.asarray(L0_attn_Wk, dtype=np.float32)
    Wv = np.asarray(L0_attn_Wv, dtype=np.float32)
    Wo = np.asarray(L0_attn_Wo, dtype=np.float32)
    wqt = np.ascontiguousarray(
        Wq.reshape(H, 128, NDT, 128).transpose(0, 3, 2, 1)).reshape(H * 128, -1)
    wot = np.ascontiguousarray(
        Wo.reshape(NDT, 128, NDT, 128).transpose(0, 3, 2, 1)).reshape(D, -1)
    wkt = np.ascontiguousarray(
        Wk.reshape(DH, NDT, 128).transpose(2, 1, 0)).reshape(128, -1)
    wvt = np.ascontiguousarray(
        Wv.reshape(DH, NDT, 128).transpose(2, 1, 0)).reshape(128, -1)
    qgT = _u8view(np.ascontiguousarray(np.asarray(L0_ffn_qw_gate).T))
    quT = _u8view(np.ascontiguousarray(np.asarray(L0_ffn_qw_up).T))
    qdT = _u8view(np.ascontiguousarray(np.asarray(L0_ffn_qw_down).T))
    zgT = _u8view(np.ascontiguousarray(np.asarray(L0_ffn_z_gate).T))
    sgT = np.ascontiguousarray(np.asarray(L0_ffn_s_gate, dtype=np.float32).T)
    zuT = _u8view(np.ascontiguousarray(np.asarray(L0_ffn_z_up).T))
    suT = np.ascontiguousarray(np.asarray(L0_ffn_s_up, dtype=np.float32).T)
    zdT = _u8view(np.ascontiguousarray(np.asarray(L0_ffn_z_down).T))
    sdT = np.ascontiguousarray(np.asarray(L0_ffn_s_down, dtype=np.float32).T)

    cos, sinS = _rope_tables()
    alpha = np.linspace(0.0, 1.0, 256, dtype=np.float32)
    idf = np.eye(128, dtype=np.float32)

    in_maps = []
    for cid in range(NCORE):
        q0 = TOWN * cid
        lo, hi = q0 - HALO, q0 + TOWN + HALO
        xe = np.zeros((TEXT, D), dtype=np.float32)
        slo, shi = max(lo, 0), min(hi, S)
        xe[slo - lo:shi - lo, :] = x[slo:shi]
        cq = np.ascontiguousarray(cos[q0:q0 + TOWN].T)
        sq = np.ascontiguousarray(sinS[q0:q0 + TOWN].T)
        ck = np.zeros((128, TEXT), dtype=np.float32)
        sk = np.zeros((128, TEXT), dtype=np.float32)
        ck[:, slo - lo:shi - lo] = cos[slo:shi].T
        sk[:, slo - lo:shi - lo] = sinS[slo:shi].T
        alc = np.empty((128, 8), dtype=np.float32)
        onc = np.empty((128, 8), dtype=np.float32)
        for j in range(8):
            a = alpha[0:128] if j % 2 == 0 else alpha[128:256]
            if cid == 0 and j < 2:
                alc[:, j] = 1.0
                onc[:, j] = 0.0
            else:
                alc[:, j] = a
                onc[:, j] = 1.0 - a
        mkc = np.zeros((128, 20), dtype=np.float32)
        p = np.arange(128)
        for w in range(5):
            for kt in range(4):
                kp = q0 - HALO + 256 * w + 128 * kt + p
                mkc[:, 4 * w + kt] = np.where((kp >= 0) & (kp < S), 0.0, NEG)
        in_maps.append(dict(
            x_ext=xe, wqt=wqt, wot=wot, wkt=wkt, wvt=wvt,
            qg=qgT, qu=quT, qd=qdT, zg=zgT, sg=sgT, zu=zuT, su=suT,
            zd=zdT, sd=sdT,
            cosq=cq, sinq=sq, cosk=ck, sink=sk,
            alpc=alc, onec=onc, maskc=mkc, idf=idf,
        ))

    nc = build_program()
    res = run_bass_kernel_spmd(nc, in_maps, core_ids=list(range(NCORE)))
    outs = [np.asarray(res.results[cid]["out"], dtype=np.float32)
            for cid in range(NCORE)]
    return np.concatenate(outs, axis=0).reshape(1, S, D)
